# version 02
# baseline (speedup 1.0000x reference)
import math
import zlib
import numpy as np
import ml_dtypes
import concourse.bass as bass
import concourse.mybir as mybir
from concourse.bass_utils import run_bass_kernel_spmd
from concourse.bass_core_rust import IndirectOffsetOnAxis

F32 = mybir.dt.float32
BF16 = mybir.dt.bfloat16
I32 = mybir.dt.int32
AF = mybir.ActivationFunctionType
ALU = mybir.AluOpType
AX = mybir.AxisListType
BF = ml_dtypes.bfloat16

D = 1024
T = 2048
NB = 2
NCHUNK = D // 128
NTT = T // 512
H = 16
DH = 64

ENGINES = ("sync", "gpsimd", "tensor", "vector", "scalar")


class Prog:

    POOLS = {"sync#c": 2, "gpsimd#c": 2, "tensor": 8, "vector": 10, "scalar": 6}

    def __init__(self, nc):
        self.nc = nc
        self.extra_waits = []
        self.ops = []
        self.pool = []
        self.cnt_ = []
        self.prange = {}
        for eng, n in self.POOLS.items():
            self.prange[eng] = (len(self.pool), n)
            for i in range(n):
                self.pool.append(nc.alloc_semaphore(f"p_{eng.replace(chr(35), chr(95))}_{i}"))
                self.cnt_.append(0)
        self.ded = {}
        self.gmap = {}

    @staticmethod
    def _group(name):
        return "".join(c for c in name if not c.isdigit())

    def _idx_for(self, name, engine, amt=1):
        if engine in ("sync", "gpsimd"):
            if amt != 16:
                engine = engine + "#c"
            else:
                engine = "dma"
        g = self._group(name)
        if g.endswith("!") or engine == "dma":
            key = (g, engine)
            if key not in self.ded:
                self.pool.append(self.nc.alloc_semaphore(f"d{len(self.ded)}"))
                self.cnt_.append(0)
                self.ded[key] = len(self.pool) - 1
            idx = self.ded[key]
        else:
            base, n = self.prange[engine]
            idx = base + (zlib.crc32(g.encode()) & 0x7FFFFFFF) % n
        em = self.gmap.setdefault(g, {})
        em[engine] = idx
        return idx

    def cnt(self, name):
        g = self._group(name)
        return sum(self.cnt_[i] for i in self.gmap.get(g, {}).values())

    def op(self, engine, fn, waits=(), incs=()):
        self.ops.append((engine, fn, list(waits), list(incs)))

    def prod(self, engine, fn, chan_name, amt, waits=()):
        i = self._idx_for(chan_name, engine, amt)
        self.cnt_[i] += amt
        w = list(waits) + list(self.extra_waits)
        self.op(engine, fn, waits=w, incs=[(self.pool[i], amt)])

    def after(self, *names):
        out = {}
        for n in names:
            g = self._group(n)
            for idx in self.gmap.get(g, {}).values():
                if self.cnt_[idx] > 0:
                    out[idx] = self.cnt_[idx]
        return [(self.pool[i], v) for i, v in out.items()]

    def after_lag(self, name, lag, step=1):
        g = self._group(name)
        assert g.endswith("!")
        em = self.gmap.get(g, {})
        assert len(em) <= 1
        out = []
        for idx in em.values():
            v = self.cnt_[idx] - lag * step
            if v > 0:
                out.append((self.pool[idx], v))
        return out

    def build(self):
        nc = self.nc
        with nc.Block() as blk:
            for engname in ENGINES:
                myops = [o for o in self.ops if o[0] == engname]
                if not myops:
                    continue

                def make(myops=myops):
                    def emit(eng):
                        for _, fn, waits, incs in myops:
                            for sem, val in waits:
                                eng.wait_ge(sem, val)
                            ins = fn(eng)
                            for sem, amt in incs:
                                assert ins is not None
                                ins = ins.then_inc(sem, amt)
                    return emit

                getattr(blk, engname)(make())


class Builder:
    def __init__(self, layers, nb=NB):
        self.layers = layers
        self.nb = nb
        nc = bass.Bass("TRN2", target_bir_lowering=False)
        self.nc = nc
        self.P = Prog(nc)
        self.uid = 0
        self.once = set()
        self.x_in = nc.dram_tensor("x_in", [nb, T, D], F32, kind="ExternalInput").ap()
        self.x_out = nc.dram_tensor("x_out", [nb, T, D], F32, kind="ExternalOutput").ap()
        self.win = {}
        self.xs = nc.alloc_sbuf_tensor("xs", [128, NCHUNK * T], F32)
        self.xb = nc.alloc_sbuf_tensor("xb", [128, NCHUNK * T], BF16)
        self.wpool = nc.alloc_sbuf_tensor("wpool", [128, 16384], BF16)
        self.wpool2 = nc.alloc_sbuf_tensor("wpool2", [128, 8192], BF16)
        self.cols = nc.alloc_sbuf_tensor("cols", [128, 16], F32)
        self.ident = nc.alloc_sbuf_tensor("identsb", [128, 128], BF16)
        self.onesrow = nc.alloc_sbuf_tensor("onesrow", [1, 128], F32)
        self.onescol = nc.alloc_sbuf_tensor("onescol", [128, 1], BF16)
        self.onescolf = nc.alloc_sbuf_tensor("onescolf", [128, 1], F32)
        self.scra = nc.alloc_sbuf_tensor("scra", [128, 16384], BF16)
        self.scrb = nc.alloc_sbuf_tensor("scrb", [128, 8192], BF16)
        self.scrf = nc.alloc_sbuf_tensor("scrf", [128, 1536], F32)
        c_ident = nc.inline_tensor(np.eye(128, dtype=BF), "c_ident")
        self.pb = [nc.alloc_psum_tensor(f"pb{i}", [128, 512], F32) for i in range(8)]

        P = self.P
        P.prod("sync", lambda e: e.dma_start(out=self.ident[:], in_=c_ident.ap()), "init", 16)
        P.prod("vector", lambda e: e.memset(self.onesrow[:], 1.0), "init", 1)
        P.prod("vector", lambda e: e.memset(self.onescol[:], 1.0), "init", 1)
        P.prod("vector", lambda e: e.memset(self.onescolf[:], 1.0), "init", 1)
        self._eps_cols = {}

    def nid(self, s):
        self.uid += 1
        return f"{s}{self.uid}"

    def initw(self):
        return self.P.after("init")

    def win_decl(self, name, shape):
        if name not in self.win:
            self.win[name] = self.nc.dram_tensor(name, list(shape), F32,
                                                 kind="ExternalInput").ap()
        return self.win[name]

    def dram(self, name, shape, dtype=BF16):
        return self.nc.dram_tensor(name, list(shape), dtype, kind="Internal").ap()

    _sbcache = None

    def sb(self, name, shape, dtype=BF16):
        if self._sbcache is None:
            self._sbcache = {}
        key = (name, tuple(shape), dtype)
        if key not in self._sbcache:
            self._sbcache[key] = self.nc.alloc_sbuf_tensor(self.nid(name), list(shape), dtype)
        return self._sbcache[key]

    def eps_col(self, val, nparts=128):
        if val not in self._eps_cols:
            idx = len(self._eps_cols)
            assert idx < 16
            self.P.prod("vector",
                        (lambda e, idx=idx, val=val: e.memset(self.cols[:, idx:idx + 1], val)),
                        "init", 1)
            self._eps_cols[val] = idx
        i = self._eps_cols[val]
        return self.cols[0:nparts, i:i + 1]

    def xbok(self):
        return self.P.after("xbok")

    def prep_cast(self, src_ap, dst_ap, rows, cols, chname):
        P = self.P
        r = 0
        while r < rows:
            rr = min(512, rows - r)
            P.prod("gpsimd",
                   (lambda e, r=r, rr=rr: e.dma_start(out=dst_ap[r:r + rr, :],
                                                      in_=src_ap[r:r + rr, :])),
                   chname, 16)
            r += rr

    def prep_transpose_cast(self, src_ap, o_dim, i_dim, chname):
        tmp = self.dram(self.nid("wtmp"), [o_dim, i_dim], BF16)
        self.prep_cast(src_ap, tmp, o_dim, i_dim, chname + "_c")
        return tmp

    def prep_transpose(self, src_ap, dst_ap, o_dim, i_dim, chname, tmp=None):
        P = self.P
        if tmp is None:
            tmp = self.prep_transpose_cast(src_ap, o_dim, i_dim, chname)
        wc = P.after(chname + "_c")
        sbuf = self.scra[:, 0:4096]
        for ib in range(i_dim // 128):
            tw = P.after("ptS")
            for ob in range(o_dim // 128):
                P.prod("sync",
                       (lambda e, ib=ib, ob=ob: e.dma_start_transpose(
                           sbuf[0:128, ob * 128:(ob + 1) * 128],
                           tmp[ob * 128:(ob + 1) * 128, ib * 128:(ib + 1) * 128])),
                       "ptT", 16, waits=wc + tw)
            P.prod("sync",
                   (lambda e, ib=ib: e.dma_start(
                       out=dst_ap[ib * 128:(ib + 1) * 128, 0:o_dim],
                       in_=sbuf[0:128, 0:o_dim])),
                   "ptS", 16, waits=P.after("ptT"))
        P.prod("sync", lambda e: e.nop(), chname, 1, waits=P.after("ptS"))

    def load_w(self, wc_ap, dst_sb, dst_off, in_dim, o_lo, o_hi, chname, waits):
        P = self.P
        ow = o_hi - o_lo
        for ci in range(in_dim // 128):
            P.prod("sync",
                   (lambda e, ci=ci: e.dma_start(
                       out=dst_sb[:, dst_off + ci * ow: dst_off + (ci + 1) * ow],
                       in_=wc_ap[ci * 128:(ci + 1) * 128, o_lo:o_hi])),
                   chname, 16, waits=waits)

    def load_vec_cols(self, src_ap, n, chname, dst, dst_off=0, waits=()):
        P = self.P
        nch = max(n // 128, 1)
        np_ = min(n, 128)
        stage = self.sb("vrow", [32, 128], F32)
        bstage = self.sb("vrowb", [32, 128], BF16)
        nm = self.nid("vc")
        assert nch <= 32
        src2 = src_ap.rearrange("(c p) -> c p", p=np_)
        P.prod("sync", lambda e: e.dma_start(out=stage[0:nch, 0:np_], in_=src2),
               nm + "ld", 16, waits=list(waits) + P.after("vcC"))
        P.prod("vector", lambda e: e.tensor_copy(bstage[0:nch, 0:np_], stage[0:nch, 0:np_]),
               nm + "c", 1, waits=P.after(nm + "ld"))
        pbank = self.pb[7]
        P.prod("tensor",
               lambda e: e.transpose(pbank.ap().bitcast(BF16)[0:np_, 0:nch],
                                     bstage[0:nch, 0:np_], self.ident[0:nch, 0:nch]),
               nm + "t", 1, waits=P.after(nm + "c", "pb7r") + self.initw())
        P.prod("vector",
               lambda e: e.tensor_copy(dst[0:np_, dst_off:dst_off + nch],
                                       pbank.ap().bitcast(BF16)[0:np_, 0:nch]),
               "pb7r", 1, waits=P.after(nm + "t"))
        P.prod("vector", lambda e: e.nop(), "vcC", 1, waits=P.after("pb7r"))
        P.prod("vector", lambda e: e.nop(), chname, 1, waits=P.after("pb7r"))

    def emit_loadx(self, b):
        P = self.P
        nm = self.nid("lx")
        stg = self.scrf[:, 0:1024]
        hi = self.sb("tph", [128, D], BF16)
        lo = self.sb("tpl", [128, D], BF16)
        for tt in range(T // 128):
            P.prod("sync",
                   (lambda e, tt=tt: e.dma_start(out=stg.bitcast(F32) if False else stg, in_=self.x_in[b, tt * 128:(tt + 1) * 128, :])),
                   nm + "ld", 16, waits=P.after(nm + "lo", "xst"))
            w = P.after(nm + "ld")
            P.prod("vector", (lambda e: e.tensor_copy(hi[:], stg)), nm + "hi", 1,
                   waits=w + P.after(nm + "t"))
            P.prod("vector", (lambda e: e.tensor_sub(lo[:], stg, hi[:])), nm + "lo", 1,
                   waits=P.after(nm + "hi") + P.after(nm + "t2"))
            for c in range(NCHUNK):
                pv = self.pb[c % 4].ap().bitcast(BF16)
                pv2 = self.pb[4 + c % 4].ap().bitcast(BF16)
                P.prod("tensor",
                       (lambda e, c=c, pv=pv: e.transpose(pv[0:128, 0:128], hi[:, c * 128:(c + 1) * 128],
                                                          self.ident[:])),
                       nm + "t", 1, waits=P.after(nm + "hi", nm + "evh") + self.initw())
                P.prod("vector",
                       (lambda e, c=c, tt=tt, pv=pv: e.tensor_copy(
                           self.xs[:, c * T + tt * 128: c * T + (tt + 1) * 128], pv[0:128, 0:128])),
                       nm + "evh", 1, waits=P.after(nm + "t"))
                P.prod("tensor",
                       (lambda e, c=c, pv2=pv2: e.transpose(pv2[0:128, 0:128], lo[:, c * 128:(c + 1) * 128],
                                                            self.ident[:])),
                       nm + "t2", 1, waits=P.after(nm + "lo", nm + "ev"))
                P.prod("vector",
                       (lambda e, c=c, tt=tt, pv2=pv2: e.tensor_tensor(
                           self.xs[:, c * T + tt * 128: c * T + (tt + 1) * 128],
                           self.xs[:, c * T + tt * 128: c * T + (tt + 1) * 128],
                           pv2[0:128, 0:128], ALU.add)),
                       nm + "ev", 1, waits=P.after(nm + "t2", nm + "evh"))
        for c in range(NCHUNK):
            P.prod("vector",
                   (lambda e, c=c: e.tensor_copy(self.xb[:, c * T:(c + 1) * T],
                                                 self.xs[:, c * T:(c + 1) * T])),
                   "xbok", 1, waits=P.after(nm + "ev", "postln"))

    def emit_storex(self, b):
        P = self.P
        nm = self.nid("sx")
        hi = self.sb("tph", [128, D], BF16)
        lo = self.sb("tpl", [128, D], BF16)
        otile = self.scrf[:, 0:1024]
        for tt in range(T // 128):
            w = P.after("postln") + P.after(nm + "t", nm + "t2")
            for c in range(NCHUNK):
                P.prod("vector",
                       (lambda e, c=c, tt=tt: e.tensor_copy(
                           hi[:, c * 128:(c + 1) * 128],
                           self.xs[:, c * T + tt * 128: c * T + (tt + 1) * 128])),
                       nm + "hi", 1, waits=w)
                P.prod("vector",
                       (lambda e, c=c, tt=tt: e.tensor_sub(
                           lo[:, c * 128:(c + 1) * 128],
                           self.xs[:, c * T + tt * 128: c * T + (tt + 1) * 128],
                           hi[:, c * 128:(c + 1) * 128])),
                       nm + "lo", 1, waits=P.after(nm + "hi"))
            for c in range(NCHUNK):
                pv = self.pb[c % 4].ap().bitcast(BF16)
                pv2 = self.pb[4 + c % 4].ap().bitcast(BF16)
                P.prod("tensor",
                       (lambda e, c=c, pv=pv: e.transpose(pv[0:128, 0:128],
                                                          hi[:, c * 128:(c + 1) * 128], self.ident[:])),
                       nm + "t", 1, waits=P.after(nm + "hi", nm + "evh") + self.initw())
                P.prod("vector",
                       (lambda e, c=c, pv=pv: e.tensor_copy(otile[:, c * 128:(c + 1) * 128],
                                                            pv[0:128, 0:128])),
                       nm + "evh", 1, waits=P.after(nm + "t") + P.after(nm + "st"))
                P.prod("tensor",
                       (lambda e, c=c, pv2=pv2: e.transpose(pv2[0:128, 0:128],
                                                            lo[:, c * 128:(c + 1) * 128], self.ident[:])),
                       nm + "t2", 1, waits=P.after(nm + "lo", nm + "ev")),
                P.prod("vector",
                       (lambda e, c=c, pv2=pv2: e.tensor_tensor(
                           otile[:, c * 128:(c + 1) * 128], otile[:, c * 128:(c + 1) * 128],
                           pv2[0:128, 0:128], ALU.add)),
                       nm + "ev", 1, waits=P.after(nm + "t2", nm + "evh"))
            P.prod("sync",
                   (lambda e, tt=tt, b=b: e.dma_start(
                       out=self.x_out[b, tt * 128:(tt + 1) * 128, :], in_=otile)),
                   "xst", 16, waits=P.after(nm + "ev"))
            P.prod("sync", lambda e: e.nop(), nm + "st", 1, waits=P.after("xst"))

    def emit_postln(self, waits):
        P = self.P
        nm = self.nid("ln")
        sq = self.scrb[:, 0:512]
        murow = self.sb("tph", [128, D], BF16).ap().bitcast(F32)[0:1, 0:512]
        rstdrow = self.sb("tpl", [128, D], BF16).ap().bitcast(F32)[0:1, 0:512]
        varrow = self.scrf[0:1, 1024:1536]
        stat = self.pb[7]
        bc = self.pb[6]
        for ts in range(NTT):
            w = waits + P.after(nm + "apl")
            for c in range(NCHUNK):
                P.prod("vector",
                       (lambda e, c=c, ts=ts: e.tensor_copy(
                           self.xb[:, c * T + ts * 512: c * T + (ts + 1) * 512],
                           self.xs[:, c * T + ts * 512: c * T + (ts + 1) * 512])),
                       nm + "cast", 1, waits=w)
            wc = P.after(nm + "cast")
            for c in range(NCHUNK):
                P.prod("tensor",
                       (lambda e, c=c, ts=ts: e.matmul(
                           stat[0:1, 0:512], self.onescol[:, 0:1],
                           self.xb[:, c * T + ts * 512: c * T + (ts + 1) * 512],
                           start=(c == 0), stop=(c == NCHUNK - 1))),
                       nm + "mmu", 1,
                       waits=wc + P.after(nm + "apl", "pb7r") + self.initw()),
            P.prod("vector",
                   (lambda e: e.tensor_scalar(murow[:], stat[0:1, 0:512], 1.0 / D, None, ALU.mult)),
                   nm + "muev", 1, waits=P.after(nm + "mmu"))
            for c in range(NCHUNK):
                P.prod("scalar",
                       (lambda e, c=c, ts=ts: e.activation(
                           sq[:], self.xb[:, c * T + ts * 512: c * T + (ts + 1) * 512],
                           AF.Square)),
                       nm + "sq", 1, waits=wc + P.after(nm + "mvar")),
                P.prod("tensor",
                       (lambda e, c=c: e.matmul(
                           stat[0:1, 0:512], self.onescol[:, 0:1], sq[:],
                           start=(c == 0), stop=(c == NCHUNK - 1))),
                       nm + "mvar", 1,
                       waits=P.after(nm + "sq", nm + "muev"))
            P.prod("vector",
                   (lambda e: e.tensor_scalar(varrow[:], stat[0:1, 0:512], 1.0 / D, None, ALU.mult)),
                   nm + "v1", 1, waits=P.after(nm + "mvar"))
            P.prod("scalar",
                   (lambda e: e.activation(rstdrow[:], murow[:], AF.Square)),
                   nm + "v2", 1, waits=P.after(nm + "v1", nm + "muev"))
            P.prod("vector",
                   (lambda e: e.tensor_sub(varrow[:], varrow[:], rstdrow[:])),
                   nm + "v3", 1, waits=P.after(nm + "v2"))
            epsap = self.eps_col(1e-5, 1)
            P.prod("scalar",
                   (lambda e, epsap=epsap: e.activation(varrow[:], varrow[:], AF.Sqrt,
                                                        bias=epsap)),
                   nm + "v4", 1, waits=P.after(nm + "v3") + self.initw())
            P.prod("vector",
                   (lambda e: e.reciprocal(rstdrow[:], varrow[:])),
                   nm + "v5", 1, waits=P.after(nm + "v4"))
            P.prod("tensor",
                   (lambda e: e.matmul(bc[0:128, 0:512], self.onesrow[0:1, 0:128],
                                       murow[0:1, 0:512], start=True, stop=True)),
                   nm + "bmu", 1, waits=P.after(nm + "muev", nm + "apl") + self.initw()),
            P.prod("tensor",
                   (lambda e: e.matmul(stat[0:128, 0:512], self.onesrow[0:1, 0:128],
                                       rstdrow[0:1, 0:512], start=True, stop=True)),
                   nm + "brs", 1, waits=P.after(nm + "v5"))
            wb = P.after(nm + "bmu", nm + "brs")
            for c in range(NCHUNK):
                sl = slice(c * T + ts * 512, c * T + (ts + 1) * 512)
                P.prod("vector",
                       (lambda e, sl=sl: e.tensor_sub(self.xs[:, sl], self.xs[:, sl],
                                                      bc[0:128, 0:512])),
                       nm + "ap1", 1, waits=wb)
                P.prod("vector",
                       (lambda e, sl=sl: e.tensor_mul(self.xs[:, sl], self.xs[:, sl],
                                                      stat[0:128, 0:512])),
                       nm + "ap2", 1, waits=P.after(nm + "ap1"))
                P.prod("vector",
                       (lambda e, sl=sl: e.tensor_copy(self.xb[:, sl], self.xs[:, sl])),
                       nm + "apl", 1, waits=P.after(nm + "ap2"))
        P.prod("vector", lambda e: e.nop(), "postln", 1, waits=P.after(nm + "apl"))
        P.prod("vector", lambda e: e.nop(), "xbok", 1, waits=P.after(nm + "apl"))
        return P.after("postln")

    def emit_l2f(self, b):
        nc, P = self.nc, self.P
        U1 = self.win_decl("L2_ffn_U1", [1024, 256])
        V1 = self.win_decl("L2_ffn_V1", [256, 4096])
        b1v = self.win_decl("L2_ffn_b1", [4096])
        U2 = self.win_decl("L2_ffn_U2", [4096, 256])
        V2 = self.win_decl("L2_ffn_V2", [256, 1024])
        b2v = self.win_decl("L2_ffn_b2", [1024])
        key = "l2f"
        if key not in self.once:
            self.once.add(key)
            P.extra_waits = P.after("postln") + self.xbok()
            self.wc_l2f = (self.dram("wcU1", [1024, 256]), self.dram("wcV1", [256, 4096]),
                           self.dram("wcU2", [4096, 256]), self.dram("wcV2", [256, 1024]))
            self.prep_cast(U1, self.wc_l2f[0], 1024, 256, key + "p")
            self.prep_cast(V1, self.wc_l2f[1], 256, 4096, key + "p")
            self.prep_cast(U2, self.wc_l2f[2], 4096, 256, key + "p")
            self.prep_cast(V2, self.wc_l2f[3], 256, 1024, key + "p")
            self.l2f_bcols = self.nc.alloc_sbuf_tensor("l2f_bcols", [128, 40], F32)
            self.load_vec_cols(b1v, 4096, key + "b", dst=self.l2f_bcols, dst_off=0)
            self.load_vec_cols(b2v, 1024, key + "b", dst=self.l2f_bcols, dst_off=32)
            P.extra_waits = []
        wcU1, wcV1, wcU2, wcV2 = self.wc_l2f
        bcols = self.l2f_bcols
        nm = self.nid("l2f_")
        wprep = P.after(key + "p", key + "b")

        Pm = self.scrb[:, 0:4096]
        Hm = self.scra[:, 0:8192]
        Qm = self.scrb[:, 4096:8192]
        t1 = self.scrf[:, 0:512]
        t2 = self.scrf[:, 512:1024]

        import os
        stages = int(os.environ.get("L2F_STAGES", "9"))
        wld = wprep + P.after("postln")
        self.load_w(wcU1, self.wpool, 0, 1024, 0, 256, nm + "wu1", wld)
        self.load_w(wcV1, self.wpool, 2048, 256, 0, 4096, nm + "wv1", wld)
        self.load_w(wcU2, self.wpool2, 0, 4096, 0, 256, nm + "wu2", wld)
        self.load_w(wcV2, self.wpool, 10240, 256, 0, 1024, nm + "wv2", wld)

        for ts in range(NTT):
            for oc in range(2):
                bank = self.pb[oc]
                P.prod("tensor",
                       (lambda e, ts=ts, oc=oc, bank=bank: [
                           e.matmul(bank[0:128, 0:512],
                                    self.wpool[:, ci * 256 + oc * 128: ci * 256 + (oc + 1) * 128],
                                    self.xb[:, ci * T + ts * 512:ci * T + (ts + 1) * 512],
                                    start=(ci == 0), stop=(ci == NCHUNK - 1))
                           for ci in range(NCHUNK)][-1]),
                       nm + "mmP", 1,
                       waits=P.after(nm + "wu1", nm + "evP", "postln") + self.xbok())
                P.prod("vector",
                       (lambda e, ts=ts, oc=oc, bank=bank: e.tensor_copy(
                           Pm[:, oc * T + ts * 512: oc * T + (ts + 1) * 512], bank[0:128, 0:512])),
                       nm + "evP", 1, waits=P.after(nm + "mmP"))
        if stages < 2:
            return self.emit_postln(P.after(nm + "evP"))
        c1 = math.sqrt(2.0 / math.pi)
        c2 = c1 * 0.044715
        for ts in range(NTT):
            wP = P.after(nm + "evP")
            for half in range(2):
                for hh in range(16):
                    hc = half * 16 + hh
                    bank = self.pb[hh % 4]
                    P.prod("tensor",
                           (lambda e, ts=ts, hc=hc, bank=bank: [
                               e.matmul(bank[0:128, 0:512],
                                        self.wpool[:, 2048 + ci * 4096 + hc * 128: 2048 + ci * 4096 + (hc + 1) * 128],
                                        Pm[:, ci * T + ts * 512: ci * T + (ts + 1) * 512],
                                        start=(ci == 0), stop=(ci == 1))
                               for ci in range(2)][-1]),
                           nm + "mmH", 1,
                           waits=wP + P.after(nm + "wv1")
                           + P.after_lag(nm + "g1!", 3) + P.after(nm + "mmQ"))
                    mmw = P.after(nm + "mmH")
                    P.prod("scalar",
                           (lambda e, hc=hc, bank=bank: e.activation(
                               t2[:], bank[0:128, 0:512], AF.Identity,
                               bias=bcols[:, hc:hc + 1], scale=1.0)),
                           nm + "g1!", 1, waits=mmw + P.after(nm + "gel!")),
                    P.prod("vector",
                           (lambda e: e.tensor_mul(t1[:], t2[:], t2[:])),
                           nm + "g2", 1, waits=P.after(nm + "g1!", nm + "gel!")),
                    P.prod("vector",
                           (lambda e: e.tensor_scalar(t1[:], t1[:], c2, c1, ALU.mult, ALU.add)),
                           nm + "g3", 1, waits=P.after(nm + "g2"))
                    P.prod("vector",
                           (lambda e: e.tensor_mul(t1[:], t1[:], t2[:])),
                           nm + "g4", 1, waits=P.after(nm + "g3"))
                    P.prod("scalar",
                           (lambda e: e.activation(t1[:], t1[:], AF.Tanh)),
                           nm + "g5", 1, waits=P.after(nm + "g4"))
                    P.prod("vector",
                           (lambda e: e.tensor_scalar(t1[:], t1[:], 1.0, None, ALU.add)),
                           nm + "g6", 1, waits=P.after(nm + "g5"))
                    P.prod("vector",
                           (lambda e, hh=hh: e.tensor_mul(Hm[:, hh * 512:(hh + 1) * 512], t1[:], t2[:])),
                           nm + "gel!", 1, waits=P.after(nm + "g6"))
                for oc in range(2):
                    bank = self.pb[4 + oc]
                    P.prod("tensor",
                           (lambda e, oc=oc, bank=bank, half=half: [
                               e.matmul(bank[0:128, 0:512],
                                        self.wpool2[:, (half * 16 + ci) * 256 + oc * 128: (half * 16 + ci) * 256 + (oc + 1) * 128],
                                        Hm[:, ci * 512:(ci + 1) * 512],
                                        start=(half == 0 and ci == 0), stop=(half == 1 and ci == 15))
                               for ci in range(16)][-1]),
                           nm + "mmQ", 1,
                           waits=P.after(nm + "gel!", nm + "wu2", nm + "evQ"))
            for oc in range(2):
                bank = self.pb[4 + oc]
                P.prod("vector",
                       (lambda e, ts=ts, oc=oc, bank=bank: e.tensor_scalar(
                           Qm[:, oc * T + ts * 512: oc * T + (ts + 1) * 512],
                           bank[0:128, 0:512], 0.5, None, ALU.mult)),
                       nm + "evQ", 1, waits=P.after(nm + "mmQ"))
        if stages < 3:
            return self.emit_postln(P.after(nm + "evQ"))
        for ts in range(NTT):
            for oc in range(NCHUNK):
                bank = self.pb[oc % 4]
                P.prod("tensor",
                       (lambda e, ts=ts, oc=oc, bank=bank: [
                           e.matmul(bank[0:128, 0:512],
                                    self.wpool[:, 10240 + ci * 1024 + oc * 128: 10240 + ci * 1024 + (oc + 1) * 128],
                                    Qm[:, ci * T + ts * 512: ci * T + (ts + 1) * 512],
                                    start=(ci == 0), stop=(ci == 1))
                           for ci in range(2)][-1]),
                       nm + "mmO", 1,
                       waits=P.after(nm + "evQ", nm + "wv2") + P.after_lag(nm + "evO!", 3)),
                P.prod("vector",
                       (lambda e, oc=oc, bank=bank: e.tensor_scalar(
                           t1[:], bank[0:128, 0:512], bcols[:, 32 + oc:32 + oc + 1], None, ALU.add)),
                       nm + "evO1", 1, waits=P.after(nm + "mmO", nm + "evO!")),
                P.prod("vector",
                       (lambda e, ts=ts, oc=oc: e.tensor_tensor(
                           self.xs[:, oc * T + ts * 512: oc * T + (ts + 1) * 512],
                           self.xs[:, oc * T + ts * 512: oc * T + (ts + 1) * 512],
                           t1[:], ALU.add)),
                       nm + "evO!", 1, waits=P.after(nm + "evO1"))
        return self.emit_postln(P.after(nm + "evO!"))



    def emit_l3f(self, b):
        nc, P = self.nc, self.P
        Wr = self.win_decl("L3_ffn_W_r", [1024, 1024])
        Wk = self.win_decl("L3_ffn_W_k", [1024, 1024])
        Wv = self.win_decl("L3_ffn_W_v", [1024, 1024])
        mur = self.win_decl("L3_ffn_mu_r", [1024])
        muk = self.win_decl("L3_ffn_mu_k", [1024])
        key = "l3f"
        if key not in self.once:
            self.once.add(key)
            P.extra_waits = P.after("postln") + self.xbok()
            self.wc_l3f = (self.dram("wcWr", [1024, 1024]), self.dram("wcWk", [1024, 1024]),
                           self.dram("wcWv", [1024, 1024]))
            tmps = [self.prep_transpose_cast(w, 1024, 1024, key + "p")
                    for w in (Wr, Wk, Wv)]
            for w, wc, tmp in zip((Wr, Wk, Wv), self.wc_l3f, tmps):
                self.prep_transpose(w, wc, 1024, 1024, key + "p", tmp=tmp)
            self.l3f_cols = self.nc.alloc_sbuf_tensor("l3f_cols", [128, 32], F32)
            self.load_vec_cols(mur, 1024, key + "b", dst=self.l3f_cols, dst_off=0)
            self.load_vec_cols(muk, 1024, key + "b", dst=self.l3f_cols, dst_off=8)
            P.prod("vector",
                   (lambda e: e.tensor_scalar(self.l3f_cols[:, 16:32], self.l3f_cols[:, 0:16],
                                              -1.0, 1.0, ALU.mult, ALU.add)),
                   key + "b", 1, waits=P.after(key + "b"))
            P.extra_waits = []
        wcWr, wcWk, wcWv = self.wc_l3f
        mc = self.l3f_cols
        nm = self.nid("l3f_")
        wld = P.after(key + "p", key + "b", "postln")
        self.load_w(wcWr, self.wpool, 0, 1024, 0, 1024, nm + "ww", wld)
        self.load_w(wcWv, self.wpool, 8192, 1024, 0, 1024, nm + "ww", wld)
        self.load_w(wcWk, self.wpool2, 0, 1024, 0, 1024, nm + "ww", wld)
        kact = self.scra
        xset = self.scrb
        t1 = self.scrf[:, 0:512]
        t2 = self.scrf[:, 512:1024]
        t3 = self.scrf[:, 1024:1536]

        def build_shift(dst_off, mu_off, omu_off, ci, ts, chname):
            base = ci * T + ts * 512
            dsl = slice(dst_off + ci * 512, dst_off + (ci + 1) * 512)
            P.prod("vector",
                   (lambda e: e.tensor_scalar(t1[:], self.xb[:, base:base + 512],
                                              mc[:, mu_off + ci:mu_off + ci + 1], None, ALU.mult)),
                   nm + "sh1", 1, waits=self.xbok() + P.after(nm + "xk", nm + "xr",
                                                                              nm + "mmK", nm + "mmA", nm + "sgm"))
            if ts > 0:
                P.prod("vector",
                       (lambda e: e.scalar_tensor_tensor(
                           xset[:, dsl], self.xb[:, base - 1:base + 511],
                           mc[:, omu_off + ci:omu_off + ci + 1], t1[:], ALU.mult, ALU.add)),
                       chname, 1, waits=P.after(nm + "sh1"))
            else:
                P.prod("vector",
                       (lambda e: e.scalar_tensor_tensor(
                           xset[:, dsl.start + 1:dsl.stop], self.xb[:, base:base + 511],
                           mc[:, omu_off + ci:omu_off + ci + 1], t1[:, 1:512], ALU.mult, ALU.add)),
                       chname, 1, waits=P.after(nm + "sh1"))
                P.prod("vector",
                       (lambda e: e.tensor_copy(xset[:, dsl.start:dsl.start + 1], t1[:, 0:1])),
                       chname, 1, waits=P.after(nm + "sh1"))

        for ts in range(NTT):
            for ci in range(NCHUNK):
                build_shift(0, 8, 24, ci, ts, nm + "xk")
            for oc in range(NCHUNK):
                bank = self.pb[oc % 4]
                P.prod("tensor",
                       (lambda e, oc=oc, bank=bank: [
                           e.matmul(bank[0:128, 0:512],
                                    self.wpool2[:, ci * 1024 + oc * 128: ci * 1024 + (oc + 1) * 128],
                                    xset[:, ci * 512:(ci + 1) * 512],
                                    start=(ci == 0), stop=(ci == NCHUNK - 1))
                           for ci in range(NCHUNK)][-1]),
                       nm + "mmK", 1,
                       waits=P.after(nm + "xk", nm + "ww", nm + "kre", "postln"))
                P.prod("vector",
                       (lambda e, bank=bank: e.tensor_scalar_max(t2[:], bank[0:128, 0:512], 0.0)),
                       nm + "kre", 1, waits=P.after(nm + "mmK", nm + "ksq"))
                P.prod("scalar",
                       (lambda e, oc=oc, ts=ts: e.activation(
                           kact[:, oc * T + ts * 512: oc * T + (ts + 1) * 512], t2[:], AF.Square)),
                       nm + "ksq", 1, waits=P.after(nm + "kre"))
            for ci in range(NCHUNK):
                build_shift(4096, 0, 16, ci, ts, nm + "xr")
            for oc in range(NCHUNK):
                bankA = self.pb[4 + oc % 2]
                bankB = self.pb[oc % 4]
                P.prod("tensor",
                       (lambda e, oc=oc, ts=ts, bankA=bankA: [
                           e.matmul(bankA[0:128, 0:512],
                                    self.wpool[:, ci * 1024 + oc * 128: ci * 1024 + (oc + 1) * 128],
                                    self.scrb[:, 4096 + ci * 512:4096 + (ci + 1) * 512],
                                    start=(ci == 0), stop=(ci == NCHUNK - 1))
                           for ci in range(NCHUNK)][-1]),
                       nm + "mmA", 1,
                       waits=P.after(nm + "xr", nm + "sgm", nm + "ksq"))
                P.prod("tensor",
                       (lambda e, oc=oc, ts=ts, bankB=bankB: [
                           e.matmul(bankB[0:128, 0:512],
                                    self.wpool[:, 8192 + ci * 1024 + oc * 128: 8192 + ci * 1024 + (oc + 1) * 128],
                                    kact[:, ci * T + ts * 512: ci * T + (ts + 1) * 512],
                                    start=(ci == 0), stop=(ci == NCHUNK - 1))
                           for ci in range(NCHUNK)][-1]),
                       nm + "mmB", 1,
                       waits=P.after(nm + "ksq", nm + "vmul"))
                P.prod("scalar",
                       (lambda e, bankA=bankA: e.activation(t1[:], bankA[0:128, 0:512], AF.Sigmoid)),
                       nm + "sgm", 1, waits=P.after(nm + "mmA", nm + "vmul"))
                P.prod("vector",
                       (lambda e, bankB=bankB: e.tensor_mul(t3[:], bankB[0:128, 0:512], t1[:])),
                       nm + "vmul", 1, waits=P.after(nm + "mmB", nm + "sgm", nm + "vadd"))
                P.prod("vector",
                       (lambda e, oc=oc, ts=ts: e.tensor_tensor(
                           self.xs[:, oc * T + ts * 512: oc * T + (ts + 1) * 512],
                           self.xs[:, oc * T + ts * 512: oc * T + (ts + 1) * 512],
                           t3[:], ALU.add)),
                       nm + "vadd", 1, waits=P.after(nm + "vmul"))
        import os
        if os.environ.get("DBG_L3F"):
            dbg1 = self.nc.dram_tensor("dbg_kact", [128, NCHUNK * T], BF16, kind="ExternalOutput").ap()
            dbg2 = self.nc.dram_tensor("dbg_xset", [128, 8192], BF16, kind="ExternalOutput").ap()
            for q in range(4):
                P.prod("sync", (lambda e, q=q: e.dma_start(out=dbg1[:, q * 4096:(q + 1) * 4096],
                                                           in_=kact[:, q * 4096:(q + 1) * 4096])),
                       "dbg", 16, waits=P.after(nm + "vadd"))
            for q in range(2):
                P.prod("sync", (lambda e, q=q: e.dma_start(out=dbg2[:, q * 4096:(q + 1) * 4096],
                                                           in_=self.scrb[:, q * 4096:(q + 1) * 4096])),
                       "dbg", 16, waits=P.after(nm + "vadd"))
            dbg3 = self.nc.dram_tensor("dbg_cols", [128, 32], F32, kind="ExternalOutput").ap()
            P.prod("sync", (lambda e: e.dma_start(out=dbg3, in_=self.l3f_cols[:])),
                   "dbg", 16, waits=P.after(nm + "vadd"))
            dbg4 = self.nc.dram_tensor("dbg_xb", [128, NCHUNK * T], BF16, kind="ExternalOutput").ap()
            for q in range(4):
                P.prod("sync", (lambda e, q=q: e.dma_start(out=dbg4[:, q * 4096:(q + 1) * 4096],
                                                           in_=self.xb[:, q * 4096:(q + 1) * 4096])),
                       "dbg", 16, waits=P.after(nm + "vadd"))
            P.op("sync", lambda e: e.nop(), waits=P.after("dbg"))
            return self.emit_postln(P.after(nm + "vadd", "dbg"))
        return self.emit_postln(P.after(nm + "vadd"))



    def prep_ternary(self, src_ap, dst_ap, o_dim, i_dim, chname):
        P = self.P
        nm = self.nid("tern")
        cw = min(i_dim, 2048)
        ncc = i_dim // cw
        wtout = self.scra[:, 0:cw]
        wrow = self.scra.ap().bitcast(F32)[:, 1024:1024 + cw]
        wabs = self.scra.ap().bitcast(F32)[:, 3072:3072 + cw]
        asum = self.scrf[:, 0:1]
        acol = self.scrf[:, 1:2]
        a11 = self.scrf[0:1, 2:3]
        tmp = self.dram(self.nid("terntmp"), [o_dim, i_dim], BF16)
        nch = o_dim // 128
        P.prod("vector", lambda e: e.memset(asum[:], 0.0), nm + "as", 1)
        for r in range(nch):
            for cc in range(ncc):
                P.prod("sync",
                       (lambda e, r=r, cc=cc: e.dma_start(
                           out=wrow[:], in_=src_ap[r * 128:(r + 1) * 128, cc * cw:(cc + 1) * cw])),
                       nm + "ld", 16,
                       waits=P.after(nm + "abs", nm + "sgn", nm + "st", "ptT", "ptS"))
                P.prod("scalar",
                       (lambda e: e.activation(wabs[:], wrow[:], AF.Abs, accum_out=acol[:])),
                       nm + "abs", 1, waits=P.after(nm + "ld", nm + "as"))
                P.prod("vector",
                       (lambda e: e.tensor_add(asum[:], asum[:], acol[:])),
                       nm + "as", 1, waits=P.after(nm + "abs"))
        P.prod("tensor",
               (lambda e: e.matmul(self.pb[5][0:1, 0:1], asum[:], self.onescolf[:, 0:1],
                                   start=True, stop=True)),
               nm + "t1", 1, waits=P.after(nm + "as", nm + "bc") + self.initw())
        P.prod("vector",
               (lambda e: e.tensor_scalar(a11[:], self.pb[5][0:1, 0:1],
                                          0.5 / (o_dim * i_dim), None, ALU.mult)),
               nm + "t2", 1, waits=P.after(nm + "t1"))
        P.prod("tensor",
               (lambda e: e.matmul(self.pb[5][0:128, 0:1], self.onesrow[0:1, 0:128], a11[:],
                                   start=True, stop=True)),
               nm + "t3", 1, waits=P.after(nm + "t2"))
        P.prod("vector",
               (lambda e: e.tensor_copy(acol[:], self.pb[5][0:128, 0:1])),
               nm + "bc", 1, waits=P.after(nm + "t3"))
        for r in range(nch):
            for cc in range(ncc):
                P.prod("sync",
                       (lambda e, r=r, cc=cc: e.dma_start(
                           out=wrow[:], in_=src_ap[r * 128:(r + 1) * 128, cc * cw:(cc + 1) * cw])),
                       nm + "ld2", 16, waits=P.after(nm + "gt", nm + "sgn", nm + "bc"))
                w2 = P.after(nm + "ld2")
                P.prod("scalar",
                       (lambda e: e.activation(wabs[:], wrow[:], AF.Abs)),
                       nm + "ab2", 1, waits=w2 + P.after(nm + "gt"))
                P.prod("vector",
                       (lambda e: e.tensor_scalar(wabs[:], wabs[:], acol[:], None, ALU.is_gt)),
                       nm + "gt", 1, waits=P.after(nm + "ab2"))
                P.prod("scalar",
                       (lambda e: e.activation(wrow[:], wrow[:], AF.Sign)),
                       nm + "sgn", 1, waits=w2 + P.after(nm + "mul", nm + "ab2")),
                P.prod("vector",
                       (lambda e: e.tensor_mul(wtout[:], wabs[:], wrow[:])),
                       nm + "mul", 1, waits=P.after(nm + "gt", nm + "sgn", nm + "st")),
                P.prod("sync",
                       (lambda e, r=r, cc=cc: e.dma_start(
                           out=tmp[r * 128:(r + 1) * 128, cc * cw:(cc + 1) * cw], in_=wtout[:])),
                       nm + "st", 16, waits=P.after(nm + "mul"))
        P.prod("sync", lambda e: e.nop(), chname + "_c", 1, waits=P.after(nm + "st"))
        self.prep_transpose(src_ap, dst_ap, o_dim, i_dim, chname, tmp=tmp)

    def emit_l0f(self, b):
        nc, P = self.nc, self.P
        wu = self.win_decl("L0_ffn_w_up", [4096, 1024])
        wd = self.win_decl("L0_ffn_w_down", [1024, 4096])
        key = "l0f"
        if key not in self.once:
            self.once.add(key)
            P.extra_waits = P.after("postln") + self.xbok()
            self.wc_l0f = (self.dram("wcWu", [1024, 4096]), self.dram("wcWd", [4096, 1024]))
            self.prep_ternary(wu, self.wc_l0f[0], 4096, 1024, key + "pu")
            self.prep_ternary(wd, self.wc_l0f[1], 1024, 4096, key + "pd")
            P.extra_waits = []
        wcU, wcD = self.wc_l0f
        nm = self.nid("l0f_")
        wprep = P.after(key + "pu", key + "pd", "postln")
        Hm = self.scra
        t1 = self.scrf[:, 0:512]
        t2 = self.scrf[:, 512:1024]
        t3 = self.scrf[:, 1024:1536]
        sqb = self.scrb[:, 0:512]
        r1 = self.sb("tph", [128, D], BF16).ap().bitcast(F32)[0:1, 0:512]
        stats = self.pb[5]
        bcb = self.pb[6]

        for ts in range(NTT):
            for c in range(NCHUNK):
                P.prod("scalar",
                       (lambda e, c=c, ts=ts: e.activation(
                           sqb[:], self.xb[:, c * T + ts * 512: c * T + (ts + 1) * 512],
                           AF.Square)),
                       nm + "sq", 1, waits=self.xbok() + P.after(nm + "smm", "postln"))
                P.prod("tensor",
                       (lambda e, c=c: e.matmul(stats[0:1, 0:512], self.onescol[:, 0:1], sqb[:],
                                                start=(c == 0), stop=(c == NCHUNK - 1))),
                       nm + "smm", 1, waits=P.after(nm + "sq", nm + "r1u", nm + "r2u") + self.initw())
            P.prod("scalar",
                   (lambda e: e.activation(r1[:], stats[0:1, 0:512], AF.Sqrt, scale=1.0 / D)),
                   nm + "r1a", 1, waits=P.after(nm + "smm"))
            P.prod("vector",
                   (lambda e: e.tensor_scalar(r1[:], r1[:], 1e-8, None, ALU.add)),
                   nm + "r1b", 1, waits=P.after(nm + "r1a"))
            P.prod("vector",
                   (lambda e: e.reciprocal(r1[:], r1[:])),
                   nm + "r1u", 1, waits=P.after(nm + "r1b"))
            P.prod("tensor",
                   (lambda e: e.matmul(bcb[0:128, 0:512], self.onesrow[0:1, 0:128], r1[:],
                                       start=True, stop=True)),
                   nm + "r1m", 1, waits=P.after(nm + "r1u", nm + "hmul")),
            P.prod("vector",
                   (lambda e: e.tensor_copy(t1[:], bcb[0:128, 0:512])),
                   nm + "r1c", 1, waits=P.after(nm + "r1m"))
            for half in range(2):
                self.load_w(wcU, self.wpool, 0, 1024, half * 2048, (half + 1) * 2048,
                            nm + "wu", wprep + P.after(nm + "mmH", nm + "mmO"))
                for hh in range(16):
                    hc = half * 16 + hh
                    bank = self.pb[hh % 4]
                    P.prod("tensor",
                           (lambda e, hc=hc, hh=hh, ts=ts, bank=bank: [
                               e.matmul(bank[0:128, 0:512],
                                        self.wpool[:, ci * 2048 + hh * 128: ci * 2048 + (hh + 1) * 128],
                                        self.xb[:, ci * T + ts * 512:ci * T + (ts + 1) * 512],
                                        start=(ci == 0), stop=(ci == NCHUNK - 1))
                               for ci in range(NCHUNK)][-1]),
                           nm + "mmH", 1,
                           waits=P.after(nm + "wu", nm + "r1c") + P.after_lag(nm + "hgel!", 3))
                    P.prod("vector",
                           (lambda e, bank=bank: e.tensor_mul(t2[:], bank[0:128, 0:512], t1[:])),
                           nm + "hmul", 1, waits=P.after(nm + "mmH", nm + "hsq"))
                    P.prod("scalar",
                           (lambda e, hc=hc: e.activation(
                               Hm[:, hc * 512:(hc + 1) * 512], t2[:], AF.Gelu)),
                           nm + "hgel!", 1, waits=P.after(nm + "hmul"))
                    P.prod("scalar",
                           (lambda e, hc=hc: e.activation(sqb[:], Hm[:, hc * 512:(hc + 1) * 512],
                                                          AF.Square)),
                           nm + "hsq", 1, waits=P.after(nm + "hgel!", nm + "smm2", nm + "smm"))
                    P.prod("tensor",
                           (lambda e, hc=hc: e.matmul(stats[0:1, 0:512], self.onescol[:, 0:1],
                                                      sqb[:], start=(hc == 0), stop=(hc == 31))),
                           nm + "smm2", 1, waits=P.after(nm + "hsq", nm + "r1m"))
            P.prod("scalar",
                   (lambda e: e.activation(r1[:], stats[0:1, 0:512], AF.Sqrt, scale=1.0 / 4096)),
                   nm + "r2a", 1, waits=P.after(nm + "smm2", nm + "oc1"))
            P.prod("vector",
                   (lambda e: e.tensor_scalar(r1[:], r1[:], 1e-8, None, ALU.add)),
                   nm + "r2b", 1, waits=P.after(nm + "r2a"))
            P.prod("vector",
                   (lambda e: e.reciprocal(r1[:], r1[:])),
                   nm + "r2u", 1, waits=P.after(nm + "r2b"))
            P.prod("tensor",
                   (lambda e: e.matmul(bcb[0:128, 0:512], self.onesrow[0:1, 0:128], r1[:],
                                       start=True, stop=True)),
                   nm + "r2m", 1, waits=P.after(nm + "r2u", nm + "oc1", nm + "hmul")),
            P.prod("vector",
                   (lambda e: e.tensor_copy(t1[:], bcb[0:128, 0:512])),
                   nm + "r2c", 1, waits=P.after(nm + "r2m"))
            for oc in range(NCHUNK):
                self.load_w(wcD, self.wpool2, 0, 4096, oc * 128, (oc + 1) * 128,
                            nm + "wd", wprep + P.after(nm + "mmO"))
                bank = self.pb[4 + oc % 1]
                P.prod("tensor",
                       (lambda e, oc=oc, ts=ts, bank=bank: [
                           e.matmul(bank[0:128, 0:512],
                                    self.wpool2[:, ci * 128: (ci + 1) * 128],
                                    Hm[:, ci * 512:(ci + 1) * 512],
                                    start=(ci == 0), stop=(ci == 31))
                           for ci in range(32)][-1]),
                       nm + "mmO", 1,
                       waits=P.after(nm + "wd", nm + "hgel!", nm + "r2c", nm + "oc1"))
                P.prod("vector",
                       (lambda e, bank=bank: e.tensor_mul(t3[:], bank[0:128, 0:512], t1[:])),
                       nm + "oc1", 1, waits=P.after(nm + "mmO", nm + "oc2"))
                P.prod("vector",
                       (lambda e, oc=oc, ts=ts: e.tensor_tensor(
                           self.xs[:, oc * T + ts * 512: oc * T + (ts + 1) * 512],
                           self.xs[:, oc * T + ts * 512: oc * T + (ts + 1) * 512],
                           t3[:], ALU.add)),
                       nm + "oc2", 1, waits=P.after(nm + "oc1"))
        return self.emit_postln(P.after(nm + "oc2"))



    def emit_l1f(self, b):
        nc, P = self.nc, self.P
        Wr = self.win_decl("L1_ffn_Wr", [1024, 8])
        W1 = self.win_decl("L1_ffn_W1", [8, 1024, 4096])
        b1v = self.win_decl("L1_ffn_b1", [8, 4096])
        W2 = self.win_decl("L1_ffn_W2", [8, 4096, 1024])
        b2v = self.win_decl("L1_ffn_b2", [8, 1024])
        key = "l1f"
        CAP = 320
        NR = 2560
        BIGF = 1e4
        BIGI = 1e7
        if key not in self.once:
            self.once.add(key)
            P.extra_waits = P.after("postln") + self.xbok()
            self.wc_l1f = (self.dram("wcMr", [1024, 8]),
                           self.dram("wcM1", [8192, 4096]),
                           self.dram("wcM2", [32768, 1024]))
            self.prep_cast(Wr, self.wc_l1f[0], 1024, 8, key + "p")
            w1f = W1.rearrange("e c h -> (e c) h")
            for r in range(0, 8192, 256):
                P.prod("gpsimd",
                       (lambda e, r=r: e.dma_start(out=self.wc_l1f[1][r:r + 256, :],
                                                   in_=w1f[r:r + 256, :])),
                       key + "p", 16)
            w2f = W2.rearrange("e h c -> (e h) c")
            self.prep_cast(w2f, self.wc_l1f[2], 32768, 1024, key + "p")
            self.l1f_bcols = nc.alloc_sbuf_tensor("l1f_bcols", [128, 320], F32)
            for e_ in range(8):
                self.load_vec_cols(b1v[e_], 4096, key + "b", dst=self.l1f_bcols,
                                   dst_off=e_ * 32)
                self.load_vec_cols(b2v[e_], 1024, key + "b", dst=self.l1f_bcols,
                                   dst_off=256 + e_ * 8)
            self.l1f_wr = nc.alloc_sbuf_tensor("l1f_wr", [128, 64], BF16)
            self.load_w(self.wc_l1f[0], self.l1f_wr, 0, 1024, 0, 8, key + "wr",
                        P.after(key + "p"))
            c_sut = nc.inline_tensor(np.triu(np.ones((128, 128)), 1).astype(BF),
                                     "c_l1f_sut")
            c_on = nc.inline_tensor(np.ones((128, 128), BF), "c_l1f_on")
            c_rev = nc.inline_tensor(
                np.ascontiguousarray((8.0 - np.arange(8, dtype=np.float32))[None, :]
                                     .repeat(128, 0)), "c_l1f_rev")
            tid = np.zeros((128, 96), dtype=np.float32)
            p_ = np.arange(128)
            for tt in range(16):
                tid[:, tt * 6 + 0] = tt * 4 + p_ // 32
                tid[:, tt * 6 + 1] = p_ % 32
                tid[:, tt * 6 + 2] = tt * 8 + p_ // 16
                tid[:, tt * 6 + 3] = (2 * p_) % 32
                tid[:, tt * 6 + 4] = tt * 8 + p_ // 16
                tid[:, tt * 6 + 5] = (2 * p_ + 1) % 32
            c_tid = nc.inline_tensor(tid.astype(BF), "c_l1f_tid")
            c_iot = nc.inline_tensor(
                np.ascontiguousarray(np.arange(512, dtype=np.float32)[None, :]
                                     .repeat(128, 0)), "c_l1f_iot")
            self.l1f_csut = c_sut
            self.l1f_cson = c_on
            self.l1f_crev = c_rev
            self.l1f_ctid = c_tid
            self.l1f_ciot = c_iot
            self.l1f_w = nc.alloc_sbuf_tensor("l1f_w", [128, 64], F32)
            self.l1f_sl6 = nc.alloc_sbuf_tensor("l1f_sl6", [128, 16], BF16)
            self.l1f_invi = nc.alloc_sbuf_tensor("l1f_invi", [128, 20], I32)
            self.l1f_si = nc.alloc_sbuf_tensor("l1f_si", [128, 20], I32)
            self.l1f_wc = nc.alloc_sbuf_tensor("l1f_wc", [128, 24], F32)
            self.l1f_buf = self.dram("l1f_buf", [NR, 1024])
            self.l1f_ybuf = self.dram("l1f_ybuf", [NR, 1024])
            self.l1f_xtm = self.dram("l1f_xtm", [T, 1024])
            self.l1f_ytok = self.dram("l1f_ytok", [2 * T, 1024])
            self.l1f_rows = self.dram("l1f_rows", [3, NR], F32)
            P.extra_waits = []
        nm = self.nid("l1f_")
        bcols = self.l1f_bcols
        buf, ybuf = self.l1f_buf, self.l1f_ybuf
        rt = self.scrf[:, 0:8]
        A = self.scrf[:, 8:16]
        Bv = self.scrf[:, 16:24]
        tot = self.scrf[:, 24:32]
        prev = self.scrf[:, 32:40]
        etmp = self.scrf[:, 48:56]
        CB = 56

        def C(i):
            return self.scrf[:, CB + i:CB + i + 1]
        ohsb = self.scrb[:, 6976:6984]
        xtm = self.scrb[:, 5632:6656]
        zb = self.scrb[:, 2560:3584]
        sutb = self.scrb[:, 3584:3712]
        onb = self.scrb[:, 3712:3840]
        tidb = self.scrb[:, 3840:3936]
        revb = self.scrb.ap().bitcast(F32)[:, 1968:1976]
        wcld = P.after("postln", nm + "pmm", nm + "rv", nm + "ia")
        P.prod("sync",
               (lambda e: e.dma_start(out=sutb[:], in_=self.l1f_csut.ap())),
               nm + "cld", 16, waits=wcld)
        P.prod("sync",
               (lambda e: e.dma_start(out=onb[:], in_=self.l1f_cson.ap())),
               nm + "cld", 16, waits=wcld)
        P.prod("sync",
               (lambda e: e.dma_start(out=tidb[:], in_=self.l1f_ctid.ap())),
               nm + "cld", 16, waits=wcld)
        P.prod("sync",
               (lambda e: e.dma_start(out=revb[:], in_=self.l1f_crev.ap())),
               nm + "cld", 16, waits=wcld)

        def rv(fn, *extra):
            P.prod("vector", fn, nm + "rv", 1,
                   waits=P.after(nm + "rv", nm + "rev") + sum(map(list, extra), []))

        P.prod("vector", lambda e: e.memset(zb[0:128, :], 0.0), nm + "zst", 1,
               waits=P.after("postln", nm + "fyw"))
        for k in range(2 * T // 128):
            P.prod("sync",
                   (lambda e, k=k: e.dma_start(
                       out=self.l1f_ytok[k * 128:(k + 1) * 128, :], in_=zb[0:128, :])),
                   nm + "fbz", 16, waits=P.after(nm + "zst", nm + "fgd"))
        P.prod("vector", lambda e: e.memset(prev[:], 0.0), nm + "rpv", 1,
               waits=P.after("postln", nm + "rto"))
        P.prod("vector", lambda e: e.memset(self.l1f_sl6[:], 1.0), nm + "ia", 1,
               waits=P.after("postln", nm + "imm"))
        for tt in range(16):
            P.prod("tensor",
                   (lambda e, tt=tt: [
                       e.matmul(self.pb[0][0:128, 0:8],
                                self.xb[:, c * T + tt * 128:c * T + tt * 128 + 128],
                                self.l1f_wr[:, c * 8:(c + 1) * 8],
                                start=(c == 0), stop=(c == NCHUNK - 1))
                       for c in range(NCHUNK)][-1]),
                   nm + "zmm", 1,
                   waits=P.after(key + "wr", nm + "rev", "postln") + self.xbok())
            P.prod("vector", lambda e: e.tensor_copy(rt[:], self.pb[0][0:128, 0:8]),
                   nm + "rev", 1, waits=P.after(nm + "zmm", nm + "rv"))
            rv(lambda e: e.tensor_reduce(C(0), rt[:], AX.X, ALU.max))
            rv(lambda e: e.tensor_scalar(A[:], rt[:], C(0), None, ALU.is_equal))
            rv(lambda e: e.tensor_tensor(A[:], A[:], revb[:], ALU.mult),
               P.after(nm + "cld"))
            rv(lambda e: e.tensor_reduce(C(2), A[:], AX.X, ALU.max))
            rv(lambda e: e.tensor_scalar(A[:], A[:], C(2), None, ALU.is_equal))
            rv(lambda e: e.tensor_scalar(C(3), C(2), -1.0, 8.0, ALU.mult, ALU.add))
            rv(lambda e: e.scalar_tensor_tensor(etmp[:], A[:], -BIGF, rt[:],
                                                ALU.mult, ALU.add))
            rv(lambda e: e.tensor_reduce(C(4), etmp[:], AX.X, ALU.max))
            rv(lambda e: e.tensor_scalar(Bv[:], etmp[:], C(4), None, ALU.is_equal))
            rv(lambda e: e.tensor_tensor(Bv[:], Bv[:], revb[:], ALU.mult))
            rv(lambda e: e.tensor_reduce(C(5), Bv[:], AX.X, ALU.max))
            rv(lambda e: e.tensor_scalar(Bv[:], Bv[:], C(5), None, ALU.is_equal))
            rv(lambda e: e.tensor_scalar(C(6), C(5), -1.0, 8.0, ALU.mult, ALU.add))
            rv(lambda e: e.tensor_scalar(C(1), C(0), -1.0, None, ALU.mult))
            P.prod("scalar",
                   (lambda e: e.activation(etmp[:], rt[:], AF.Exp, bias=C(1),
                                           accum_out=C(7))),
                   nm + "rs", 1, waits=P.after(nm + "rv", nm + "rs"))
            rv(lambda e: e.reciprocal(C(8), C(7)), P.after(nm + "rs"))
            P.prod("scalar",
                   (lambda e: e.activation(C(9), C(4), AF.Exp, bias=C(1))),
                   nm + "rs", 1, waits=P.after(nm + "rv", nm + "rs"))
            rv(lambda e: e.tensor_tensor(C(10), C(9), C(8), ALU.mult), P.after(nm + "rs"))
            rv(lambda e: e.tensor_tensor(etmp[:], A[:], Bv[:], ALU.add))
            rv(lambda e: e.tensor_copy(ohsb[:], etmp[:]))
            P.prod("tensor",
                   (lambda e: e.matmul(self.pb[1][0:128, 0:8], sutb[:],
                                       ohsb[:], start=True, stop=True)),
                   nm + "pmm", 1,
                   waits=P.after(nm + "rv", nm + "rto", nm + "cld"))
            P.prod("tensor",
                   (lambda e: e.matmul(self.pb[2][0:128, 0:8], onb[:],
                                       ohsb[:], start=True, stop=True)),
                   nm + "pmm", 1, waits=P.after(nm + "rv", nm + "rpv", nm + "cld"))
            P.prod("vector",
                   (lambda e: e.tensor_tensor(tot[:], self.pb[1][0:128, 0:8], prev[:],
                                              ALU.add)),
                   nm + "rto", 1, waits=P.after(nm + "pmm", nm + "rpv", nm + "rv"))
            rv(lambda e: e.tensor_tensor(etmp[:], A[:], tot[:], ALU.mult),
               P.after(nm + "rto"))
            rv(lambda e: e.tensor_reduce(C(11), etmp[:], AX.X, ALU.add))
            rv(lambda e: e.tensor_tensor(Bv[:], Bv[:], tot[:], ALU.mult))
            rv(lambda e: e.tensor_reduce(C(12), Bv[:], AX.X, ALU.add))
            P.prod("vector",
                   (lambda e: e.tensor_tensor(prev[:], prev[:], self.pb[2][0:128, 0:8],
                                              ALU.add)),
                   nm + "rpv", 1, waits=P.after(nm + "pmm", nm + "rto"))
            rv(lambda e: e.tensor_scalar(C(13), C(11), float(CAP), None, ALU.is_lt))
            rv(lambda e: e.tensor_scalar(C(14), C(12), float(CAP), None, ALU.is_lt))
            rv(lambda e, tt=tt: e.tensor_tensor(self.l1f_w[:, tt * 4:tt * 4 + 1],
                                                C(13), C(8), ALU.mult),
               P.after(nm + "ia"))
            rv(lambda e, tt=tt: e.tensor_tensor(self.l1f_w[:, tt * 4 + 1:tt * 4 + 2],
                                                C(14), C(10), ALU.mult))
            rv(lambda e: e.tensor_scalar(C(15), C(11), float(CAP), BIGI,
                                         ALU.is_ge, ALU.mult))
            rv(lambda e, tt=tt: e.scalar_tensor_tensor(
                self.l1f_w[:, tt * 4 + 2:tt * 4 + 3], C(3), float(CAP), C(11),
                ALU.mult, ALU.add))
            rv(lambda e, tt=tt: e.tensor_tensor(
                self.l1f_w[:, tt * 4 + 2:tt * 4 + 3],
                self.l1f_w[:, tt * 4 + 2:tt * 4 + 3], C(15), ALU.add))
            rv(lambda e: e.tensor_scalar(C(15), C(12), float(CAP), BIGI,
                                         ALU.is_ge, ALU.mult))
            rv(lambda e, tt=tt: e.scalar_tensor_tensor(
                self.l1f_w[:, tt * 4 + 3:tt * 4 + 4], C(6), float(CAP), C(12),
                ALU.mult, ALU.add))
            rv(lambda e, tt=tt: e.tensor_tensor(
                self.l1f_w[:, tt * 4 + 3:tt * 4 + 4],
                self.l1f_w[:, tt * 4 + 3:tt * 4 + 4], C(15), ALU.add))
            for c in range(NCHUNK):
                bank = self.pb[3 + c % 2]
                P.prod("tensor",
                       (lambda e, c=c, tt=tt, bank=bank: e.transpose(
                           bank.ap().bitcast(BF16)[0:128, 0:128],
                           self.xb[:, c * T + tt * 128:c * T + tt * 128 + 128],
                           self.ident[:])),
                       nm + "xtp", 1,
                       waits=self.xbok() + self.initw() + P.after(nm + "xte"))
                P.prod("vector",
                       (lambda e, c=c, bank=bank: e.tensor_copy(
                           xtm[:, c * 128:(c + 1) * 128],
                           bank.ap().bitcast(BF16)[0:128, 0:128])),
                       nm + "xte", 1, waits=P.after(nm + "xtp", nm + "fxw"))
            P.prod("sync",
                   (lambda e, tt=tt: e.dma_start(
                       out=self.l1f_xtm[tt * 128:(tt + 1) * 128, :], in_=xtm[:])),
                   nm + "fxw", 16, waits=P.after(nm + "xte", nm + "fxg"))
        sl6 = self.l1f_sl6
        iot = self.scrf[:, 1024:1536]
        P.prod("sync",
               (lambda e: e.dma_start(out=iot, in_=self.l1f_ciot.ap())),
               nm + "ild", 16, waits=P.after("postln", nm + "ieq"))
        eqf = (self.scrf[:, 0:512], self.scrf[:, 512:1024])
        eqb = (self.scrb[:, 5632:6144], self.scrb[:, 6144:6656])
        segs = [self.scra.ap().bitcast(F32)[0:1, 5120 + q * 512:5120 + (q + 1) * 512]
                for q in range(6)]
        seg6 = self.scra.ap().bitcast(F32)[0:6, 5120:5632]
        rowst = self.scra.ap().bitcast(F32)[0:1, 5120:7680]
        ibank = self.pb[5]
        for rc in range(5):
            for ts2 in range(32):
                tt, sl = ts2 // 2, ts2 % 2
                db = ts2 % 2
                wia = P.after(nm + "imm") + P.after(nm + "ia")
                P.prod("vector",
                       (lambda e, tt=tt, db=db: e.tensor_copy(
                           sl6[:, db * 8:db * 8 + 2],
                           tidb[:, tt * 6:tt * 6 + 2])),
                       nm + "ia", 1, waits=wia + P.after(nm + "cld"))
                P.prod("vector",
                       (lambda e, tt=tt, sl=sl, db=db: e.tensor_copy(
                           sl6[:, db * 8 + 2:db * 8 + 4],
                           tidb[:, tt * 6 + 2 + 2 * sl:tt * 6 + 4 + 2 * sl])),
                       nm + "ia", 1, waits=P.after(nm + "ia"))
                P.prod("vector",
                       (lambda e, tt=tt, sl=sl, db=db: e.tensor_copy(
                           sl6[:, db * 8 + 4:db * 8 + 5],
                           self.l1f_w[:, tt * 4 + sl:tt * 4 + sl + 1])),
                       nm + "ia", 1, waits=P.after(nm + "ia", nm + "rv"))
                P.prod("vector",
                       (lambda e, rc=rc, tt=tt, sl=sl, db=db: e.tensor_scalar(
                           eqf[db][:], iot, float(rc * 512),
                           self.l1f_w[:, tt * 4 + 2 + sl:tt * 4 + 3 + sl],
                           ALU.add, ALU.is_equal)),
                       nm + "ieq", 1,
                       waits=P.after(nm + "imm", nm + "rv", nm + "ild"))
                P.prod("vector",
                       (lambda e, db=db: e.tensor_copy(eqb[db][:], eqf[db][:])),
                       nm + "ieb", 1, waits=P.after(nm + "ieq", nm + "fxw"))
                P.prod("tensor",
                       (lambda e, db=db, first=(ts2 == 0), last=(ts2 == 31): e.matmul(
                           ibank[0:6, 0:512], sl6[:, db * 8:db * 8 + 6], eqb[db][:],
                           start=first, stop=last)),
                       nm + "imm", 1,
                       waits=P.after(nm + "ieb", nm + "ia", nm + "iev"))
            P.prod("vector",
                   (lambda e: e.tensor_copy(seg6[:], ibank[0:6, 0:512])),
                   nm + "iev", 1, waits=P.after(nm + "imm", nm + "irw", nm + "icb"))
            for q in range(1, 6):
                P.prod("sync",
                       (lambda e, q=q: e.dma_start(
                           out=segs[q],
                           in_=self.scra.ap().bitcast(F32)[q:q + 1, 5120:5632])),
                       nm + "ish", 16, waits=P.after(nm + "iev", nm + "icb",
                                                     nm + "irw"))
            wsh = P.after(nm + "ish")
            P.prod("vector",
                   (lambda e: e.scalar_tensor_tensor(segs[0], segs[0], 32.0, segs[1],
                                                     ALU.mult, ALU.add)),
                   nm + "icb", 1, waits=wsh + P.after(nm + "iev"))
            P.prod("vector",
                   (lambda e: e.scalar_tensor_tensor(segs[2], segs[2], 32.0, segs[3],
                                                     ALU.mult, ALU.add)),
                   nm + "icb", 1, waits=P.after(nm + "icb"))
            P.prod("vector",
                   (lambda e: e.tensor_scalar(segs[1], segs[5], -BIGI, BIGI,
                                              ALU.mult, ALU.add)),
                   nm + "icb", 1, waits=P.after(nm + "icb"))
            P.prod("vector",
                   (lambda e: e.tensor_tensor(segs[2], segs[2], segs[1], ALU.add)),
                   nm + "icb", 1, waits=P.after(nm + "icb"))
            for j, q in ((0, 0), (1, 2), (2, 4)):
                P.prod("sync",
                       (lambda e, j=j, q=q, rc=rc: e.dma_start(
                           out=self.l1f_rows[j:j + 1, rc * 512:(rc + 1) * 512],
                           in_=segs[q])),
                       nm + "irw", 16, waits=P.after(nm + "icb"))
        xb7 = self.pb[6]
        for j in range(3):
            P.prod("sync",
                   (lambda e, j=j: e.dma_start(out=rowst,
                                               in_=self.l1f_rows[j:j + 1, :])),
                   nm + "irl", 16, waits=P.after(nm + "irw", nm + "ixm"))
            if j < 2:
                dst = self.l1f_invi if j == 0 else self.l1f_si
                for k in range(20):
                    P.prod("tensor",
                           (lambda e, k=k: e.matmul(
                               xb7[0:128, 0:1],
                               self.scra.ap().bitcast(F32)[0:1,
                                                           5120 + k * 128:5248 + k * 128],
                               self.onescolf[0:1, 0:1], start=True, stop=True)),
                           nm + "ixm", 1,
                           waits=P.after(nm + "irl", nm + "ixe") + self.initw())
                    P.prod("vector",
                           (lambda e, k=k, dst=dst: e.tensor_copy(
                               dst[:, k:k + 1], xb7[0:128, 0:1])),
                           nm + "ixe", 1, waits=P.after(nm + "ixm", nm + "fxg",
                                                        nm + "fys"))
            else:
                for ex in range(8):
                    for r in range(3):
                        nt = (128, 128, 64)[r]
                        off = ex * 320 + r * 128
                        P.prod("tensor",
                               (lambda e, off=off, nt=nt: e.matmul(
                                   xb7[0:nt, 0:1],
                                   self.scra.ap().bitcast(F32)[0:1,
                                                               5120 + off:5120 + off + nt],
                                   self.onescolf[0:1, 0:1], start=True, stop=True)),
                               nm + "ixm", 1,
                               waits=P.after(nm + "irl", nm + "ixe") + self.initw())
                        P.prod("vector",
                               (lambda e, ex=ex, r=r, nt=nt: e.tensor_copy(
                                   self.l1f_wc[0:nt, ex * 3 + r:ex * 3 + r + 1],
                                   xb7[0:nt, 0:1])),
                               nm + "ixe", 1, waits=P.after(nm + "ixm", nm + "ywf"))
        stgx = self.scrb[:, 5632:6656]
        for k in range(20):
            P.prod("gpsimd",
                   (lambda e, k=k: e.indirect_dma_start(
                       stgx[:], None, self.l1f_xtm,
                       IndirectOffsetOnAxis(self.l1f_invi[:, k:k + 1], 0))),
                   nm + "fxg", 16,
                   waits=P.after(nm + "ixe", nm + "fxw", nm + "fxs", nm + "imm"))
            P.prod("sync",
                   (lambda e, k=k: e.dma_start(out=buf[k * 128:(k + 1) * 128, :],
                                               in_=stgx[:])),
                   nm + "fxs", 16, waits=P.after(nm + "fxg", nm + "fbl"))
        xf = self.scrb[:, 0:2560]
        stg = self.scrb[:, 5632:6656]
        ytile = self.scrb[:, 6656:6976]
        NTOK = (128, 128, 64)
        for ex in range(8):
            for r in range(3):
                nt = NTOK[r]
                P.prod("sync",
                       (lambda e, ex=ex, r=r, nt=nt: e.dma_start(
                           out=stg[0:nt, :],
                           in_=buf[ex * CAP + r * 128:ex * CAP + r * 128 + nt, :])),
                       nm + "fbl", 16,
                       waits=P.after(nm + "fxs", nm + "fxt", nm + "imm"))
                for c in range(NCHUNK):
                    bank = self.pb[3 + c % 2]
                    P.prod("tensor",
                           (lambda e, c=c, nt=nt, bank=bank: e.transpose(
                               bank.ap().bitcast(BF16)[0:128, 0:nt],
                               stg[0:nt, c * 128:(c + 1) * 128],
                               self.ident[0:nt, 0:nt])),
                           nm + "fxt", 1,
                           waits=P.after(nm + "fbl", nm + "fxe") + self.initw())
                    P.prod("vector",
                           (lambda e, c=c, r=r, nt=nt, bank=bank: e.tensor_copy(
                               xf[0:128, c * 320 + r * 128:c * 320 + r * 128 + nt],
                               bank.ap().bitcast(BF16)[0:128, 0:nt])),
                           nm + "fxe", 1, waits=P.after(nm + "fxt", nm + "hmm"))
            for half in range(2):
                for ci in range(NCHUNK):
                    P.prod("sync",
                           (lambda e, ex=ex, half=half, ci=ci: e.dma_start(
                               out=self.wpool[:, ci * 2048:(ci + 1) * 2048],
                               in_=self.wc_l1f[1][ex * 1024 + ci * 128:
                                                  ex * 1024 + (ci + 1) * 128,
                                                  half * 2048:(half + 1) * 2048])),
                           nm + "wl", 16,
                           waits=P.after(key + "p", "postln", nm + "hmm", nm + "imm"))
                for hh in range(16):
                    bank = self.pb[hh % 4]
                    P.prod("tensor",
                           (lambda e, hh=hh, bank=bank: [
                               e.matmul(bank[0:128, 0:320],
                                        self.wpool[:, ci * 2048 + hh * 128:
                                                   ci * 2048 + (hh + 1) * 128],
                                        xf[:, ci * 320:(ci + 1) * 320],
                                        start=(ci == 0), stop=(ci == NCHUNK - 1))
                               for ci in range(NCHUNK)][-1]),
                           nm + "hmm", 1,
                           waits=P.after(nm + "wl", nm + "fxe")
                           + P.after_lag(nm + "hev!", 3))
                    P.prod("vector",
                           (lambda e, ex=ex, half=half, hh=hh, bank=bank: e.tensor_scalar(
                               self.scra[:, (half * 16 + hh) * 320:
                                         (half * 16 + hh + 1) * 320],
                               bank[0:128, 0:320],
                               bcols[:, ex * 32 + half * 16 + hh:
                                     ex * 32 + half * 16 + hh + 1],
                               0.0, ALU.add, ALU.max)),
                           nm + "hev!", 1, waits=P.after(nm + "hmm"))
            for q in range(4):
                for ci in range(NCHUNK):
                    P.prod("sync",
                           (lambda e, ex=ex, q=q, ci=ci: e.dma_start(
                               out=self.wpool2[:, ci * 1024:(ci + 1) * 1024],
                               in_=self.wc_l1f[2][ex * 4096 + q * 1024 + ci * 128:
                                                  ex * 4096 + q * 1024 + (ci + 1) * 128, :])),
                           nm + "wm", 16,
                           waits=P.after(key + "p", "postln", nm + "dmm"))
                for oc in range(NCHUNK):
                    P.prod("tensor",
                           (lambda e, q=q, oc=oc: [
                               e.matmul(self.pb[oc][0:128, 0:320],
                                        self.wpool2[:, hc * 1024 + oc * 128:
                                                    hc * 1024 + (oc + 1) * 128],
                                        self.scra[:, (q * 8 + hc) * 320:
                                                  (q * 8 + hc + 1) * 320],
                                        start=(q == 0 and hc == 0),
                                        stop=(q == 3 and hc == 7))
                               for hc in range(8)][-1]),
                           nm + "dmm", 1,
                           waits=P.after(nm + "wm", nm + "hev!", nm + "yev", nm + "fye"))
            for oc in range(NCHUNK):
                P.prod("vector",
                       (lambda e, ex=ex, oc=oc: e.tensor_scalar(
                           ytile[:], self.pb[oc][0:128, 0:320],
                           bcols[:, 256 + ex * 8 + oc:256 + ex * 8 + oc + 1],
                           None, ALU.add)),
                       nm + "yev", 1, waits=P.after(nm + "dmm", nm + "fyt"))
                for r in range(3):
                    nt = NTOK[r]
                    P.prod("tensor",
                           (lambda e, oc=oc, r=r, nt=nt: e.transpose(
                               self.pb[oc].ap().bitcast(BF16)[0:nt, 0:128],
                               ytile[0:128, r * 128:r * 128 + nt],
                               self.ident[:])),
                           nm + "fyt", 1,
                           waits=P.after(nm + "yev", nm + "fye") + self.initw())
                    P.prod("vector",
                           (lambda e, oc=oc, r=r, nt=nt: e.tensor_copy(
                               self.scrb[0:nt, 2560 + r * 1024 + oc * 128:
                                         2560 + r * 1024 + (oc + 1) * 128],
                               self.pb[oc].ap().bitcast(BF16)[0:nt, 0:128])),
                           nm + "fye", 1, waits=P.after(nm + "fyt", nm + "ywf",
                                                        nm + "fyw", nm + "fbz"))
            for r in range(3):
                nt = NTOK[r]
                P.prod("vector",
                       (lambda e, ex=ex, r=r, nt=nt: e.tensor_scalar(
                           self.scrb[0:nt, 2560 + r * 1024:2560 + (r + 1) * 1024],
                           self.scrb[0:nt, 2560 + r * 1024:2560 + (r + 1) * 1024],
                           self.l1f_wc[0:nt, ex * 3 + r:ex * 3 + r + 1],
                           None, ALU.mult)),
                       nm + "ywf", 1, waits=P.after(nm + "fye", nm + "ixe"))
                P.prod("sync",
                       (lambda e, ex=ex, r=r, nt=nt: e.dma_start(
                           out=ybuf[ex * CAP + r * 128:ex * CAP + r * 128 + nt, :],
                           in_=self.scrb[0:nt, 2560 + r * 1024:2560 + (r + 1) * 1024])),
                       nm + "fyw", 16, waits=P.after(nm + "ywf", nm + "fys"))
        for k in range(20):
            P.prod("sync",
                   (lambda e, k=k: e.dma_start(out=stgx[:],
                                               in_=ybuf[k * 128:(k + 1) * 128, :])),
                   nm + "fyl", 16,
                   waits=P.after(nm + "fyw", nm + "fys", nm + "fbl"))
            P.prod("gpsimd",
                   (lambda e, k=k: e.indirect_dma_start(
                       self.l1f_ytok, IndirectOffsetOnAxis(self.l1f_si[:, k:k + 1], 0),
                       stgx[:], None,
                       bounds_check=2 * T - 1, oob_is_err=False)),
                   nm + "fys", 16,
                   waits=P.after(nm + "fyl", nm + "fbz", nm + "ixe", nm + "fgd"))
        g1sb = self.scrb[:, 0:1024]
        g2sb = self.scrb[:, 1024:2048]
        cb = self.scrb[:, 2048:3072]
        y2v = self.l1f_ytok.rearrange("(t s) c -> t s c", s=2)
        for tt in range(16):
            P.prod("sync",
                   (lambda e, tt=tt: e.dma_start(
                       out=g1sb[:], in_=y2v[tt * 128:(tt + 1) * 128, 0, :])),
                   nm + "fgd", 16, waits=P.after(nm + "fys", nm + "hmm", nm + "cm"))
            P.prod("sync",
                   (lambda e, tt=tt: e.dma_start(
                       out=g2sb[:], in_=y2v[tt * 128:(tt + 1) * 128, 1, :])),
                   nm + "fgd", 16, waits=P.after(nm + "fys", nm + "hmm", nm + "cm"))
            P.prod("vector",
                   (lambda e: e.tensor_tensor(cb[:], g1sb[:], g2sb[:], ALU.add)),
                   nm + "cm", 1, waits=P.after(nm + "fgd", nm + "cmt", nm + "fyw"))
            for c in range(NCHUNK):
                bank = self.pb[3 + c % 2]
                P.prod("tensor",
                       (lambda e, c=c, bank=bank: e.transpose(
                           bank.ap().bitcast(BF16)[0:128, 0:128],
                           cb[:, c * 128:(c + 1) * 128], self.ident[:])),
                       nm + "cmt", 1,
                       waits=P.after(nm + "cm", nm + "cme") + self.initw())
                P.prod("vector",
                       (lambda e, c=c, tt=tt, bank=bank: e.tensor_tensor(
                           self.xs[:, c * T + tt * 128:c * T + tt * 128 + 128],
                           self.xs[:, c * T + tt * 128:c * T + tt * 128 + 128],
                           bank.ap().bitcast(BF16)[0:128, 0:128], ALU.add)),
                       nm + "cme", 1, waits=P.after(nm + "cmt"))
        return self.emit_postln(P.after(nm + "cme"))

    def emit_l1a(self, b):
        nc, P = self.nc, self.P
        wqkv = self.win_decl("L1_attn_w_qkv", [3072, 1024])
        wout = self.win_decl("L1_attn_w_out", [1024, 1024])
        bout = self.win_decl("L1_attn_b_out", [1024])
        key = "l1a"
        if key not in self.once:
            self.once.add(key)
            P.extra_waits = P.after("postln") + self.xbok()
            self.wc_l1a = (self.dram("wcQKV", [1024, 3072]), self.dram("wcWo1", [1024, 1024]))
            t1_ = self.prep_transpose_cast(wqkv, 3072, 1024, key + "p")
            t2_ = self.prep_transpose_cast(wout, 1024, 1024, key + "p")
            self.prep_transpose(wqkv, self.wc_l1a[0], 3072, 1024, key + "p", tmp=t1_)
            self.prep_transpose(wout, self.wc_l1a[1], 1024, 1024, key + "p", tmp=t2_)
            self.l1a_cols = self.nc.alloc_sbuf_tensor("l1a_cols", [128, 8], F32)
            self.load_vec_cols(bout, 1024, key + "b", dst=self.l1a_cols, dst_off=0)
            P.extra_waits = []
        if not hasattr(self, "attn_dram"):
            self.attn_dram = self.dram("attn_dram", [1024, T], BF16)
        wcQKV, wcWo = self.wc_l1a
        nm = self.nid("l1a_")
        wprep = P.after(key + "p", key + "b", "postln")

        qf = self.scra[:, 0:2048]
        kf = self.scra[:, 2048:4096]
        vf = self.scra[:, 4096:6144]
        vtm = self.scra[:, 6144:8224]
        et = self.scra[:, 8224:9248]
        qkp = self.scra[:, 9248:11296]
        keepb = self.scra[0:1, 11296:13344]
        wpoolF = self.wpool.ap().bitcast(F32)
        scrbF = self.scrb.ap().bitcast(F32)
        Drow = wpoolF[0:1, 4096:6144]
        diagrow = wpoolF[0:1, 6144:8192]
        keepf = scrbF[0:1, 1536:3584]
        kbias = self.sb("l1a_kbias", [128, 16], F32)
        g11 = self.scrf[0:1, 512:513]
        tq = self.scrf[0:1, 513:514]
        tk = self.scrf[0:1, 514:515]
        gcol = self.scrf[:, 515:516]
        nbg = self.scrf[:, 516:517]
        r512 = self.scrf[0:1, 517:1029]
        m11 = self.scrf[0:1, 1030:1031]
        s11 = self.scrf[0:1, 1031:1032]
        SC = 1.0 / 8.0
        BIG = 30000.0

        for hp in range(8):
            wv = wprep + P.after(nm + "pvm", nm + "expd", nm + "pmm")
            for woff in (0, 1024, 2048):
                self.load_w(wcQKV, self.wpool, woff, 1024, woff + hp * 128, woff + hp * 128 + 128,
                            nm + "wl", wv)
            for j, dst in enumerate((qf, kf, vf)):
                woff = j * 1024
                for ts in range(NTT):
                    bank = self.pb[ts % 2]
                    P.prod("tensor",
                           (lambda e, dst=dst, woff=woff, ts=ts, bank=bank: [
                               e.matmul(bank[0:128, 0:512],
                                        self.wpool[:, woff + ci * 128: woff + (ci + 1) * 128],
                                        self.xb[:, ci * T + ts * 512:ci * T + (ts + 1) * 512],
                                        start=(ci == 0), stop=(ci == NCHUNK - 1))
                               for ci in range(NCHUNK)][-1]),
                           nm + "pmm", 1,
                           waits=P.after(nm + "wl", nm + "pev", "postln") + self.xbok())
                    P.prod("vector",
                           (lambda e, dst=dst, ts=ts, bank=bank: e.tensor_copy(
                               dst[:, ts * 512:(ts + 1) * 512], bank[0:128, 0:512])),
                           nm + "pev", 1, waits=P.after(nm + "pmm", nm + "qk", nm + "gsq"))
            P.prod("vector", lambda e: e.nop(), nm + "pq", 1, waits=P.after(nm + "pev"))
            P.prod("vector", lambda e: e.memset(vtm[:], 1.0), nm + "vms", 1,
                   waits=P.after(nm + "pvm"))
            for h in range(2):
                for kt in range(16):
                    pv = self.pb[2 + kt % 2].ap().bitcast(BF16)
                    P.prod("tensor",
                           (lambda e, h=h, kt=kt, pv=pv: e.transpose(
                               pv[0:128, 0:64],
                               vf[h * 64:(h + 1) * 64, kt * 128:(kt + 1) * 128],
                               self.ident[h * 64:h * 64 + 64, h * 64:h * 64 + 64])),
                           nm + "vtr", 1,
                           waits=P.after(nm + "pq", nm + "vev") + self.initw())
                    P.prod("vector",
                           (lambda e, h=h, kt=kt, pv=pv: e.tensor_copy(
                               vtm[:, h * 1040 + kt * 65: h * 1040 + kt * 65 + 64], pv[0:128, 0:64])),
                           nm + "vev", 1, waits=P.after(nm + "vtr", nm + "vms"))
            for j, (src, dst11) in enumerate(((qf, tq), (kf, tk))):
                for ts in range(NTT):
                    P.prod("scalar",
                           (lambda e, src=src, ts=ts: e.activation(
                               self.scrb[:, 0:512], src[:, ts * 512:(ts + 1) * 512], AF.Square)),
                           nm + "gsq", 1, waits=P.after(nm + "pq", nm + "gmm", nm + "ost", "postln"))
                    P.prod("tensor",
                           (lambda e, ts=ts: e.matmul(
                               self.pb[4][0:1, 0:512], self.onescol[:, 0:1], self.scrb[:, 0:512],
                               start=True, stop=True)),
                           nm + "gmm", 1, waits=P.after(nm + "gsq", nm + "gcp") + self.initw())
                    P.prod("vector",
                           (lambda e, ts=ts: e.tensor_copy(Drow[:, ts * 512:(ts + 1) * 512],
                                                           self.pb[4][0:1, 0:512])),
                           nm + "gcp", 1, waits=P.after(nm + "gmm", nm + "grm"))
                P.prod("vector",
                       (lambda e: e.reduce_max(g11[:], Drow[:], axis=AX.X)),
                       nm + "grm", 1, waits=P.after(nm + "gcp"))
                P.prod("scalar",
                       (lambda e, dst11=dst11: e.activation(dst11, g11[:], AF.Sqrt)),
                       nm + "gsr", 1, waits=P.after(nm + "grm"))
            P.prod("vector",
                   (lambda e: e.tensor_mul(g11[:], tq[:], tk[:])),
                   nm + "g1", 1, waits=P.after(nm + "gsr"))
            P.prod("vector",
                   (lambda e: e.tensor_scalar(g11[:], g11[:], -SC, None, ALU.mult)),
                   nm + "g2", 1, waits=P.after(nm + "g1"))
            P.prod("tensor",
                   (lambda e: e.matmul(self.pb[4][0:128, 0:1], self.onesrow[0:1, 0:128],
                                       g11[:], start=True, stop=True)),
                   nm + "g3", 1, waits=P.after(nm + "g2", nm + "gcp")),
            P.prod("vector",
                   (lambda e: e.tensor_copy(gcol[:], self.pb[4][0:128, 0:1])),
                   nm + "g4", 1, waits=P.after(nm + "g3"))
            P.prod("vector",
                   (lambda e: e.tensor_scalar(nbg[:], gcol[:], -BIG, None, ALU.add)),
                   nm + "g5", 1, waits=P.after(nm + "g4"))

            for h in range(2):
                hb = h * 64
                hglob = hp * 2 + h
                for ts in range(NTT):
                    P.prod("vector",
                           (lambda e, ts=ts, hb=hb: e.tensor_mul(
                               qkp[hb:hb + 64, ts * 512:(ts + 1) * 512],
                               qf[hb:hb + 64, ts * 512:(ts + 1) * 512],
                               kf[hb:hb + 64, ts * 512:(ts + 1) * 512])),
                           nm + "qk", 1, waits=P.after(nm + "pq", nm + "dgm"))
                    P.prod("tensor",
                           (lambda e, ts=ts, hb=hb: e.matmul(
                               self.pb[4][0:1, 0:512],
                               self.onescol[hb:hb + 64, 0:1],
                               qkp[hb:hb + 64, ts * 512:(ts + 1) * 512],
                               start=True, stop=True)),
                           nm + "dgm", 1, waits=P.after(nm + "qk", nm + "dgc", nm + "g4", nm + "k13")),
                    P.prod("vector",
                           (lambda e, ts=ts: e.tensor_copy(diagrow[:, ts * 512:(ts + 1) * 512],
                                                           self.pb[4][0:1, 0:512])),
                           nm + "dgc", 1, waits=P.after(nm + "dgm"))
                for qt in range(NTT):
                    for kt in range(16):
                        bank = self.pb[kt % 2]
                        P.prod("tensor",
                               (lambda e, kt=kt, qt=qt, hb=hb, bank=bank: e.matmul(
                                   bank[0:128, 0:512],
                                   kf[hb:hb + 64, kt * 128:(kt + 1) * 128],
                                   qf[hb:hb + 64, qt * 512:(qt + 1) * 512],
                                   start=True, stop=True)),
                               nm + "smm", 1,
                               waits=P.after(nm + "expd", nm + "g4") + P.after_lag(nm + "lag!", 2)),
                        esl = et[:, (kt % 2) * 512:(kt % 2) * 512 + 512]
                        P.prod("scalar",
                               (lambda e, bank=bank, esl=esl: e.activation(
                                   esl, bank[0:128, 0:512], AF.Exp, bias=gcol[:], scale=SC)),
                               nm + "expd", 1, waits=P.after(nm + "smm", nm + "dmm")),
                        P.prod("tensor",
                               (lambda e, kt=kt, esl=esl: e.matmul(
                                   self.pb[5][0:1, 0:512], self.onescol[:, 0:1], esl,
                                   start=(kt == 0), stop=(kt == 15))),
                               nm + "dmm", 1, waits=P.after(nm + "expd", nm + "dcp"))
                        P.prod("tensor", lambda e: e.nop(), nm + "lag!", 1,
                               waits=P.after(nm + "dmm"))
                    P.prod("vector",
                           (lambda e, qt=qt: e.tensor_copy(Drow[:, qt * 512:(qt + 1) * 512],
                                                           self.pb[5][0:1, 0:512])),
                           nm + "dcp", 1, waits=P.after(nm + "dmm"))
                P.prod("vector",
                       (lambda e: e.reciprocal(Drow[:], Drow[:])),
                       nm + "k1", 1, waits=P.after(nm + "dcp"))
                P.prod("scalar",
                       (lambda e: e.activation(diagrow[:], diagrow[:], AF.Exp,
                                               bias=g11[:], scale=SC)),
                       nm + "k2", 1, waits=P.after(nm + "dgc", nm + "g2", nm + "kbc"))
                P.prod("vector",
                       (lambda e: e.tensor_mul(diagrow[:], diagrow[:], Drow[:])),
                       nm + "k3", 1, waits=P.after(nm + "k1", nm + "k2"))
                P.prod("vector",
                       (lambda e: e.reduce_sum(m11[:], diagrow[:], axis=AX.X)),
                       nm + "k4", 1, waits=P.after(nm + "k3"))
                P.prod("vector",
                       (lambda e: e.tensor_scalar(m11[:], m11[:], 1.0 / T, None, ALU.mult)),
                       nm + "k5", 1, waits=P.after(nm + "k4"))
                P.prod("vector",
                       (lambda e: e.tensor_scalar(diagrow[:], diagrow[:], m11[:], None, ALU.subtract)),
                       nm + "k6", 1, waits=P.after(nm + "k5"))
                P.prod("scalar",
                       (lambda e: e.activation(keepf[:], diagrow[:], AF.Square)),
                       nm + "k7", 1, waits=P.after(nm + "k6", nm + "kbc", nm + "od3"))
                P.prod("vector",
                       (lambda e: e.reduce_sum(s11[:], keepf[:], axis=AX.X)),
                       nm + "k8", 1, waits=P.after(nm + "k7"))
                P.prod("scalar",
                       (lambda e: e.activation(s11[:], s11[:], AF.Sqrt, scale=1.0 / (T - 1))),
                       nm + "k9", 1, waits=P.after(nm + "k8"))
                P.prod("vector",
                       (lambda e: e.tensor_scalar(s11[:], s11[:], 1.5, None, ALU.mult)),
                       nm + "k10", 1, waits=P.after(nm + "k9"))
                P.prod("scalar",
                       (lambda e: e.activation(diagrow[:], diagrow[:], AF.Abs)),
                       nm + "k11", 1, waits=P.after(nm + "k6", nm + "k8"))
                P.prod("vector",
                       (lambda e: e.tensor_scalar(keepf[:], diagrow[:], s11[:], None, ALU.is_le)),
                       nm + "k12", 1, waits=P.after(nm + "k10", nm + "k11"))
                P.prod("vector",
                       (lambda e: e.tensor_copy(keepb[:], keepf[:])),
                       nm + "k13", 1, waits=P.after(nm + "k12", nm + "ktr"))
                for kt in range(16):
                    pv = self.pb[2 + kt % 2].ap().bitcast(BF16)
                    P.prod("tensor",
                           (lambda e, kt=kt, pv=pv: e.transpose(
                               pv[0:128, 64:65], keepb[0:1, kt * 128:(kt + 1) * 128],
                               self.ident[0:1, 0:1])),
                           nm + "ktr", 1, waits=P.after(nm + "k13", nm + "kbb") + self.initw())
                    P.prod("vector",
                           (lambda e, kt=kt, pv=pv: e.tensor_scalar(
                               kbias[:, kt:kt + 1], pv[0:128, 64:65], BIG, None, ALU.mult)),
                           nm + "kbb", 1, waits=P.after(nm + "ktr", nm + "exp2"))
                    P.prod("vector",
                           (lambda e, kt=kt: e.tensor_tensor(
                               kbias[:, kt:kt + 1], kbias[:, kt:kt + 1], nbg[:], ALU.add)),
                           nm + "kbc", 1, waits=P.after(nm + "kbb", nm + "g5"))
                for qt in range(NTT):
                    for kt in range(16):
                        bank = self.pb[kt % 2]
                        P.prod("tensor",
                               (lambda e, kt=kt, qt=qt, hb=hb, bank=bank: e.matmul(
                                   bank[0:128, 0:512],
                                   kf[hb:hb + 64, kt * 128:(kt + 1) * 128],
                                   qf[hb:hb + 64, qt * 512:(qt + 1) * 512],
                                   start=True, stop=True)),
                               nm + "sm2", 1,
                               waits=P.after(nm + "exp2", nm + "kbc") + P.after_lag(nm + "lagb!", 2)),
                        esl = et[:, (kt % 2) * 512:(kt % 2) * 512 + 512]
                        P.prod("scalar",
                               (lambda e, kt=kt, bank=bank, esl=esl: e.activation(
                                   esl, bank[0:128, 0:512], AF.Exp,
                                   bias=kbias[:, kt:kt + 1], scale=SC)),
                               nm + "exp2", 1, waits=P.after(nm + "sm2", nm + "pvm")),
                        P.prod("tensor",
                               (lambda e, kt=kt, h=h, esl=esl: e.matmul(
                                   self.pb[6][0:65, 0:512],
                                   vtm[:, h * 1040 + kt * 65: h * 1040 + (kt + 1) * 65], esl,
                                   start=(kt == 0), stop=(kt == 15))),
                               nm + "pvm", 1, waits=P.after(nm + "exp2", nm + "oev", nm + "orc"))
                        P.prod("tensor", lambda e: e.nop(), nm + "lagb!", 1,
                               waits=P.after(nm + "pvm"))
                    P.prod("vector",
                           (lambda e: e.tensor_scalar(r512[:], self.pb[6][64:65, 0:512],
                                                      1e-30, None, ALU.add)),
                           nm + "od", 1, waits=P.after(nm + "pvm"))
                    P.prod("vector",
                           (lambda e: e.reciprocal(r512[:], r512[:])),
                           nm + "od2", 1, waits=P.after(nm + "od"))
                    P.prod("vector",
                           (lambda e, qt=qt: e.tensor_mul(r512[:], r512[:],
                                                          keepf[:, qt * 512:(qt + 1) * 512])),
                           nm + "od3", 1, waits=P.after(nm + "od2"))
                    P.prod("tensor",
                           (lambda e: e.matmul(self.pb[7][0:64, 0:512],
                                               self.onesrow[0:1, 0:64], r512[:],
                                               start=True, stop=True)),
                           nm + "obc", 1, waits=P.after(nm + "od3", nm + "omu")),
                    P.prod("vector",
                           (lambda e: e.tensor_copy(self.scrb[0:64, 512:1024],
                                                    self.pb[6][0:64, 0:512])),
                           nm + "oev", 1, waits=P.after(nm + "od", nm + "ost"))
                    P.prod("vector",
                           (lambda e: e.tensor_mul(self.scrb[0:64, 0:512],
                                                   self.scrb[0:64, 512:1024],
                                                   self.pb[7][0:64, 0:512])),
                           nm + "omu", 1, waits=P.after(nm + "obc", nm + "oev"))
                    P.prod("sync",
                           (lambda e, hglob=hglob, qt=qt: e.dma_start(
                               out=self.attn_dram[hglob * 64:(hglob + 1) * 64,
                                                  qt * 512:(qt + 1) * 512],
                               in_=self.scrb[0:64, 0:512])),
                           nm + "ost", 16, waits=P.after(nm + "omu"))
        self.load_w(wcWo, self.wpool2, 0, 1024, 0, 1024, nm + "wo",
                    wprep + P.after(nm + "pmm"))
        atile = self.scrb[:, 1024 + 0:1024 + 512]
        for ts in range(NTT):
            for ci in range(NCHUNK):
                P.prod("sync",
                       (lambda e, ci=ci, ts=ts: e.dma_start(
                           out=self.scrb[:, 1024 + ci * 512: 1024 + (ci + 1) * 512],
                           in_=self.attn_dram[ci * 128:(ci + 1) * 128, ts * 512:(ts + 1) * 512])),
                       nm + "ald", 16, waits=P.after(nm + "ost", nm + "omm"))
            for oc in range(NCHUNK):
                bank = self.pb[oc % 4]
                P.prod("tensor",
                       (lambda e, oc=oc, ts=ts, bank=bank: [
                           e.matmul(bank[0:128, 0:512],
                                    self.wpool2[:, ci * 1024 + oc * 128: ci * 1024 + (oc + 1) * 128],
                                    self.scrb[:, 1024 + ci * 512: 1024 + (ci + 1) * 512],
                                    start=(ci == 0), stop=(ci == NCHUNK - 1))
                           for ci in range(NCHUNK)][-1]),
                       nm + "omm", 1,
                       waits=P.after(nm + "ald", nm + "wo") + P.after_lag(nm + "oadd!", 3)),
                P.prod("vector",
                       (lambda e, oc=oc, ts=ts, bank=bank: e.scalar_tensor_tensor(
                           self.xs[:, oc * T + ts * 512: oc * T + (ts + 1) * 512],
                           bank[0:128, 0:512], self.l1a_cols[:, oc:oc + 1],
                           self.xs[:, oc * T + ts * 512: oc * T + (ts + 1) * 512],
                           ALU.add, ALU.add)),
                       nm + "oadd!", 1, waits=P.after(nm + "omm"))
        return self.emit_postln(P.after(nm + "oadd!"))



    def emit_l0a(self, b):
        nc, P = self.nc, self.P
        Wk = self.win_decl("L0_attn_Wk", [1024, 1024])
        bk = self.win_decl("L0_attn_bk", [1024])
        Wq = self.win_decl("L0_attn_Wq", [1024, 1024])
        bq = self.win_decl("L0_attn_bq", [1024])
        Wv = self.win_decl("L0_attn_Wv", [1024, 1024])
        bv = self.win_decl("L0_attn_bv", [1024])
        Wbeta = self.win_decl("L0_attn_Wbeta", [16, 1024])
        bbeta = self.win_decl("L0_attn_bbeta", [16])
        convk = self.win_decl("L0_attn_conv_k", [1024, 4])
        convq = self.win_decl("L0_attn_conv_q", [1024, 4])
        convv = self.win_decl("L0_attn_conv_v", [1024, 4])
        rmsw = self.win_decl("L0_attn_rms_w", [64])
        Wout = self.win_decl("L0_attn_Wout", [1024, 1024])
        bout = self.win_decl("L0_attn_bout", [1024])
        key = "l0a"
        NCH = 32
        if key not in self.once:
            self.once.add(key)
            P.extra_waits = P.after("postln") + self.xbok()
            self.wc_l0a = {n: self.dram("wcA" + n, [1024, 1024])
                           for n in ("k", "q", "v", "o")}
            tmps = {}
            for n, w in (("k", Wk), ("q", Wq), ("v", Wv), ("o", Wout)):
                tmps[n] = self.prep_transpose_cast(w, 1024, 1024, key + "p")
            for n, w in (("k", Wk), ("q", Wq), ("v", Wv), ("o", Wout)):
                self.prep_transpose(w, self.wc_l0a[n], 1024, 1024, key + "p",
                                    tmp=tmps[n])
            self.l0a_wb = nc.alloc_sbuf_tensor("l0a_wb", [128, 128], BF16)
            wbst = self.scrb.ap().bitcast(F32)[0:16, 2048:3072]
            wbb = self.scrb[0:16, 6144:7168]
            P.prod("sync",
                   (lambda e: e.dma_start(out=wbst, in_=Wbeta)),
                   key + "wbl", 16, waits=P.after(key + "pfs", key + "pfv"))
            P.prod("vector",
                   (lambda e: e.tensor_copy(wbb[:], wbst)),
                   key + "wbc", 1, waits=P.after(key + "wbl"))
            for c8 in range(8):
                P.prod("tensor",
                       (lambda e, c8=c8: e.transpose(
                           self.pb[3].ap().bitcast(BF16)[0:128, 0:16],
                           wbb[0:16, c8 * 128:(c8 + 1) * 128],
                           self.ident[0:16, 0:16])),
                       key + "wbt", 1,
                       waits=P.after(key + "wbc", key + "wbe") + self.initw())
                P.prod("vector",
                       (lambda e, c8=c8: e.tensor_copy(
                           self.l0a_wb[:, c8 * 16:(c8 + 1) * 16],
                           self.pb[3].ap().bitcast(BF16)[0:128, 0:16])),
                       key + "wbe", 1, waits=P.after(key + "wbt"))
            P.prod("vector", lambda e: e.nop(), key + "wb", 1,
                   waits=P.after(key + "wbe"))
            self.l0a_rw = nc.alloc_sbuf_tensor("l0a_rw", [128, 1], F32)
            for hp in range(2):
                P.prod("sync",
                       (lambda e, hp=hp: e.dma_start(
                           out=self.l0a_rw[hp * 64:(hp + 1) * 64, :],
                           in_=rmsw.rearrange("(d o) -> d o", o=1))),
                       key + "rw", 16)
            fstg = self.scrb[:, 0:2048]
            for c in range(8):
                P.prod("sync",
                       (lambda e, c=c: e.dma_start(
                           out=fstg[:, 0:1024],
                           in_=self.wc_l0a["o"][c * 128:(c + 1) * 128, :])),
                       key + "pfl", 16,
                       waits=P.after(key + "p", key + "pfs"))
                P.prod("vector",
                       (lambda e: e.tensor_scalar(fstg[:, 1024:2048], fstg[:, 0:1024],
                                                  self.l0a_rw[:], None, ALU.mult)),
                       key + "pfv", 1, waits=P.after(key + "pfl", key + "rw",
                                                     key + "pfs"))
                P.prod("sync",
                       (lambda e, c=c: e.dma_start(
                           out=self.wc_l0a["o"][c * 128:(c + 1) * 128, :],
                           in_=fstg[:, 1024:2048])),
                       key + "pfs", 16, waits=P.after(key + "pfv"))
            P.prod("sync", lambda e: e.nop(), key + "pf", 1, waits=P.after(key + "pfs"))
            self.l0a_cols = nc.alloc_sbuf_tensor("l0a_cols", [128, 136], F32)
            for j3, cv in enumerate((convk, convq, convv)):
                P.prod("sync",
                       (lambda e, j3=j3, cv=cv: e.dma_start(
                           out=self.l0a_cols[:, j3 * 32:(j3 + 1) * 32],
                           in_=cv.rearrange("(c p) j -> p c j", p=128))),
                       key + "b", 16)
            self.load_vec_cols(bk, 1024, key + "b", dst=self.l0a_cols, dst_off=96)
            self.load_vec_cols(bq, 1024, key + "b", dst=self.l0a_cols, dst_off=104)
            self.load_vec_cols(bv, 1024, key + "b", dst=self.l0a_cols, dst_off=112)
            self.load_vec_cols(bout, 1024, key + "b", dst=self.l0a_cols, dst_off=120)
            self.l0a_bb = nc.alloc_sbuf_tensor("l0a_bb", [16, 1], F32)
            P.prod("sync",
                   (lambda e: e.dma_start(out=self.l0a_bb[:],
                                          in_=bbeta.rearrange("(h o) -> h o", o=1))),
                   key + "b", 16)
            Cf = 64.0
            cm = np.zeros((NCH * 128, 256), np.float32)
            wi = np.zeros((128, NCH), np.float32)
            for i in range(NCH):
                tsv = float(i * 64)
                idx = np.arange(1, 65, dtype=np.float64) + tsv
                cs = np.cumsum(idx)
                weighted = idx + idx.sum() - cs
                w_i = weighted / ((1 + tsv + Cf) * (tsv + Cf) / 2)
                tri0 = (1 + tsv) * tsv / 2
                a_ = tri0 / (tri0 + cs)
                c_ = 1.0 - a_
                r = cs[:, None]
                jx = (cs - idx)[None, :]
                tm = np.where(r >= jx, (r - jx) / (tri0 + r), 0.0)
                tmt = tm.T
                blk = np.zeros((128, 256), np.float64)
                blk[0:64, 0:64] = tmt
                blk[64:128, 64:128] = tmt
                blk[:, 128:192] = np.tile(a_[None, :], (128, 1))
                blk[:, 192:256] = np.tile(c_[None, :], (128, 1))
                cm[i * 128:(i + 1) * 128, :] = blk
                wi[:, i] = np.tile(w_i, 2)
            self.l0a_cmask = nc.inline_tensor(cm.astype(BF), "c_l0a_cm")
            c_wi = nc.inline_tensor(np.ascontiguousarray(wi.astype(np.float32)),
                                    "c_l0a_wi")
            c_mska = np.zeros((128, 128), np.float32)
            su = -np.triu(np.ones((64, 64)), 1)
            c_mska[0:64, 0:64] = su
            c_mska[64:128, 64:128] = su
            c_mska_t = nc.inline_tensor(c_mska.astype(BF), "c_l0a_mska")
            self.l0a_cwi = c_wi
            self.l0a_cmska = c_mska_t
            c_ind = np.zeros((128, 2), np.float32)
            c_ind[0:64, 0] = 1.0
            c_ind[64:128, 1] = 1.0
            c_ind_t = nc.inline_tensor(c_ind.astype(BF), "c_l0a_ind")
            self.l0a_ind = nc.alloc_sbuf_tensor("l0a_ind", [128, 2], BF16)
            P.prod("sync", lambda e: e.dma_start(out=self.l0a_ind[:], in_=c_ind_t.ap()),
                   key + "c", 16)
            self.l0a_kd = self.dram("l0a_kd", [1024, T])
            self.l0a_qd = self.dram("l0a_qd", [1024, T])
            self.l0a_vd = self.dram("l0a_vd", [1024, T])
            self.l0a_ed = self.dram("l0a_ed", [16, T], F32)
            P.extra_waits = []
        nm = self.nid("l0a_")
        cols = self.l0a_cols
        wprep = P.after(key + "p", key + "b", key + "c", key + "pf", key + "wb")

        rawk = self.scrb[:, 0:2052]
        rawq = self.scrb[:, 2052:4104]
        rawv = self.scrb[:, 4104:6156]
        raws = (rawk, rawq, rawv)
        knb = self.scrb[:, 6156:6668]
        cf = self.scrf[:, 1024:1536]
        sqb = self.scrb[:, 6668:7180]
        e16 = self.scra.ap().bitcast(F32)[0:16, 0:512]
        krow = (self.scrf[0:1, 0:512], self.scrf[0:1, 512:1024])
        for ts in range(NTT):
            P.prod("tensor",
                   (lambda e, ts=ts: [
                       e.matmul(self.pb[3][0:16, 0:512],
                                self.l0a_wb[:, ci * 16:(ci + 1) * 16],
                                self.xb[:, ci * T + ts * 512:ci * T + (ts + 1) * 512],
                                start=(ci == 0), stop=(ci == NCHUNK - 1))
                       for ci in range(NCHUNK)][-1]),
                   nm + "abm", 1,
                   waits=wprep + P.after(nm + "abe", "postln") + self.xbok())
            P.prod("scalar",
                   (lambda e: e.activation(e16[:], self.pb[3][0:16, 0:512],
                                           AF.Sigmoid, bias=self.l0a_bb[:])),
                   nm + "abe", 1, waits=P.after(nm + "abm", nm + "aes") + wprep)
            P.prod("sync",
                   (lambda e, ts=ts: e.dma_start(
                       out=self.l0a_ed[:, ts * 512:(ts + 1) * 512], in_=e16[:])),
                   nm + "aes", 16, waits=P.after(nm + "abe", nm + "bec"))
        for c in range(8):
            wld = wprep + P.after(nm + "apm", "postln")
            for jn, wn in enumerate(("k", "q", "v")):
                for ci in range(NCHUNK):
                    P.prod("sync",
                           (lambda e, jn=jn, wn=wn, c=c, ci=ci: e.dma_start(
                               out=self.wpool[:, jn * 1024 + ci * 128:
                                              jn * 1024 + (ci + 1) * 128],
                               in_=self.wc_l0a[wn][ci * 128:(ci + 1) * 128,
                                                   c * 128:(c + 1) * 128])),
                           nm + "awl", 16, waits=wld)
            for j3 in range(3):
                P.prod("vector",
                       (lambda e, j3=j3: e.memset(raws[j3][:, 0:3], 0.0)),
                       nm + "arz", 1,
                       waits=P.after("postln", nm + "acv", key + "pf"))
            for ts in range(NTT):
                for j3 in range(3):
                    bank = self.pb[j3]
                    P.prod("tensor",
                           (lambda e, j3=j3, ts=ts, bank=bank: [
                               e.matmul(bank[0:128, 0:512],
                                        self.wpool[:, j3 * 1024 + ci * 128:
                                                   j3 * 1024 + (ci + 1) * 128],
                                        self.xb[:, ci * T + ts * 512:
                                                ci * T + (ts + 1) * 512],
                                        start=(ci == 0), stop=(ci == NCHUNK - 1))
                               for ci in range(NCHUNK)][-1]),
                           nm + "apm", 1,
                           waits=P.after(nm + "awl", nm + "arv", "postln")
                           + self.xbok())
                    P.prod("vector",
                           (lambda e, j3=j3, ts=ts, c=c, bank=bank: e.tensor_scalar(
                               raws[j3][:, 3 + ts * 512:3 + (ts + 1) * 512],
                               bank[0:128, 0:512],
                               cols[:, 96 + j3 * 8 + c:96 + j3 * 8 + c + 1],
                               None, ALU.add)),
                           nm + "arv", 1,
                           waits=P.after(nm + "apm", nm + "arz", nm + "acv"))
            for ts in range(NTT):
                for j3, dn in ((0, self.l0a_kd), (1, self.l0a_qd), (2, self.l0a_vd)):
                    P.prod("vector",
                           (lambda e, j3=j3, ts=ts, c=c: e.tensor_scalar(
                               cf[:], raws[j3][:, ts * 512:ts * 512 + 512],
                               cols[:, j3 * 32 + c * 4:j3 * 32 + c * 4 + 1],
                               None, ALU.mult)),
                           nm + "acv", 1,
                           waits=P.after(nm + "arv", nm + "akn", nm + "asq"))
                    for j in range(1, 4):
                        P.prod("vector",
                               (lambda e, j3=j3, ts=ts, j=j, c=c: e.scalar_tensor_tensor(
                                   cf[:], raws[j3][:, ts * 512 + j:ts * 512 + j + 512],
                                   cols[:, j3 * 32 + c * 4 + j:j3 * 32 + c * 4 + j + 1],
                                   cf[:], ALU.mult, ALU.add)),
                               nm + "acv", 1, waits=P.after(nm + "acv"))
                    if j3 < 2:
                        P.prod("scalar",
                               (lambda e: e.activation(sqb[:], cf[:], AF.Square)),
                               nm + "asq", 1, waits=P.after(nm + "acv", nm + "anm")),
                        for h in range(2):
                            P.prod("tensor",
                                   (lambda e, h=h: e.matmul(
                                       self.pb[4][0:1, 0:512],
                                       self.l0a_ind[:, h:h + 1], sqb[:],
                                       start=True, stop=True)),
                                   nm + "anm", 1,
                                   waits=P.after(nm + "asq", nm + "anr", key + "c")
                                   + self.initw())
                            P.prod("scalar",
                                   (lambda e, h=h: e.activation(
                                       krow[h][:], self.pb[4][0:1, 0:512], AF.Sqrt)),
                                   nm + "anr", 1,
                                   waits=P.after(nm + "anm", nm + "abc2")),
                            P.prod("vector",
                                   (lambda e, h=h: e.tensor_scalar(
                                       krow[h][:], krow[h][:], 1e-6, None, ALU.add)),
                                   nm + "anr2", 1, waits=P.after(nm + "anr"))
                            P.prod("vector",
                                   (lambda e, h=h: e.reciprocal(krow[h][:], krow[h][:])),
                                   nm + "anr2", 1, waits=P.after(nm + "anr2"))
                            P.prod("tensor",
                                   (lambda e, h=h: e.matmul(
                                       self.pb[5][h * 64:(h + 1) * 64, 0:512],
                                       self.onesrow[0:1, 0:64], krow[h][:],
                                       start=True, stop=True)),
                                   nm + "abc2", 1,
                                   waits=P.after(nm + "anr2", nm + "akn") + self.initw())
                        for h in range(2):
                            P.prod("vector",
                                   (lambda e, h=h: e.tensor_mul(
                                       knb[h * 64:(h + 1) * 64, :],
                                       cf[h * 64:(h + 1) * 64, :],
                                       self.pb[5][h * 64:(h + 1) * 64, 0:512])),
                                   nm + "akn", 1,
                                   waits=P.after(nm + "abc2", nm + "ast"))
                    else:
                        P.prod("scalar",
                               (lambda e: e.activation(sqb[:], cf[:], AF.Sigmoid)),
                               nm + "asq", 1, waits=P.after(nm + "acv", nm + "anm")),
                        P.prod("vector",
                               (lambda e: e.tensor_mul(knb[:], cf[:], sqb[:])),
                               nm + "akn", 1, waits=P.after(nm + "asq", nm + "ast"))
                    P.prod("sync",
                           (lambda e, dn=dn, c=c, ts=ts: e.dma_start(
                               out=dn[c * 128:(c + 1) * 128, ts * 512:(ts + 1) * 512],
                               in_=knb[:])),
                           nm + "ast", 16, waits=P.after(nm + "akn", nm + "bld"))
        karr = self.scrb[:, 0:512]
        qarr = self.scrb[:, 512:1024]
        varr = self.scrb[:, 1024:1536]
        Ab = self.scrb[:, 1536:2304]
        QaQc = self.scrb[:, 2304:2432]
        msk = self.scrb[:, 2432:2688]
        ktm = self.scrb[:, 2688:2752]
        vtm = self.scrb[:, 2752:2816]
        Ub = self.scrb[:, 2816:2880]
        Uwb = self.scrb[:, 2880:2944]
        On = self.scrb[:, 2944:3008]
        M3T = self.scrb[:, 3008:3136]
        sqt = self.scrb[:, 3136:3200]
        Ttmp = self.scrb[:, 3200:3328]
        w2f32 = self.wpool2.ap().bitcast(F32)
        Wt32 = w2f32[:, 0:512]
        Wa32 = w2f32[:, 512:1024]
        Wtb = self.wpool2[:, 2048:2560]
        Wab = self.wpool2[:, 2560:3072]
        ebs = self.scrf[:, 0:128]
        tmpf = self.scrf[:, 128:256]
        U32 = self.scrf[:, 256:320]
        mscol = self.scrf[:, 320:321]
        rccol = self.scrf[:, 321:322]
        ecol = self.scrf[:, 324:325]
        erow = self.scrf[0:1, 364:492]
        eps6 = self.eps_col(1e-6, 128)
        mskab = self.scrb[:, 3328:3456]
        wib = self.scrb.ap().bitcast(F32)[:, 1728:1760]
        wbcl = P.after(nm + "acv", nm + "arv", nm + "arz", nm + "bm", nm + "buw")
        P.prod("sync",
               (lambda e: e.dma_start(out=mskab[:], in_=self.l0a_cmska.ap())),
               nm + "bcl", 16, waits=wbcl)
        P.prod("sync",
               (lambda e: e.dma_start(out=wib[:], in_=self.l0a_cwi.ap())),
               nm + "bcl", 16, waits=wbcl)
        for t_, nmm in ((Wt32, "bi"), (Wa32, "bi"), (Wtb, "bi"), (Wab, "bi")):
            P.prod("vector", (lambda e, t_=t_: e.memset(t_[:], 0.0)), nm + "bi", 1,
                   waits=P.after("postln", nm + "bwq", nm + "bom", nm + "bwm"))
        for i in range(NCH):
            tsv = float(i * 64)
            coef1 = tsv / (tsv + 64.0) * ((1 + tsv) / (1 + tsv + 64.0))
            coef2 = 64.0 / (tsv + 64.0) * ((tsv * 2 + 1 + 64.0) / (1 + tsv + 64.0))
            wld = P.after(nm + "ast") + P.after(
                nm + "bgm", nm + "bwm", nm + "bkq", nm + "btv", nm + "bqa",
                nm + "bm", nm + "buw")
            for src, dst in ((self.l0a_kd, karr), (self.l0a_qd, qarr),
                             (self.l0a_vd, varr)):
                P.prod("sync",
                       (lambda e, src=src, dst=dst, i=i: e.dma_start(
                           out=dst[:],
                           in_=src[:, i * 64:(i + 1) * 64].rearrange(
                               "(f p) c -> p f c", p=128))),
                       nm + "bld", 16, waits=wld)
            P.prod("sync",
                   (lambda e, i=i: e.dma_start(
                       out=msk[:], in_=self.l0a_cmask.ap()[i * 128:(i + 1) * 128, :])),
                   nm + "bld", 16, waits=wld)
            wbl = P.after(nm + "bld")
            for p in range(8):
                for hh in range(2):
                    P.prod("sync",
                           (lambda e, i=i, p=p, hh=hh: e.dma_start(
                               out=erow[0:1, hh * 64:(hh + 1) * 64],
                               in_=self.l0a_ed[2 * p + hh:2 * p + hh + 1,
                                               i * 64:(i + 1) * 64])),
                           nm + "bec", 16,
                           waits=P.after(nm + "aes", nm + "bbc", nm + "brh"))
                    P.prod("sync",
                           (lambda e, i=i, p=p, hh=hh: e.dma_start(
                               out=ecol[hh * 64:(hh + 1) * 64, :],
                               in_=self.l0a_ed[2 * p + hh,
                                               i * 64:(i + 1) * 64].rearrange(
                                   "(t o) -> t o", o=1))),
                           nm + "bec", 16,
                           waits=P.after(nm + "aes", nm + "bbc", nm + "brh"))
                for h in range(2):
                    P.prod("tensor",
                           (lambda e, h=h, p=p: e.transpose(
                               self.pb[2].ap().bitcast(BF16)[0:64, h * 64:h * 64 + 64],
                               karr[h * 64:h * 64 + 64, p * 64:p * 64 + 64],
                               self.ident[h * 64:h * 64 + 64, h * 64:h * 64 + 64])),
                           nm + "btv", 1,
                           waits=wbl + self.initw() + P.after(nm + "bte"))
                    P.prod("tensor",
                           (lambda e, h=h, p=p: e.transpose(
                               self.pb[6].ap().bitcast(BF16)[0:64, h * 64:h * 64 + 64],
                               varr[h * 64:h * 64 + 64, p * 64:p * 64 + 64],
                               self.ident[h * 64:h * 64 + 64, h * 64:h * 64 + 64])),
                           nm + "btv", 1,
                           waits=wbl + self.initw() + P.after(nm + "bte"))
                for h in range(2):
                    P.prod("vector",
                           (lambda e, h=h: e.tensor_copy(
                               ktm[h * 64:(h + 1) * 64, :],
                               self.pb[2].ap().bitcast(BF16)[0:64, h * 64:h * 64 + 64])),
                           nm + "bte", 1,
                           waits=P.after(nm + "btv", nm + "bku", nm + "brh"))
                    P.prod("vector",
                           (lambda e, h=h: e.tensor_copy(
                               vtm[h * 64:(h + 1) * 64, :],
                               self.pb[6].ap().bitcast(BF16)[0:64, h * 64:h * 64 + 64])),
                           nm + "bte", 1,
                           waits=P.after(nm + "btv", nm + "brh"))
                for h in range(2):
                    P.prod("tensor",
                           (lambda e, h=h, p=p: e.matmul(
                               self.pb[0][h * 64:(h + 1) * 64, h * 64:h * 64 + 64],
                               karr[h * 64:h * 64 + 64, p * 64:p * 64 + 64],
                               karr[h * 64:h * 64 + 64, p * 64:p * 64 + 64],
                               start=True, stop=True)),
                           nm + "bgm", 1, waits=wbl + P.after(nm + "bm"))
                P.prod("tensor",
                       (lambda e, p=p: e.matmul(
                           self.pb[1][0:128, 0:128], self.onesrow[0:1, 0:128],
                           erow[0:1, 0:128], start=True, stop=True)),
                       nm + "bbc", 1,
                       waits=P.after(nm + "bec", nm + "bev") + self.initw())
                P.prod("vector",
                       (lambda e: e.tensor_copy(ebs[:], self.pb[1][0:128, 0:128])),
                       nm + "bev", 1, waits=P.after(nm + "bbc", nm + "bm"))
                P.prod("vector",
                       (lambda e: e.tensor_tensor(tmpf[:], self.pb[0][0:128, 0:128],
                                                  mskab[:], ALU.mult)),
                       nm + "bm", 1,
                       waits=P.after(nm + "bgm", nm + "bcl", nm + "bev"))
                P.prod("vector",
                       (lambda e: e.tensor_tensor(Ab[:, 0:128], tmpf[:], ebs[:],
                                                  ALU.mult)),
                       nm + "bm", 1, waits=P.after(nm + "bm", nm + "bsq", nm + "bap"))
                P.prod("tensor",
                       (lambda e: e.transpose(self.pb[2].ap().bitcast(BF16)[0:128, 0:128],
                                              Ab[:, 0:128], self.ident[:])),
                       nm + "btr", 1,
                       waits=P.after(nm + "bm", nm + "bte2", nm + "bte") + self.initw())
                P.prod("vector",
                       (lambda e: e.tensor_copy(Ttmp[:],
                                                self.pb[2].ap().bitcast(BF16)[0:128, 0:128])),
                       nm + "bte2", 1, waits=P.after(nm + "btr", nm + "bsq"))
                for k in range(1, 6):
                    P.prod("tensor",
                           (lambda e, k=k: e.matmul(self.pb[3][0:128, 0:128],
                                                    Ttmp[:], Ab[:, (k - 1) * 128:k * 128],
                                                    start=True, stop=True)),
                           nm + "bsq", 1,
                           waits=P.after(nm + "bte2", nm + "bse", nm + "bm"))
                    P.prod("vector",
                           (lambda e, k=k: e.tensor_copy(
                               Ab[:, k * 128:(k + 1) * 128], self.pb[3][0:128, 0:128])),
                           nm + "bse", 1, waits=P.after(nm + "bsq", nm + "bap"))
                    if k < 5:
                        P.prod("tensor",
                               (lambda e, k=k: e.transpose(
                                   self.pb[2].ap().bitcast(BF16)[0:128, 0:128],
                                   Ab[:, k * 128:(k + 1) * 128], self.ident[:])),
                               nm + "btr", 1,
                               waits=P.after(nm + "bse", nm + "bte2"))
                        P.prod("vector",
                               (lambda e: e.tensor_copy(
                                   Ttmp[:], self.pb[2].ap().bitcast(BF16)[0:128, 0:128])),
                               nm + "bte2", 1, waits=P.after(nm + "btr", nm + "bsq"))
                for h in range(2):
                    P.prod("tensor",
                           (lambda e, h=h, p=p: e.matmul(
                               self.pb[4][h * 64:(h + 1) * 64, 0:64],
                               karr[h * 64:h * 64 + 64, p * 64:p * 64 + 64],
                               Wtb[h * 64:h * 64 + 64, p * 64:p * 64 + 64],
                               start=True, stop=True)),
                           nm + "bwm", 1,
                           waits=wbl + P.after(nm + "brh", nm + "bi", nm + "bwq"))
                P.prod("vector",
                       (lambda e: e.tensor_tensor(tmpf[:, 0:64], vtm[:],
                                                  self.pb[4][0:128, 0:64],
                                                  ALU.subtract)),
                       nm + "brh", 1, waits=P.after(nm + "bwm", nm + "bte"))
                P.prod("vector",
                       (lambda e: e.tensor_scalar(U32[:], tmpf[:, 0:64],
                                                   ecol[:], None, ALU.mult)),
                       nm + "brh", 1, waits=P.after(nm + "brh"))
                P.prod("vector",
                       (lambda e: e.tensor_copy(Ub[:], U32[:])),
                       nm + "bub", 1, waits=P.after(nm + "brh", nm + "bom", nm + "bku"))
                for k in range(5, -1, -1):
                    P.prod("tensor",
                           (lambda e, k=k: e.matmul(self.pb[4][0:128, 0:64],
                                                    Ab[:, k * 128:(k + 1) * 128], Ub[:],
                                                    start=True, stop=True)),
                           nm + "bap", 1,
                           waits=P.after(nm + "bub", nm + "bau", nm + "bse",
                                         nm + "brh"))
                    P.prod("vector",
                           (lambda e: e.tensor_tensor(U32[:], U32[:],
                                                      self.pb[4][0:128, 0:64], ALU.add)),
                           nm + "bau", 1, waits=P.after(nm + "bap"))
                    P.prod("vector",
                           (lambda e: e.tensor_copy(Ub[:], U32[:])),
                           nm + "bub", 1, waits=P.after(nm + "bau"))
                P.prod("vector",
                       (lambda e, p=p: e.tensor_tensor(QaQc[:, 0:64],
                                                       qarr[:, p * 64:p * 64 + 64],
                                                       msk[:, 128:192], ALU.mult)),
                       nm + "bqa", 1, waits=wbl + P.after(nm + "bom"))
                P.prod("vector",
                       (lambda e, p=p: e.tensor_tensor(QaQc[:, 64:128],
                                                       qarr[:, p * 64:p * 64 + 64],
                                                       msk[:, 192:256], ALU.mult)),
                       nm + "bqa", 1, waits=P.after(nm + "bqa"))
                for h in range(2):
                    P.prod("tensor",
                           (lambda e, h=h, p=p: e.matmul(
                               self.pb[0][h * 64:(h + 1) * 64, h * 64:h * 64 + 64],
                               karr[h * 64:h * 64 + 64, p * 64:p * 64 + 64],
                               qarr[h * 64:h * 64 + 64, p * 64:p * 64 + 64],
                               start=True, stop=True)),
                           nm + "bkq", 1, waits=wbl + P.after(nm + "bm"))
                P.prod("vector",
                       (lambda e: e.tensor_tensor(M3T[:], self.pb[0][0:128, 0:128],
                                                  msk[:, 0:128], ALU.mult)),
                       nm + "bm3", 1, waits=P.after(nm + "bkq", nm + "bom"))
                for h in range(2):
                    P.prod("tensor",
                           (lambda e, h=h, p=p: e.matmul(
                               self.pb[5][h * 64:(h + 1) * 64, 0:64],
                               QaQc[h * 64:h * 64 + 64, 0:64],
                               Wab[h * 64:h * 64 + 64, p * 64:p * 64 + 64],
                               start=True, stop=False)),
                           nm + "bom", 1,
                           waits=P.after(nm + "bqa", nm + "bm3", nm + "bub",
                                         nm + "bo", nm + "bi", nm + "bwq"))
                    P.prod("tensor",
                           (lambda e, h=h, p=p: e.matmul(
                               self.pb[5][h * 64:(h + 1) * 64, 0:64],
                               QaQc[h * 64:h * 64 + 64, 64:128],
                               Wtb[h * 64:h * 64 + 64, p * 64:p * 64 + 64],
                               start=False, stop=False)),
                           nm + "bom", 1, waits=P.after(nm + "bom"))
                    P.prod("tensor",
                           (lambda e, h=h: e.matmul(
                               self.pb[5][h * 64:(h + 1) * 64, 0:64],
                               M3T[h * 64:h * 64 + 64, h * 64:h * 64 + 64],
                               Ub[h * 64:h * 64 + 64, :],
                               start=False, stop=True)),
                           nm + "bom", 1, waits=P.after(nm + "bom"))
                P.prod("scalar",
                       (lambda e: e.activation(sqt[:], self.pb[5][0:128, 0:64],
                                               AF.Square, accum_out=mscol[:])),
                       nm + "bos", 1, waits=P.after(nm + "bom", nm + "bo")),
                P.prod("vector",
                       (lambda e: e.tensor_scalar(rccol[:], mscol[:], 1.0 / 64, None,
                                                  ALU.mult)),
                       nm + "bo", 1, waits=P.after(nm + "bos"))
                P.prod("scalar",
                       (lambda e, eps6=eps6: e.activation(rccol[:], rccol[:], AF.Sqrt,
                                                          bias=eps6)),
                       nm + "bo", 1, waits=P.after(nm + "bo") + self.initw())
                P.prod("vector",
                       (lambda e: e.reciprocal(rccol[:], rccol[:])),
                       nm + "bo", 1, waits=P.after(nm + "bo"))
                P.prod("vector",
                       (lambda e: e.tensor_scalar(On[:], self.pb[5][0:128, 0:64],
                                                  rccol[:], None, ALU.mult)),
                       nm + "bo", 1, waits=P.after(nm + "bo"))
                for h in range(2):
                    P.prod("tensor",
                           (lambda e, h=h: e.transpose(
                               self.pb[2].ap().bitcast(BF16)[0:64, h * 64:h * 64 + 64],
                               On[h * 64:h * 64 + 64, :],
                               self.ident[h * 64:h * 64 + 64, h * 64:h * 64 + 64])),
                           nm + "bot", 1,
                           waits=P.after(nm + "bo", nm + "boe", nm + "bte2"))
                for h in range(2):
                    P.prod("vector",
                           (lambda e, h=h, p=p, i=i: e.tensor_copy(
                               self.scra[h * 64:(h + 1) * 64,
                                         p * 2048 + i * 64:p * 2048 + (i + 1) * 64],
                               self.pb[2].ap().bitcast(BF16)[0:64, h * 64:h * 64 + 64])),
                           nm + "boe", 1, waits=P.after(nm + "bot"))
                P.prod("vector",
                       (lambda e, i=i: e.tensor_scalar(Uwb[:], U32[:],
                                                       wib[:, i:i + 1], None,
                                                       ALU.mult)),
                       nm + "buw", 1,
                       waits=P.after(nm + "bau", nm + "bkw") + P.after(nm + "bcl"))
                for h in range(2):
                    P.prod("tensor",
                           (lambda e, h=h: e.matmul(
                               self.pb[4][h * 64:(h + 1) * 64, 0:64],
                               ktm[h * 64:h * 64 + 64, :], Ub[h * 64:h * 64 + 64, :],
                               start=True, stop=True)),
                           nm + "bku", 1,
                           waits=P.after(nm + "bub", nm + "bau", nm + "bte",
                                         nm + "bwq", nm + "bom"))
                    P.prod("tensor",
                           (lambda e, h=h: e.matmul(
                               self.pb[7][h * 64:(h + 1) * 64, 0:64],
                               ktm[h * 64:h * 64 + 64, :], Uwb[h * 64:h * 64 + 64, :],
                               start=True, stop=True)),
                           nm + "bkw", 1,
                           waits=P.after(nm + "buw", nm + "bte", nm + "bwq"))
                P.prod("vector",
                       (lambda e, p=p, coef1=coef1: e.tensor_scalar(
                           Wa32[:, p * 64:(p + 1) * 64], Wa32[:, p * 64:(p + 1) * 64],
                           coef1, None, ALU.mult)),
                       nm + "bwq", 1,
                       waits=P.after(nm + "bom", nm + "bku", nm + "bkw", nm + "bi"))
                P.prod("vector",
                       (lambda e, p=p, coef2=coef2: e.scalar_tensor_tensor(
                           Wa32[:, p * 64:(p + 1) * 64], Wt32[:, p * 64:(p + 1) * 64],
                           coef2, Wa32[:, p * 64:(p + 1) * 64], ALU.mult, ALU.add)),
                       nm + "bwq", 1, waits=P.after(nm + "bwq"))
                P.prod("vector",
                       (lambda e, p=p: e.tensor_tensor(
                           Wa32[:, p * 64:(p + 1) * 64], Wa32[:, p * 64:(p + 1) * 64],
                           self.pb[7][0:128, 0:64], ALU.add)),
                       nm + "bwq", 1, waits=P.after(nm + "bwq"))
                P.prod("vector",
                       (lambda e, p=p: e.tensor_copy(Wab[:, p * 64:(p + 1) * 64],
                                                     Wa32[:, p * 64:(p + 1) * 64])),
                       nm + "bwq", 1, waits=P.after(nm + "bwq"))
                P.prod("vector",
                       (lambda e, p=p: e.tensor_tensor(
                           Wt32[:, p * 64:(p + 1) * 64], Wt32[:, p * 64:(p + 1) * 64],
                           self.pb[4][0:128, 0:64], ALU.add)),
                       nm + "bwq", 1, waits=P.after(nm + "bwq"))
                P.prod("vector",
                       (lambda e, p=p: e.tensor_copy(Wtb[:, p * 64:(p + 1) * 64],
                                                     Wt32[:, p * 64:(p + 1) * 64])),
                       nm + "bwq", 1, waits=P.after(nm + "bwq"))
        t3 = self.scrf[:, 0:512]
        for gc in range(8):
            P.prod("sync",
                   (lambda e, gc=gc: e.dma_start(
                       out=self.wpool[:, gc * 1024:(gc + 1) * 1024],
                       in_=self.wc_l0a["o"][gc * 128:(gc + 1) * 128, :])),
                   nm + "wcl", 16,
                   waits=wprep + P.after(nm + "apm", nm + "abm", "postln"))
        for ts in range(NTT):
            for oc in range(NCHUNK):
                bank = self.pb[oc % 4]
                P.prod("tensor",
                       (lambda e, ts=ts, oc=oc, bank=bank: [
                           e.matmul(bank[0:128, 0:512],
                                    self.wpool[:, gc * 1024 + oc * 128:
                                               gc * 1024 + (oc + 1) * 128],
                                    self.scra[:, gc * 2048 + ts * 512:
                                              gc * 2048 + (ts + 1) * 512],
                                    start=(gc == 0), stop=(gc == NCHUNK - 1))
                           for gc in range(NCHUNK)][-1]),
                       nm + "com", 1,
                       waits=P.after(nm + "wcl", nm + "boe", nm + "coe"))
                P.prod("vector",
                       (lambda e, oc=oc, bank=bank: e.tensor_scalar(
                           t3[:], bank[0:128, 0:512],
                           cols[:, 120 + oc:120 + oc + 1], None, ALU.add)),
                       nm + "coe", 1, waits=P.after(nm + "com", nm + "cox"))
                P.prod("vector",
                       (lambda e, ts=ts, oc=oc: e.tensor_tensor(
                           self.xs[:, oc * T + ts * 512:oc * T + (ts + 1) * 512],
                           self.xs[:, oc * T + ts * 512:oc * T + (ts + 1) * 512],
                           t3[:], ALU.add)),
                       nm + "cox", 1, waits=P.after(nm + "coe"))
        return self.emit_postln(P.after(nm + "cox"))

    def emit_l2a(self, b):
        nc, P = self.nc, self.P
        Wq = self.win_decl("L2_attn_Wq", [1024, 1024])
        Wqr = self.win_decl("L2_attn_Wqr", [1024, 512])
        Wkr = self.win_decl("L2_attn_Wkr", [1024, 32])
        Wkvd = self.win_decl("L2_attn_Wkvd", [1024, 256])
        Wku = self.win_decl("L2_attn_Wku", [256, 1024])
        Wvu = self.win_decl("L2_attn_Wvu", [256, 1024])
        Wo = self.win_decl("L2_attn_Wo", [1024, 1024])
        lobo = self.win_decl("L2_attn_lobo_log", [16])
        key = "l2a"
        if key not in self.once:
            self.once.add(key)
            P.extra_waits = P.after("postln") + self.xbok()
            self.wc_l2a = {}
            for n, src in (("q", Wq), ("qr", Wqr), ("kr", Wkr), ("kvd", Wkvd),
                           ("ku", Wku), ("vu", Wvu), ("o", Wo)):
                self.wc_l2a[n] = self.dram("wc2" + n, [src.shape[0], src.shape[1]])
                self.prep_cast(src, self.wc_l2a[n], src.shape[0], src.shape[1], key + "p")
            self.l2a_lobo = self.nc.alloc_sbuf_tensor("l2a_lobo", [1, 16], F32)
            P.prod("sync", (lambda e: e.dma_start(out=self.l2a_lobo[:],
                                                  in_=lobo.rearrange("(a n) -> a n", a=1))),
                   key + "b", 16)
            half = 16
            inv = (1.0 / (10000.0 ** (np.arange(half, dtype=np.float64) / half)))
            ang = np.arange(T, dtype=np.float64)[:, None] * inv[None, :]
            self.c_cos = nc.inline_tensor(np.ascontiguousarray(np.cos(ang).T.astype(BF)), "c_cos")
            self.c_sin = nc.inline_tensor(np.ascontiguousarray(np.sin(ang).T.astype(BF)), "c_sin")
            addm = np.zeros((4, 128, 512), np.float32)
            mulm = np.zeros((4, 128, 512), np.float32)
            for oi, o in enumerate((0, 128, 256, 384)):
                qq = np.arange(128)[:, None]
                kk = np.arange(512)[None, :]
                addm[oi] = np.where(kk <= qq + o, 0.0, -30000.0)
            for di, d in enumerate((0, 128, 256, 384)):
                pp = np.arange(128)[:, None]
                ff = np.arange(512)[None, :]
                mulm[di] = (pp + d <= ff).astype(np.float32)
            self.c_addm = nc.inline_tensor(addm.astype(BF).reshape(4 * 128, 512), "c_addm")
            self.c_mulm = nc.inline_tensor(mulm.astype(BF).reshape(4 * 128, 512), "c_mulm")
            P.extra_waits = []
        if not hasattr(self, "attn_dram"):
            self.attn_dram = self.dram("attn_dram", [1024, T], BF16)
        wc = self.wc_l2a
        nm = self.nid("l2a_")
        wprep = P.after(key + "p", key + "b", "postln")

        latf = self.scra[:, 0:4096]
        QT = {0: self.scra[:, 4096:6144], 1: self.scra[:, 6144:8192]}
        KT = {0: self.scra[:, 8192:10240], 1: self.scra[:, 10240:12288]}
        vtm = self.scra[:, 12288:14336]
        krh1 = self.scra[0:16, 14336:16384]
        krh2 = self.scra[32:48, 14336:16384]
        vfm = self.scrb[:, 0:2048]
        l2a_addm = self.scrb[:, 2048:4096]
        l2a_mulm = self.scrb[:, 4096:6144]
        et = self.scrb[:, 6144:6656]
        epi1 = self.scrb[0:64, 6656:7168]
        epi2 = self.scrb[0:64, 7168:7680]
        mcst = self.scrb[:, 7680:7696]
        WQ, WQR, WKU, WVU = 0, 1024, 1536, 1792
        costab = self.wpool[0:16, 4096:6144]
        sintab = self.wpool[0:16, 6144:8192]
        pre1 = self.wpool[0:16, 8192:10240]
        pre2 = self.wpool[0:16, 10240:12288]
        rt1 = self.wpool[0:16, 12288:14336]
        rt2 = self.wpool[0:16, 14336:16384]
        wkr_off = 2048
        mcol = self.sb("l2a_mcol", [128, 16], F32)
        g11 = self.scrf[0:1, 512:513]
        gcol = self.scrf[:, 515:516]
        lg11 = self.scrf[0:1, 516:517]
        r512 = self.scrf[0:1, 517:1029]
        rmax = self.scrf[0:1, 1029:1030]
        rtmp = self.scrf[:, 1030:1031]
        tq = self.scrf[0:1, 513:514]
        tk = self.scrf[0:1, 514:515]
        SC2 = 1.0 / math.sqrt(96.0)
        import os
        L2STG = int(os.environ.get("L2A_STAGES", "9"))

        P.prod("sync", (lambda e: e.dma_start(out=costab, in_=self.c_cos.ap())), nm + "rt", 16,
               waits=wprep)
        P.prod("sync", (lambda e: e.dma_start(out=sintab, in_=self.c_sin.ap())), nm + "rt", 16,
               waits=wprep)
        for mi in range(4):
            P.prod("sync",
                   (lambda e, mi=mi: e.dma_start(
                       out=l2a_addm[:, mi * 512:(mi + 1) * 512],
                       in_=self.c_addm.ap()[mi * 128:(mi + 1) * 128, :])),
                   nm + "mld", 16, waits=wprep + P.after(nm + "qma"))
            P.prod("sync",
                   (lambda e, mi=mi: e.dma_start(
                       out=l2a_mulm[:, mi * 512:(mi + 1) * 512],
                       in_=self.c_mulm.ap()[mi * 128:(mi + 1) * 128, :])),
                   nm + "mld", 16, waits=wprep + P.after(nm + "emk"))

        def rope16(p1, p2, o1, o2, chin, chname):
            P.prod("vector", (lambda e: e.tensor_mul(rt1[:], p1[:], costab[:])),
                   nm + "ro1", 1, waits=P.after(chin, nm + "rt", nm + "ro5", nm + "rodma"))
            P.prod("vector", (lambda e: e.tensor_mul(rt2[:], p2[:], sintab[:])),
                   nm + "ro2", 1, waits=P.after(chin, nm + "rt", nm + "ro5", nm + "rodma"))
            P.prod("vector", (lambda e, o1=o1: e.tensor_sub(o1, rt1[:], rt2[:])),
                   nm + "ro3", 1, waits=P.after(nm + "ro1", nm + "ro2"))
            P.prod("vector", (lambda e: e.tensor_mul(rt1[:], p2[:], costab[:])),
                   nm + "ro4", 1, waits=P.after(nm + "ro3"))
            P.prod("vector", (lambda e: e.tensor_mul(rt2[:], p1[:], sintab[:])),
                   nm + "ro4", 1, waits=P.after(nm + "ro3"))
            P.prod("vector", (lambda e: e.tensor_add(rt1[:], rt1[:], rt2[:])),
                   nm + "ro5", 1, waits=P.after(nm + "ro4"))
            P.prod("gpsimd", (lambda e, o2=o2: e.dma_start(out=o2, in_=rt1[:])),
                   nm + "rodma", 16, waits=P.after(nm + "ro5"))

        self.load_w(wc["kvd"], self.wpool2, 0, 1024, 0, 256, nm + "wla", wprep)
        for ts in range(NTT):
            for oc in range(2):
                bank = self.pb[oc]
                P.prod("tensor",
                       (lambda e, ts=ts, oc=oc, bank=bank: [
                           e.matmul(bank[0:128, 0:512],
                                    self.wpool2[:, ci * 256 + oc * 128: ci * 256 + (oc + 1) * 128],
                                    self.xb[:, ci * T + ts * 512:ci * T + (ts + 1) * 512],
                                    start=(ci == 0), stop=(ci == NCHUNK - 1))
                           for ci in range(NCHUNK)][-1]),
                       nm + "lmm", 1,
                       waits=P.after(nm + "wla", nm + "lev", "postln") + self.xbok())
                P.prod("vector",
                       (lambda e, ts=ts, oc=oc, bank=bank: e.tensor_copy(
                           latf[:, oc * T + ts * 512: oc * T + (ts + 1) * 512], bank[0:128, 0:512])),
                       nm + "lev", 1, waits=P.after(nm + "lmm"))
        self.load_w(wc["kr"], self.wpool, wkr_off, 1024, 0, 32, nm + "wlb", wprep)
        for half, dst in ((0, pre1), (1, pre2)):
            for ts in range(NTT):
                bank = self.pb[2]
                P.prod("tensor",
                       (lambda e, half=half, ts=ts, bank=bank: [
                           e.matmul(bank[0:16, 0:512],
                                    self.wpool[:, wkr_off + ci * 32 + half * 16:
                                                wkr_off + ci * 32 + half * 16 + 16],
                                    self.xb[:, ci * T + ts * 512:ci * T + (ts + 1) * 512],
                                    start=(ci == 0), stop=(ci == NCHUNK - 1))
                           for ci in range(NCHUNK)][-1]),
                       nm + "krm", 1,
                       waits=P.after(nm + "wlb", nm + "krev", "postln") + self.xbok())
                P.prod("vector",
                       (lambda e, dst=dst, ts=ts, bank=bank: e.tensor_copy(
                           dst[:, ts * 512:(ts + 1) * 512], bank[0:16, 0:512])),
                       nm + "krev", 1, waits=P.after(nm + "krm"))
        rope16(pre1, pre2, krh1[:], self.scra[32:48, 14336:16384], nm + "krev", nm + "kr")
        P.prod("vector", lambda e: e.nop(), nm + "krok", 1,
               waits=P.after(nm + "ro3", nm + "rodma"))

        for hp in range(8):
            wv = wprep + P.after(nm + "pvm", nm + "exp2", nm + "qmm", nm + "pmm", nm + "qrm",
                                 nm + "mex", nm + "dnm", nm + "ost", nm + "gm0", nm + "sm2")
            self.load_w(wc["q"], self.wpool, WQ, 1024, hp * 128, (hp + 1) * 128, nm + "wlc", wv)
            self.load_w(wc["qr"], self.wpool, WQR, 1024, hp * 64, (hp + 1) * 64, nm + "wlc", wv)
            self.load_w(wc["ku"], self.wpool, WKU, 256, hp * 128, (hp + 1) * 128, nm + "wlc", wv)
            self.load_w(wc["vu"], self.wpool, WVU, 256, hp * 128, (hp + 1) * 128, nm + "wlc", wv)
            for h in range(2):
                P.prod("vector",
                       (lambda e, h=h: e.memset(QT[h][64:128, :], 0.0)),
                       nm + "zp", 1, waits=P.after(nm + "qmm", nm + "pvm", nm + "exp2", nm + "sm2"))
                P.prod("vector",
                       (lambda e, h=h: e.memset(KT[h][64:128, :], 0.0)),
                       nm + "zp", 1, waits=P.after(nm + "qmm", nm + "pvm", nm + "exp2", nm + "sm2"))
            for h in range(2):
                for ts in range(NTT):
                    bank = self.pb[ts % 2]
                    P.prod("tensor",
                           (lambda e, h=h, ts=ts, bank=bank: [
                               e.matmul(bank[0:64, 0:512],
                                        self.wpool[:, WQ + ci * 128 + h * 64: WQ + ci * 128 + (h + 1) * 64],
                                        self.xb[:, ci * T + ts * 512:ci * T + (ts + 1) * 512],
                                        start=(ci == 0), stop=(ci == NCHUNK - 1))
                               for ci in range(NCHUNK)][-1]),
                           nm + "pmm", 1,
                           waits=P.after(nm + "wlc", nm + "pev", nm + "lev", nm + "zp", "postln")
                           + self.xbok())
                    P.prod("vector",
                           (lambda e, h=h, ts=ts, bank=bank: e.tensor_copy(
                               QT[h][0:64, ts * 512:(ts + 1) * 512], bank[0:64, 0:512])),
                           nm + "pev", 1, waits=P.after(nm + "pmm", nm + "gsq"))
                    bank2 = self.pb[2 + ts % 2]
                    P.prod("tensor",
                           (lambda e, h=h, ts=ts, bank2=bank2: [
                               e.matmul(bank2[0:64, 0:512],
                                        self.wpool[:, WKU + ci * 128 + h * 64:
                                                    WKU + ci * 128 + h * 64 + 64],
                                        latf[:, ci * T + ts * 512:ci * T + (ts + 1) * 512],
                                        start=(ci == 0), stop=(ci == 1))
                               for ci in range(2)][-1]),
                           nm + "pmm", 1,
                           waits=P.after(nm + "wlc", nm + "pev", nm + "lev", nm + "zp", "postln"))
                    P.prod("vector",
                           (lambda e, h=h, ts=ts, bank2=bank2: e.tensor_copy(
                               KT[h][0:64, ts * 512:(ts + 1) * 512], bank2[0:64, 0:512])),
                           nm + "pev", 1, waits=P.after(nm + "pmm", nm + "gsq"))
            for ts in range(NTT):
                bank = self.pb[2 + ts % 2]
                P.prod("tensor",
                       (lambda e, ts=ts, bank=bank: [
                           e.matmul(bank[0:128, 0:512],
                                    self.wpool[:, WVU + ci * 128: WVU + (ci + 1) * 128],
                                    latf[:, ci * T + ts * 512:ci * T + (ts + 1) * 512],
                                    start=(ci == 0), stop=(ci == 1))
                           for ci in range(2)][-1]),
                       nm + "vmm", 1,
                       waits=P.after(nm + "wlc", nm + "vevv", nm + "lev", "postln")),
                P.prod("vector",
                       (lambda e, ts=ts, bank=bank: e.tensor_copy(
                           vfm[:, ts * 512:(ts + 1) * 512], bank[0:128, 0:512])),
                       nm + "vevv", 1, waits=P.after(nm + "vmm", nm + "omm"))
            for h in range(2):
                for half, dst in ((0, pre1), (1, pre2)):
                    wcol = WQR + h * 32 + half * 16
                    for ts in range(NTT):
                        bank = self.pb[6 + half]
                        P.prod("tensor",
                               (lambda e, wcol=wcol, ts=ts, bank=bank: [
                                   e.matmul(bank[0:16, 0:512],
                                            self.wpool[:, wcol + ci * 64: wcol + ci * 64 + 16],
                                            self.xb[:, ci * T + ts * 512:ci * T + (ts + 1) * 512],
                                            start=(ci == 0), stop=(ci == NCHUNK - 1))
                                   for ci in range(NCHUNK)][-1]),
                               nm + "qrm", 1,
                               waits=P.after(nm + "wlc", nm + "qrev", nm + "krok", "postln")
                               + self.xbok())
                        P.prod("vector",
                               (lambda e, dst=dst, ts=ts, bank=bank: e.tensor_copy(
                                   dst[:, ts * 512:(ts + 1) * 512], bank[0:16, 0:512])),
                               nm + "qrev", 1, waits=P.after(nm + "qrm", nm + "ro3", nm + "rodma"))
                rope16(pre1, pre2, QT[h][64:80, :], self.scra[80:96, 4096 + h * 2048:
                                                              4096 + (h + 1) * 2048],
                       nm + "qrev", nm + "q")
            for h in range(2):
                P.prod("vector",
                       (lambda e, h=h: e.tensor_copy(KT[h][64:80, :], krh1[:])),
                       nm + "krc", 1, waits=P.after(nm + "krok", nm + "zp", nm + "qmm"))
                P.prod("gpsimd",
                       (lambda e, h=h: e.dma_start(out=self.scra[80:96, 8192 + h * 2048:
                                                                 8192 + (h + 1) * 2048],
                                                   in_=self.scra[32:48, 14336:16384])),
                       nm + "krc2", 16, waits=P.after(nm + "krok", nm + "zp", nm + "qmm"))
            P.prod("vector", lambda e: e.nop(), nm + "pq", 1,
                   waits=P.after(nm + "pev", nm + "ro3", nm + "rodma", nm + "krc", nm + "krc2",
                                 nm + "zp"))
            for h in range(2):
                for kt in range(16):
                    pv = self.pb[4 + kt % 2].ap().bitcast(BF16)
                    P.prod("tensor",
                           (lambda e, h=h, kt=kt, pv=pv: e.transpose(
                               pv[0:128, 0:64],
                               vfm[h * 64:(h + 1) * 64, kt * 128:(kt + 1) * 128],
                               self.ident[h * 64:h * 64 + 64, h * 64:h * 64 + 64])),
                           nm + "vtr", 1,
                           waits=P.after(nm + "vevv", nm + "vev", nm + "pvm") + self.initw())
                    P.prod("vector",
                           (lambda e, h=h, kt=kt, pv=pv: e.tensor_copy(
                               vtm[:, h * 1024 + kt * 64: h * 1024 + (kt + 1) * 64], pv[0:128, 0:64])),
                           nm + "vev", 1, waits=P.after(nm + "vtr"))
            if L2STG < 2:
                continue
            for h in range(2):
                hglob = hp * 2 + h
                P.prod("vector", (lambda e: e.memset(rmax[:], 0.0)), nm + "gmx", 1,
                       waits=P.after(nm + "gred", nm + "gmx2", nm + "gsr"))
                for j, fm1 in enumerate((QT[h], KT[h])):
                    for ts in range(NTT):
                        P.prod("scalar",
                               (lambda e, fm1=fm1, ts=ts: e.activation(
                                   self.scrf[:, 0:512], fm1[:, ts * 512:(ts + 1) * 512],
                                   AF.Square)),
                               nm + "gsq", 1,
                               waits=P.after(nm + "pq", nm + "gm0", nm + "qma", nm + "qrd",
                                             "postln"))
                        P.prod("tensor",
                               (lambda e: e.matmul(self.pb[6][0:1, 0:512],
                                                   self.onescolf[:, 0:1],
                                                   self.scrf[:, 0:512], start=True, stop=True)),
                               nm + "gm0", 1, waits=P.after(nm + "gsq", nm + "gred", nm + "oev")
                               + self.initw())
                        P.prod("vector",
                               (lambda e: e.reduce_max(rtmp[0:1, 0:1], self.pb[6][0:1, 0:512],
                                                       axis=AX.X)),
                               nm + "gred", 1, waits=P.after(nm + "gm0", nm + "gmx2", nm + "gmx"))
                        P.prod("vector",
                               (lambda e, j=j: e.tensor_tensor(
                                   (rmax[:] if j == 0 else tk[:]) if False else rmax[:],
                                   rmax[:], rtmp[0:1, 0:1], ALU.max)),
                               nm + "gmx2", 1, waits=P.after(nm + "gred"))
                    P.prod("scalar",
                           (lambda e, j=j: e.activation(tq[:] if j == 0 else tk[:],
                                                        rmax[:], AF.Sqrt)),
                           nm + "gsr", 1, waits=P.after(nm + "gmx2"))
                    if j == 0:
                        P.prod("vector", (lambda e: e.memset(rmax[:], 0.0)), nm + "gmx", 1,
                               waits=P.after(nm + "gred", nm + "gmx2", nm + "gsr"))
                P.prod("vector", (lambda e: e.tensor_mul(g11[:], tq[:], tk[:])),
                       nm + "g1", 1, waits=P.after(nm + "gsr"))
                P.prod("vector", (lambda e: e.tensor_scalar(g11[:], g11[:], -SC2, None, ALU.mult)),
                       nm + "g2", 1, waits=P.after(nm + "g1"))
                P.prod("tensor",
                       (lambda e: e.matmul(self.pb[6][0:128, 0:1], self.onesrow[0:1, 0:128],
                                           g11[:], start=True, stop=True)),
                       nm + "g3", 1, waits=P.after(nm + "g2", nm + "gred")),
                P.prod("vector",
                       (lambda e: e.tensor_copy(gcol[:], self.pb[6][0:128, 0:1])),
                       nm + "g4", 1, waits=P.after(nm + "g3"))
                P.prod("vector",
                       (lambda e, hglob=hglob: e.tensor_tensor(
                           lg11[:], g11[:], self.l2a_lobo[0:1, hglob:hglob + 1], ALU.add)),
                       nm + "lg", 1, waits=P.after(nm + "g2", key + "b", nm + "mex"))
                if L2STG < 3:
                    continue
                for i in range(16):
                    nkb = (i + 4) // 4
                    for kb in range(nkb):
                        bank = self.pb[kb % 2]
                        P.prod("tensor",
                               (lambda e, i=i, kb=kb, h=h, bank=bank: e.matmul(
                                   bank[0:128, 0:512],
                                   QT[h][:, i * 128:(i + 1) * 128],
                                   KT[h][:, kb * 512:(kb + 1) * 512],
                                   start=True, stop=True)),
                               nm + "qmm", 1,
                               waits=P.after(nm + "pq", nm + "g4", nm + "qrd", nm + "exp2",
                                             nm + "gm0")),
                        if kb == nkb - 1:
                            oi = (i * 128 - kb * 512) // 128
                            P.prod("vector",
                                   (lambda e, oi=oi, bank=bank: e.tensor_tensor(
                                       self.scrf[:, 0:512], bank[0:128, 0:512],
                                       l2a_addm[:, oi * 512:(oi + 1) * 512], ALU.add)),
                                   nm + "qma", 1, waits=P.after(nm + "qmm", nm + "qrd",
                                                                nm + "gsq", nm + "mld"))
                            P.prod("vector",
                                   (lambda e, i=i, kb=kb: e.tensor_reduce(
                                       mcol[:, i:i + 1] if kb == 0 else rtmp[:],
                                       self.scrf[:, 0:512], AX.X, ALU.max)),
                                   nm + "qrd", 1, waits=P.after(nm + "qma"))
                        else:
                            P.prod("vector",
                                   (lambda e, i=i, kb=kb, bank=bank: e.tensor_reduce(
                                       mcol[:, i:i + 1] if kb == 0 else rtmp[:],
                                       bank[0:128, 0:512], AX.X, ALU.max)),
                                   nm + "qrd", 1, waits=P.after(nm + "qmm", nm + "qmx"))
                        if kb > 0:
                            P.prod("vector",
                                   (lambda e, i=i: e.tensor_tensor(
                                       mcol[:, i:i + 1], mcol[:, i:i + 1], rtmp[:], ALU.max)),
                                   nm + "qmx", 1, waits=P.after(nm + "qrd"))
                if L2STG < 4:
                    continue
                for qt in range(NTT):
                    nkt = (qt + 1) * 4
                    for kt in range(nkt):
                        bank = self.pb[kt % 2]
                        P.prod("tensor",
                               (lambda e, kt=kt, qt=qt, h=h, bank=bank: e.matmul(
                                   bank[0:128, 0:512],
                                   KT[h][:, kt * 128:(kt + 1) * 128],
                                   QT[h][:, qt * 512:(qt + 1) * 512],
                                   start=True, stop=True)),
                               nm + "sm2", 1,
                               waits=P.after(nm + "exp2", nm + "qmx", nm + "qrd")),
                        P.prod("scalar",
                               (lambda e, bank=bank: e.activation(
                                   et[:], bank[0:128, 0:512], AF.Exp, bias=gcol[:], scale=SC2)),
                               nm + "exp2", 1,
                               waits=P.after(nm + "sm2", nm + "pvm", nm + "dnm", nm + "emk")),
                        d = kt * 128 - qt * 512
                        if d >= 0:
                            P.prod("vector",
                                   (lambda e, d=d: e.tensor_mul(
                                       et[:], et[:],
                                       l2a_mulm[:, (d // 128) * 512:(d // 128 + 1) * 512])),
                                   nm + "emk", 1, waits=P.after(nm + "exp2", nm + "mld"))
                        P.prod("tensor",
                               (lambda e, kt=kt, h=h, nkt=nkt: e.matmul(
                                   self.pb[6][0:64, 0:512],
                                   vtm[:, h * 1024 + kt * 64: h * 1024 + (kt + 1) * 64], et[:],
                                   start=(kt == 0), stop=(kt == nkt - 1))),
                               nm + "pvm", 1,
                               waits=P.after(nm + "exp2", nm + "emk", nm + "oev"))
                        P.prod("tensor",
                               (lambda e, kt=kt, nkt=nkt: e.matmul(
                                   self.pb[5][0:1, 0:512], self.onescol[:, 0:1], et[:],
                                   start=(kt == 0), stop=(kt == nkt - 1))),
                               nm + "dnm", 1,
                               waits=P.after(nm + "exp2", nm + "emk", nm + "od"))
                    for sub in range(4):
                        pv3 = self.pb[7].ap().bitcast(BF16)
                        P.prod("vector",
                               (lambda e, qt=qt, sub=sub: e.tensor_copy(
                                   mcst[:, sub:sub + 1], mcol[:, qt * 4 + sub:qt * 4 + sub + 1])),
                               nm + "mc1", 1, waits=P.after(nm + "qrd", nm + "qmx", nm + "mtr"))
                        P.prod("tensor",
                               (lambda e, sub=sub, pv3=pv3: e.transpose(
                                   pv3[0:1, sub * 128:sub * 128 + 128],
                                   mcst[:, sub:sub + 1], self.ident[:])),
                               nm + "mtr", 1, waits=P.after(nm + "mc1", nm + "mex", nm + "omu") + self.initw())
                    P.prod("scalar",
                           (lambda e, pv3=self.pb[7].ap().bitcast(BF16): e.activation(
                               r512[:], pv3[0:1, 0:512], AF.Exp, bias=lg11[:], scale=SC2)),
                           nm + "mex", 1, waits=P.after(nm + "mtr", nm + "lg"))
                    P.prod("vector",
                           (lambda e: e.tensor_tensor(r512[:], r512[:],
                                                      self.pb[5][0:1, 0:512], ALU.add)),
                           nm + "od", 1, waits=P.after(nm + "mex", nm + "dnm"))
                    P.prod("vector",
                           (lambda e: e.reciprocal(r512[:], r512[:])),
                           nm + "od2", 1, waits=P.after(nm + "od"))
                    P.prod("tensor",
                           (lambda e: e.matmul(self.pb[7][0:64, 0:512],
                                               self.onesrow[0:1, 0:64], r512[:],
                                               start=True, stop=True)),
                           nm + "obc", 1, waits=P.after(nm + "od2", nm + "omu", nm + "mex")),
                    P.prod("vector",
                           (lambda e: e.tensor_copy(epi1[:], self.pb[6][0:64, 0:512])),
                           nm + "oev", 1, waits=P.after(nm + "pvm", nm + "ost")),
                    P.prod("vector",
                           (lambda e: e.tensor_mul(epi2[:], epi1[:], self.pb[7][0:64, 0:512])),
                           nm + "omu", 1, waits=P.after(nm + "obc", nm + "oev"))
                    P.prod("sync",
                           (lambda e, hglob=hglob, qt=qt: e.dma_start(
                               out=self.attn_dram[hglob * 64:(hglob + 1) * 64,
                                                  qt * 512:(qt + 1) * 512],
                               in_=epi2[:])),
                           nm + "ost", 16, waits=P.after(nm + "omu"))
        if L2STG < 5:
            P.prod("vector", lambda e: e.nop(), nm + "oadd!", 1,
                   waits=P.after(nm + "pev", nm + "qrev", nm + "vev", nm + "gsr", nm + "qrd",
                                 nm + "ost", nm + "lev", nm + "krev", nm + "krc", nm + "krc2",
                                 nm + "rodma"))
            return self.emit_postln(P.after(nm + "oadd!"))
        self.load_w(wc["o"], self.wpool2, 0, 1024, 0, 1024, nm + "wo",
                    wprep + P.after(nm + "lmm"))
        for ts in range(NTT):
            for ci in range(NCHUNK):
                P.prod("sync",
                       (lambda e, ci=ci, ts=ts: e.dma_start(
                           out=self.scrb[:, ci * 512: (ci + 1) * 512],
                           in_=self.attn_dram[ci * 128:(ci + 1) * 128, ts * 512:(ts + 1) * 512])),
                       nm + "ald", 16, waits=P.after(nm + "ost", nm + "omm", nm + "vevv",
                                                     nm + "qma", nm + "emk"))
            for oc in range(NCHUNK):
                bank = self.pb[oc % 4]
                P.prod("tensor",
                       (lambda e, oc=oc, ts=ts, bank=bank: [
                           e.matmul(bank[0:128, 0:512],
                                    self.wpool2[:, ci * 1024 + oc * 128: ci * 1024 + (oc + 1) * 128],
                                    self.scrb[:, ci * 512: (ci + 1) * 512],
                                    start=(ci == 0), stop=(ci == NCHUNK - 1))
                           for ci in range(NCHUNK)][-1]),
                       nm + "omm", 1,
                       waits=P.after(nm + "ald", nm + "wo") + P.after_lag(nm + "oadd!", 3)),
                P.prod("vector",
                       (lambda e, oc=oc, ts=ts, bank=bank: e.tensor_tensor(
                           self.xs[:, oc * T + ts * 512: oc * T + (ts + 1) * 512],
                           self.xs[:, oc * T + ts * 512: oc * T + (ts + 1) * 512],
                           bank[0:128, 0:512], ALU.add)),
                       nm + "oadd!", 1, waits=P.after(nm + "omm"))
        return self.emit_postln(P.after(nm + "oadd!"))



    def emit_l3a(self, b):
        nc, P = self.nc, self.P
        Wq = self.win_decl("L3_attn_Wq", [1024, 1024])
        bq = self.win_decl("L3_attn_bq", [1024])
        Wk = self.win_decl("L3_attn_Wk", [1024, 1024])
        bk = self.win_decl("L3_attn_bk", [1024])
        Wv = self.win_decl("L3_attn_Wv", [1024, 1024])
        bv = self.win_decl("L3_attn_bv", [1024])
        Wo = self.win_decl("L3_attn_Wo", [1024, 1024])
        bo = self.win_decl("L3_attn_bo", [1024])
        alpha = self.win_decl("L3_attn_alpha", [16, 5])
        phi = self.win_decl("L3_attn_phi", [16, 5])
        temp = self.win_decl("L3_attn_temperature", [1])
        gam = self.win_decl("L3_attn_gamma", [1024])
        bet = self.win_decl("L3_attn_beta", [1024])
        key = "l3a"
        FREQS = [31.0, 37.0, 41.0, 43.0, 47.0]
        if key not in self.once:
            self.once.add(key)
            P.extra_waits = P.after("postln") + self.xbok()
            self.wc_l3a = {}
            for n, src in (("q", Wq), ("k", Wk), ("v", Wv), ("o", Wo)):
                self.wc_l3a[n] = self.dram("wc3" + n, [1024, 1024])
            tmps = [self.prep_transpose_cast(w, 1024, 1024, key + "p")
                    for w in (Wq, Wk, Wv, Wo)]
            for (n, w), tmp in zip((("q", Wq), ("k", Wk), ("v", Wv), ("o", Wo)), tmps):
                self.prep_transpose(w, self.wc_l3a[n], 1024, 1024, key + "p", tmp=tmp)
            self.l3a_bcols = self.nc.alloc_sbuf_tensor("l3a_bcols", [128, 48], F32)
            for i, v in enumerate((bq, bk, bv, bo, gam, bet)):
                self.load_vec_cols(v, 1024, key + "b", dst=self.l3a_bcols, dst_off=8 * i)
            self.l3a_rows = self.scrf[0:1, 1031:1223]
            self.l3a_cols = self.nc.alloc_sbuf_tensor("l3a_cols", [128, 176], F32)
            c_wbase = nc.inline_tensor(
                np.asarray([[2.0 * math.pi / (f * 8.0) for f in FREQS]], dtype=np.float32),
                "c_wbase")
            self.l3a_wbase = self.nc.alloc_sbuf_tensor("l3a_wbase", [1, 5], F32)
            P.prod("sync", (lambda e: e.dma_start(out=self.l3a_wbase[:], in_=c_wbase.ap())),
                   key + "b", 16)
            R = self.l3a_rows
            PI = math.pi
            P.prod("vector", (lambda e: e.tensor_scalar(R[0:1, 96 + 0:96 + 0], R[0:1, 0:1], 0.0,
                                                        None, ALU.mult)) if False else
                   (lambda e: e.nop()), key + "b", 1)
            P.extra_waits = []
        wcq = self.wc_l3a
        nm = self.nid("l3a_")
        wprep = P.after(key + "p", key + "b", "postln")
        R = self.l3a_rows
        PI = math.pi
        P.prod("sync", (lambda e: e.dma_start(out=R[0:1, 0:80],
                                              in_=alpha.rearrange("(a h) f -> a (h f)", a=1))),
               nm + "rld", 16, waits=wprep)
        P.prod("sync", (lambda e: e.dma_start(out=R[0:1, 80:160],
                                              in_=phi.rearrange("(a h) f -> a (h f)", a=1))),
               nm + "rld", 16, waits=wprep)
        P.prod("sync", (lambda e: e.dma_start(out=R[0:1, 160:161],
                                              in_=temp.rearrange("(a n) -> a n", a=1))),
               nm + "rld", 16, waits=wprep)

        P.prod("vector", (lambda e: e.tensor_scalar(R[0:1, 96:176] if False else R[0:1, 96:97],
                                                    R[0:1, 96:97], 0.0, None, ALU.mult)) if False
               else (lambda e: e.nop()), nm + "sc0", 1, waits=wprep)
        ph2 = self.scrf[0:1, 1223:1303]
        m1 = self.scrf[0:1, 1303:1383]
        m2 = self.scrf[0:1, 1383:1463]
        app = m1
        P.prod("vector", (lambda e: e.tensor_scalar(m1[:], R[0:1, 80:160], PI, None, ALU.is_gt)),
               nm + "s1", 1, waits=wprep + P.after(nm + "bc", nm + "rld"))
        P.prod("vector", (lambda e: e.scalar_tensor_tensor(ph2[:], m1[:], -2.0 * PI,
                                                           R[0:1, 80:160], ALU.mult, ALU.add)),
               nm + "s2", 1, waits=P.after(nm + "s1"))
        P.prod("vector", (lambda e: e.tensor_scalar(m1[:], ph2[:], PI / 2, None, ALU.is_gt)),
               nm + "s3", 1, waits=P.after(nm + "s2"))
        P.prod("vector", (lambda e: e.tensor_scalar(m2[:], ph2[:], -PI / 2, None, ALU.is_lt)),
               nm + "s4", 1, waits=P.after(nm + "s2"))
        tdx = self.scrf[0:1, 1463:1536]
        P.prod("vector", (lambda e: e.scalar_tensor_tensor(ph2[:], m1[:], -PI, ph2[:],
                                                           ALU.mult, ALU.add)),
               nm + "s5", 1, waits=P.after(nm + "s3", nm + "s4"))
        P.prod("vector", (lambda e: e.scalar_tensor_tensor(ph2[:], m2[:], PI, ph2[:],
                                                           ALU.mult, ALU.add)),
               nm + "s6", 1, waits=P.after(nm + "s5"))
        P.prod("vector", (lambda e: e.tensor_add(m1[:], m1[:], m2[:])),
               nm + "s7", 1, waits=P.after(nm + "s5", nm + "s6"))
        P.prod("vector", (lambda e: e.tensor_scalar(m1[:], m1[:], -2.0, 1.0, ALU.mult, ALU.add)),
               nm + "s8", 1, waits=P.after(nm + "s7"))
        P.prod("vector", (lambda e: e.tensor_mul(app[:], m1[:], R[0:1, 0:80])),
               nm + "s9", 1, waits=P.after(nm + "s8"))
        trec = self.scrf[0:1, 1463:1464]
        P.prod("scalar", (lambda e: e.activation(trec[:], R[0:1, 160:161], AF.Abs)),
               nm + "s10", 1, waits=wprep + P.after(nm + "rld"))
        P.prod("vector", (lambda e: e.tensor_scalar(trec[:], trec[:], 0.1, None, ALU.add)),
               nm + "s11", 1, waits=P.after(nm + "s10"))
        P.prod("vector", (lambda e: e.reciprocal(trec[:], trec[:])),
               nm + "s12", 1, waits=P.after(nm + "s11"))
        wrow = self.scrf[0:1, 1464:1469]
        P.prod("vector", (lambda e: e.tensor_scalar(wrow[:], self.l3a_wbase[:], trec[:],
                                                    None, ALU.mult)),
               nm + "s13", 1, waits=P.after(nm + "s12") + wprep)
        for (src, off, n) in ((ph2, 0, 80), (app, 80, 80), (wrow, 160, 5)):
            P.prod("tensor",
                   (lambda e, src=src, n=n: e.matmul(self.pb[7][0:128, 0:n],
                                                     self.onesrow[0:1, 0:128], src[0:1, 0:n],
                                                     start=True, stop=True)),
                   nm + "bm", 1,
                   waits=P.after(nm + "s6", nm + "s9", nm + "s13", nm + "bc") + self.initw())
            P.prod("vector",
                   (lambda e, off=off, n=n: e.tensor_copy(self.l3a_cols[:, off:off + n],
                                                          self.pb[7][0:128, 0:n])),
                   nm + "bc", 1, waits=P.after(nm + "bm"))
        PHB, ALB, WCB = 0, 80, 160
        if not hasattr(self, "attn_dram"):
            self.attn_dram = self.dram("attn_dram", [1024, T], BF16)
        if not hasattr(self, "y_dram"):
            self.y_dram = self.dram("y_dram", [1024, T], BF16)

        qf = self.scra[:, 0:2048]
        kf = self.scra[:, 2048:4096]
        vf = self.scra[:, 4096:6144]
        vtm = self.scra[:, 6144:8224]
        sint = self.scra[:, 8224:8736]
        acc = self.scra[:, 8736:9248]
        e2 = self.scra[:, 9248:9760]
        epi1 = self.scra[0:64, 9760:10272]
        epi2 = self.scra[0:64, 10272:10784]
        r512 = self.scrf[0:1, 517:1029]
        murow = self.sb("tph", [128, D], BF16).ap().bitcast(F32)[0:1, 0:512]
        bcl = self.l3a_bcols

        for hp in range(8):
            wv = wprep + P.after(nm + "pvm", nm + "sin", nm + "pmm", nm + "acc")
            for j, wn in enumerate(("q", "k", "v")):
                self.load_w(self.wc_l3a[wn], self.wpool, j * 1024, 1024,
                            hp * 128, (hp + 1) * 128, nm + "wl", wv)
            for j, dst in enumerate((qf, kf, vf)):
                for ts in range(NTT):
                    bank = self.pb[ts % 2]
                    P.prod("tensor",
                           (lambda e, j=j, ts=ts, bank=bank: [
                               e.matmul(bank[0:128, 0:512],
                                        self.wpool[:, j * 1024 + ci * 128: j * 1024 + (ci + 1) * 128],
                                        self.xb[:, ci * T + ts * 512:ci * T + (ts + 1) * 512],
                                        start=(ci == 0), stop=(ci == NCHUNK - 1))
                               for ci in range(NCHUNK)][-1]),
                           nm + "pmm", 1,
                           waits=P.after(nm + "wl", nm + "pev", "postln") + self.xbok())
                    P.prod("vector",
                           (lambda e, j=j, dst=dst, ts=ts, bank=bank, hp=hp: e.tensor_scalar(
                               dst[:, ts * 512:(ts + 1) * 512], bank[0:128, 0:512],
                               bcl[:, 8 * j + hp:8 * j + hp + 1], None, ALU.add)),
                           nm + "pev", 1, waits=P.after(nm + "pmm"))
            P.prod("vector", lambda e: e.memset(vtm[:], 1.0), nm + "vms", 1,
                   waits=P.after(nm + "pvm"))
            for h in range(2):
                for kt in range(16):
                    pv = self.pb[2 + kt % 2].ap().bitcast(BF16)
                    P.prod("tensor",
                           (lambda e, h=h, kt=kt, pv=pv: e.transpose(
                               pv[0:128, 0:64],
                               vf[h * 64:(h + 1) * 64, kt * 128:(kt + 1) * 128],
                               self.ident[h * 64:h * 64 + 64, h * 64:h * 64 + 64])),
                           nm + "vtr", 1,
                           waits=P.after(nm + "pev", nm + "vev") + self.initw())
                    P.prod("vector",
                           (lambda e, h=h, kt=kt, pv=pv: e.tensor_copy(
                               vtm[:, h * 1040 + kt * 65: h * 1040 + kt * 65 + 64], pv[0:128, 0:64])),
                           nm + "vev", 1, waits=P.after(nm + "vtr", nm + "vms"))
            for h in range(2):
                hb = h * 64
                hglob = hp * 2 + h
                for qt in range(NTT):
                    for kt in range(16):
                        bank = self.pb[kt % 2]
                        P.prod("tensor",
                               (lambda e, kt=kt, qt=qt, hb=hb, bank=bank: e.matmul(
                                   bank[0:128, 0:512],
                                   kf[hb:hb + 64, kt * 128:(kt + 1) * 128],
                                   qf[hb:hb + 64, qt * 512:(qt + 1) * 512],
                                   start=True, stop=True)),
                               nm + "smm", 1,
                               waits=P.after(nm + "pev", nm + "sin", nm + "bc")),
                        import os as _os
                        if _os.environ.get("DBG_L3A") and hp == 0 and h == 0 and qt == 0 and kt == 0:
                            dq = self.nc.dram_tensor("dbg_qf", [128, 2048], BF16,
                                                     kind="ExternalOutput").ap()
                            dk = self.nc.dram_tensor("dbg_kf", [128, 2048], BF16,
                                                     kind="ExternalOutput").ap()
                            P.prod("sync", (lambda e, dq=dq: e.dma_start(out=dq, in_=qf[:])),
                                   nm + "dbg3", 16, waits=P.after(nm + "pev"))
                            P.prod("sync", (lambda e, dk=dk: e.dma_start(out=dk, in_=kf[:])),
                                   nm + "dbg3", 16, waits=P.after(nm + "pev"))
                            db = self.nc.dram_tensor("dbg_bcols", [128, 48], F32,
                                                     kind="ExternalOutput").ap()
                            P.prod("sync", (lambda e, db=db: e.dma_start(out=db, in_=bcl[:])),
                                   nm + "dbg3", 16, waits=P.after(nm + "pev"))
                            dsc = self.nc.dram_tensor("dbg_sc", [128, 512], F32,
                                                      kind="ExternalOutput").ap()
                            P.prod("vector",
                                   (lambda e, bank=bank: e.tensor_copy(self.scrf[:, 0:512],
                                                                       bank[0:128, 0:512])),
                                   nm + "dbg2", 1, waits=P.after(nm + "smm"))
                            P.prod("sync",
                                   (lambda e, dsc=dsc: e.dma_start(out=dsc, in_=self.scrf[:, 0:512])),
                                   nm + "dbg3", 16, waits=P.after(nm + "dbg2"))
                        for f in range(5):
                            P.prod("scalar",
                                   (lambda e, f=f, hglob=hglob, bank=bank: e.activation(
                                       sint[:], bank[0:128, 0:512], AF.Sin,
                                       bias=self.l3a_cols[:, PHB + hglob * 5 + f:
                                                          PHB + hglob * 5 + f + 1],
                                       scale=self.l3a_cols[:, WCB + f:WCB + f + 1])),
                                   nm + "sin", 1,
                                   waits=P.after(nm + "smm", nm + "acc", nm + "sgm", nm + "dbg2")),
                            if f == 0:
                                P.prod("vector",
                                       (lambda e, f=f, hglob=hglob: e.tensor_scalar(
                                           acc[:], sint[:],
                                           self.l3a_cols[:, ALB + hglob * 5 + f:
                                                         ALB + hglob * 5 + f + 1],
                                           None, ALU.mult)),
                                       nm + "acc", 1, waits=P.after(nm + "sin", nm + "sgm"))
                            else:
                                P.prod("vector",
                                       (lambda e, f=f, hglob=hglob: e.scalar_tensor_tensor(
                                           acc[:], sint[:],
                                           self.l3a_cols[:, ALB + hglob * 5 + f:
                                                         ALB + hglob * 5 + f + 1],
                                           acc[:], ALU.mult, ALU.add)),
                                       nm + "acc", 1, waits=P.after(nm + "sin"))
                        P.prod("scalar",
                               (lambda e: e.activation(e2[:], acc[:], AF.Sigmoid)),
                               nm + "sgm", 1, waits=P.after(nm + "acc", nm + "pvm")),
                        P.prod("tensor",
                               (lambda e, kt=kt, h=h: e.matmul(
                                   self.pb[6][0:65, 0:512],
                                   vtm[:, h * 1040 + kt * 65: h * 1040 + (kt + 1) * 65], e2[:],
                                   start=(kt == 0), stop=(kt == 15))),
                               nm + "pvm", 1, waits=P.after(nm + "sgm", nm + "oev", nm + "od"))
                    P.prod("vector",
                           (lambda e: e.tensor_scalar(r512[:], self.pb[6][64:65, 0:512],
                                                      1e-10, None, ALU.add)),
                           nm + "od", 1, waits=P.after(nm + "pvm"))
                    P.prod("vector",
                           (lambda e: e.reciprocal(r512[:], r512[:])),
                           nm + "od2", 1, waits=P.after(nm + "od"))
                    P.prod("tensor",
                           (lambda e: e.matmul(self.pb[7][0:64, 0:512],
                                               self.onesrow[0:1, 0:64], r512[:],
                                               start=True, stop=True)),
                           nm + "obc", 1, waits=P.after(nm + "od2", nm + "omu", nm + "bc")),
                    P.prod("vector",
                           (lambda e: e.tensor_copy(epi1[:], self.pb[6][0:64, 0:512])),
                           nm + "oev", 1, waits=P.after(nm + "od", nm + "ost")),
                    P.prod("vector",
                           (lambda e: e.tensor_mul(epi2[:], epi1[:], self.pb[7][0:64, 0:512])),
                           nm + "omu", 1, waits=P.after(nm + "obc", nm + "oev"))
                    P.prod("sync",
                           (lambda e, hglob=hglob, qt=qt: e.dma_start(
                               out=self.attn_dram[hglob * 64:(hglob + 1) * 64,
                                                  qt * 512:(qt + 1) * 512],
                               in_=epi2[:])),
                           nm + "ost", 16, waits=P.after(nm + "omu"))
        self.load_w(self.wc_l3a["o"], self.wpool2, 0, 1024, 0, 1024, nm + "wo",
                    wprep + P.after(nm + "pmm"))
        ytile = self.scrb[:, 4096 + 0:4096 + 512]
        stats = self.pb[5]
        rowm = self.sb("tph", [128, D], BF16).ap().bitcast(F32)[0:1, 0:512]
        rowr = self.sb("tpl", [128, D], BF16).ap().bitcast(F32)[0:1, 0:512]
        sq = self.scrb[:, 4608:5120]
        for ts in range(NTT):
            for ci in range(NCHUNK):
                P.prod("sync",
                       (lambda e, ci=ci, ts=ts: e.dma_start(
                           out=self.scrb[:, ci * 512: (ci + 1) * 512],
                           in_=self.attn_dram[ci * 128:(ci + 1) * 128, ts * 512:(ts + 1) * 512])),
                       nm + "ald", 16, waits=P.after(nm + "ost", nm + "omm"))
            for oc in range(NCHUNK):
                bank = self.pb[oc % 4]
                P.prod("tensor",
                       (lambda e, oc=oc, ts=ts, bank=bank: [
                           e.matmul(bank[0:128, 0:512],
                                    self.wpool2[:, ci * 1024 + oc * 128: ci * 1024 + (oc + 1) * 128],
                                    self.scrb[:, ci * 512: (ci + 1) * 512],
                                    start=(ci == 0), stop=(ci == NCHUNK - 1))
                           for ci in range(NCHUNK)][-1]),
                       nm + "omm", 1,
                       waits=P.after(nm + "ald", nm + "wo") + P.after_lag(nm + "yst!", 3)),
                P.prod("vector",
                       (lambda e, oc=oc, bank=bank: e.tensor_scalar(
                           ytile[:], bank[0:128, 0:512], bcl[:, 24 + oc:24 + oc + 1],
                           None, ALU.add)),
                       nm + "yev", 1, waits=P.after(nm + "omm", nm + "yst!"))
                P.prod("tensor",
                       (lambda e, oc=oc: e.matmul(stats[0:1, 0:512], self.onescol[:, 0:1],
                                                  ytile[:], start=(oc == 0),
                                                  stop=(oc == NCHUNK - 1))),
                       nm + "ymm", 1, waits=P.after(nm + "yev", nm + "mu1") + self.initw()),
                P.prod("scalar",
                       (lambda e: e.activation(sq[:], ytile[:], AF.Square)),
                       nm + "ysq", 1, waits=P.after(nm + "yev", nm + "ymm2")),
                P.prod("tensor",
                       (lambda e, oc=oc: e.matmul(self.pb[7][0:1, 0:512],
                                                  self.onescol[:, 0:1], sq[:],
                                                  start=(oc == 0), stop=(oc == NCHUNK - 1))),
                       nm + "ymm2", 1, waits=P.after(nm + "ysq", nm + "mu1")),
                P.prod("sync",
                       (lambda e, oc=oc, ts=ts: e.dma_start(
                           out=self.y_dram[oc * 128:(oc + 1) * 128, ts * 512:(ts + 1) * 512],
                           in_=ytile[:])),
                       nm + "yst!", 16, waits=P.after(nm + "yev"))
            P.prod("vector",
                   (lambda e: e.tensor_scalar(rowm[:], stats[0:1, 0:512], 1.0 / D, None,
                                              ALU.mult)),
                   nm + "mu1", 1, waits=P.after(nm + "ymm", nm + "ymm2"))
            P.prod("scalar",
                   (lambda e: e.activation(r512[:], rowm[:], AF.Square)),
                   nm + "mu2", 1, waits=P.after(nm + "mu1"))
            P.prod("vector",
                   (lambda e: e.tensor_scalar(rowr[:], self.pb[7][0:1, 0:512], 1.0 / D, None,
                                              ALU.mult)),
                   nm + "mu3", 1, waits=P.after(nm + "ymm2", nm + "mu2"))
            P.prod("vector",
                   (lambda e: e.tensor_sub(rowr[:], rowr[:], r512[:])),
                   nm + "mu4", 1, waits=P.after(nm + "mu3"))
            epsap = self.eps_col(1e-5, 1)
            P.prod("scalar",
                   (lambda e, epsap=epsap: e.activation(rowr[:], rowr[:], AF.Sqrt, bias=epsap)),
                   nm + "mu5", 1, waits=P.after(nm + "mu4") + self.initw())
            P.prod("vector",
                   (lambda e: e.reciprocal(rowr[:], rowr[:])),
                   nm + "mu6", 1, waits=P.after(nm + "mu5"))
            P.prod("tensor",
                   (lambda e: e.matmul(self.pb[4][0:128, 0:512], self.onesrow[0:1, 0:128],
                                       rowm[:], start=True, stop=True)),
                   nm + "bmu", 1, waits=P.after(nm + "mu6", nm + "ya1")),
            P.prod("tensor",
                   (lambda e: e.matmul(self.pb[6][0:128, 0:512], self.onesrow[0:1, 0:128],
                                       rowr[:], start=True, stop=True)),
                   nm + "brs", 1, waits=P.after(nm + "mu6", nm + "ya2", nm + "pvm", nm + "od")),
            for oc in range(NCHUNK):
                P.prod("sync",
                       (lambda e, oc=oc, ts=ts: e.dma_start(
                           out=ytile[:], in_=self.y_dram[oc * 128:(oc + 1) * 128,
                                                         ts * 512:(ts + 1) * 512])),
                       nm + "yld", 16, waits=P.after(nm + "ya1", nm + "yst!",
                                                     nm + "ymm", nm + "ymm2"))
                P.prod("vector",
                       (lambda e, oc=oc: e.tensor_sub(self.scrf[:, 0:512], ytile[:],
                                                      self.pb[4][0:128, 0:512])),
                       nm + "ya1", 1, waits=P.after(nm + "yld", nm + "bmu", nm + "ya2")),
                P.prod("vector",
                       (lambda e: e.tensor_mul(self.scrf[:, 0:512], self.scrf[:, 0:512],
                                               self.pb[6][0:128, 0:512])),
                       nm + "ya2", 1, waits=P.after(nm + "ya1", nm + "brs"))
                P.prod("vector",
                       (lambda e, oc=oc: e.tensor_scalar(
                           self.scrf[:, 0:512], self.scrf[:, 0:512],
                           bcl[:, 32 + oc:32 + oc + 1], bcl[:, 40 + oc:40 + oc + 1],
                           ALU.mult, ALU.add)),
                       nm + "ya3", 1, waits=P.after(nm + "ya2"))
                P.prod("vector",
                       (lambda e, oc=oc, ts=ts: e.tensor_tensor(
                           self.xs[:, oc * T + ts * 512: oc * T + (ts + 1) * 512],
                           self.xs[:, oc * T + ts * 512: oc * T + (ts + 1) * 512],
                           self.scrf[:, 0:512], ALU.add)),
                       nm + "ya4", 1, waits=P.after(nm + "ya3"))
        import os
        if os.environ.get("DBG_L3A"):
            d1 = self.nc.dram_tensor("dbg_cols3", [128, 176], F32, kind="ExternalOutput").ap()
            P.prod("sync", (lambda e: e.dma_start(out=d1[:, 0:165], in_=self.l3a_cols[:, 0:165])),
                   nm + "dbg", 16, waits=P.after(nm + "ya4"))
            d2 = self.nc.dram_tensor("dbg_attn3", [1024, T], BF16, kind="ExternalOutput").ap()
            for q in range(4):
                P.prod("sync", (lambda e, q=q: e.dma_start(
                    out=d2[q * 256:(q + 1) * 256, :],
                    in_=self.attn_dram[q * 256:(q + 1) * 256, :])),
                    nm + "dbg", 16, waits=P.after(nm + "ya4"))
            return self.emit_postln(P.after(nm + "ya4", nm + "dbg"))
        return self.emit_postln(P.after(nm + "ya4"))


LAYER_EMITTERS = {"l2f": "emit_l2f", "l3f": "emit_l3f", "l0f": "emit_l0f", "l0a": "emit_l0a", "l1a": "emit_l1a", "l1f": "emit_l1f", "l2a": "emit_l2a", "l3a": "emit_l3a"}
ALL_LAYERS = ["l0a", "l0f", "l1a", "l1f", "l2a", "l2f", "l3a", "l3f"]


def build_program(layers, nb=NB):
    B = Builder(layers, nb)
    P = B.P
    for b in range(nb):
        B.emit_loadx(b)
        for name in layers:
            getattr(B, LAYER_EMITTERS[name])(b)
        B.emit_storex(b)
    P.op("sync", lambda e: e.nop(), waits=P.after("xst"))
    P.build()
    return B


def run_layers(layers, x, inputs, nb=NB):
    B = build_program(layers, nb)
    in_maps = []
    for c in range(8):
        m = {"x_in": np.ascontiguousarray(x[nb * c:nb * (c + 1)])}
        for name in B.win:
            m[name] = np.ascontiguousarray(inputs[name])
        in_maps.append(m)
    res = run_bass_kernel_spmd(B.nc, in_maps, core_ids=list(range(8)))
    return np.concatenate([r["x_out"] for r in res.results], axis=0)


def kernel(**inputs):
    x = np.ascontiguousarray(inputs["L0_attn_x"]).astype(np.float32, copy=False)
    for group in [ALL_LAYERS]:
        x = run_layers(group, x, inputs)
    return x


# version 04 (final)
# speedup vs baseline: 1.0869x; 1.0869x over previous
import math
import zlib
import numpy as np
import ml_dtypes
import concourse.bass as bass
import concourse.mybir as mybir
from concourse.bass_utils import run_bass_kernel_spmd
from concourse.bass_core_rust import IndirectOffsetOnAxis

F32 = mybir.dt.float32
BF16 = mybir.dt.bfloat16
I32 = mybir.dt.int32
AF = mybir.ActivationFunctionType
ALU = mybir.AluOpType
AX = mybir.AxisListType
BF = ml_dtypes.bfloat16

D = 1024
T = 2048
NB = 2
NCHUNK = D // 128
NTT = T // 512
H = 16
DH = 64

ENGINES = ("sync", "gpsimd", "tensor", "vector", "scalar")


class Prog:

    POOLS = {"sync#c": 2, "gpsimd#c": 2, "tensor": 8, "vector": 10, "scalar": 6}

    def __init__(self, nc):
        self.nc = nc
        self.extra_waits = []
        self.ops = []
        self.pool = []
        self.cnt_ = []
        self.prange = {}
        for eng, n in self.POOLS.items():
            self.prange[eng] = (len(self.pool), n)
            for i in range(n):
                self.pool.append(nc.alloc_semaphore(f"p_{eng.replace(chr(35), chr(95))}_{i}"))
                self.cnt_.append(0)
        self.ded = {}
        self.gmap = {}

    @staticmethod
    def _group(name):
        return "".join(c for c in name if not c.isdigit())

    def _idx_for(self, name, engine, amt=1):
        if engine in ("sync", "gpsimd"):
            if amt != 16:
                engine = engine + "#c"
            else:
                engine = "dma"
        g = self._group(name)
        if g.endswith("!") or engine == "dma":
            key = (g, engine)
            if key not in self.ded:
                self.pool.append(self.nc.alloc_semaphore(f"d{len(self.ded)}"))
                self.cnt_.append(0)
                self.ded[key] = len(self.pool) - 1
            idx = self.ded[key]
        else:
            base, n = self.prange[engine]
            idx = base + (zlib.crc32(g.encode()) & 0x7FFFFFFF) % n
        em = self.gmap.setdefault(g, {})
        em[engine] = idx
        return idx

    def cnt(self, name):
        g = self._group(name)
        return sum(self.cnt_[i] for i in self.gmap.get(g, {}).values())

    def op(self, engine, fn, waits=(), incs=()):
        self.ops.append((engine, fn, list(waits), list(incs)))

    def prod(self, engine, fn, chan_name, amt, waits=()):
        i = self._idx_for(chan_name, engine, amt)
        self.cnt_[i] += amt
        w = list(waits) + list(self.extra_waits)
        self.op(engine, fn, waits=w, incs=[(self.pool[i], amt)])

    def after(self, *names):
        out = {}
        for n in names:
            g = self._group(n)
            for idx in self.gmap.get(g, {}).values():
                if self.cnt_[idx] > 0:
                    out[idx] = self.cnt_[idx]
        return [(self.pool[i], v) for i, v in out.items()]

    def after_lag(self, name, lag, step=1):
        g = self._group(name)
        assert g.endswith("!")
        em = self.gmap.get(g, {})
        assert len(em) <= 1
        out = []
        for idx in em.values():
            v = self.cnt_[idx] - lag * step
            if v > 0:
                out.append((self.pool[idx], v))
        return out

    def build(self):
        nc = self.nc
        with nc.Block() as blk:
            for engname in ENGINES:
                myops = [o for o in self.ops if o[0] == engname]
                if not myops:
                    continue

                def make(myops=myops):
                    def emit(eng):
                        for _, fn, waits, incs in myops:
                            for sem, val in waits:
                                eng.wait_ge(sem, val)
                            ins = fn(eng)
                            for sem, amt in incs:
                                assert ins is not None
                                ins = ins.then_inc(sem, amt)
                    return emit

                getattr(blk, engname)(make())


class Builder:
    def __init__(self, layers, nb=NB):
        self.layers = layers
        self.nb = nb
        nc = bass.Bass("TRN2", target_bir_lowering=False)
        self.nc = nc
        self.P = Prog(nc)
        self.uid = 0
        self.once = set()
        self.x_in = nc.dram_tensor("x_in", [nb, T, D], F32, kind="ExternalInput").ap()
        self.x_out = nc.dram_tensor("x_out", [nb, T, D], F32, kind="ExternalOutput").ap()
        self.win = {}
        self.xs = nc.alloc_sbuf_tensor("xs", [128, NCHUNK * T], F32)
        self.xb = nc.alloc_sbuf_tensor("xb", [128, NCHUNK * T], BF16)
        self.wpool = nc.alloc_sbuf_tensor("wpool", [128, 16384], BF16)
        self.wpool2 = nc.alloc_sbuf_tensor("wpool2", [128, 8192], BF16)
        self.cols = nc.alloc_sbuf_tensor("cols", [128, 16], F32)
        self.ident = nc.alloc_sbuf_tensor("identsb", [128, 128], BF16)
        self.onesrow = nc.alloc_sbuf_tensor("onesrow", [1, 128], F32)
        self.onescol = nc.alloc_sbuf_tensor("onescol", [128, 1], BF16)
        self.onescolf = nc.alloc_sbuf_tensor("onescolf", [128, 1], F32)
        self.scra = nc.alloc_sbuf_tensor("scra", [128, 16384], BF16)
        self.scrb = nc.alloc_sbuf_tensor("scrb", [128, 8192], BF16)
        self.scrf = nc.alloc_sbuf_tensor("scrf", [128, 1536], F32)
        c_ident = nc.inline_tensor(np.eye(128, dtype=BF), "c_ident")
        self.pb = [nc.alloc_psum_tensor(f"pb{i}", [128, 512], F32) for i in range(8)]

        P = self.P
        P.prod("sync", lambda e: e.dma_start(out=self.ident[:], in_=c_ident.ap()), "init", 16)
        P.prod("vector", lambda e: e.memset(self.onesrow[:], 1.0), "init", 1)
        P.prod("vector", lambda e: e.memset(self.onescol[:], 1.0), "init", 1)
        P.prod("vector", lambda e: e.memset(self.onescolf[:], 1.0), "init", 1)
        self._eps_cols = {}

    def nid(self, s):
        self.uid += 1
        return f"{s}{self.uid}"

    def initw(self):
        return self.P.after("init")

    def win_decl(self, name, shape):
        if name not in self.win:
            self.win[name] = self.nc.dram_tensor(name, list(shape), F32,
                                                 kind="ExternalInput").ap()
        return self.win[name]

    def dram(self, name, shape, dtype=BF16):
        return self.nc.dram_tensor(name, list(shape), dtype, kind="Internal").ap()

    _sbcache = None

    def sb(self, name, shape, dtype=BF16):
        if self._sbcache is None:
            self._sbcache = {}
        key = (name, tuple(shape), dtype)
        if key not in self._sbcache:
            self._sbcache[key] = self.nc.alloc_sbuf_tensor(self.nid(name), list(shape), dtype)
        return self._sbcache[key]

    def eps_col(self, val, nparts=128):
        if val not in self._eps_cols:
            idx = len(self._eps_cols)
            assert idx < 16
            self.P.prod("vector",
                        (lambda e, idx=idx, val=val: e.memset(self.cols[:, idx:idx + 1], val)),
                        "init", 1)
            self._eps_cols[val] = idx
        i = self._eps_cols[val]
        return self.cols[0:nparts, i:i + 1]

    def xbok(self):
        return self.P.after("xbok")

    def prep_cast(self, src_ap, dst_ap, rows, cols, chname):
        P = self.P
        r = 0
        while r < rows:
            rr = min(512, rows - r)
            P.prod("gpsimd",
                   (lambda e, r=r, rr=rr: e.dma_start(out=dst_ap[r:r + rr, :],
                                                      in_=src_ap[r:r + rr, :])),
                   chname, 16)
            r += rr

    def prep_transpose_cast(self, src_ap, o_dim, i_dim, chname):
        tmp = self.dram(self.nid("wtmp"), [o_dim, i_dim], BF16)
        self.prep_cast(src_ap, tmp, o_dim, i_dim, chname + "_c")
        return tmp

    def prep_transpose(self, src_ap, dst_ap, o_dim, i_dim, chname, tmp=None):
        P = self.P
        if tmp is None:
            tmp = self.prep_transpose_cast(src_ap, o_dim, i_dim, chname)
        wc = P.after(chname + "_c")
        sbuf = self.scra[:, 0:4096]
        for ib in range(i_dim // 128):
            tw = P.after("ptS")
            for ob in range(o_dim // 128):
                P.prod("sync",
                       (lambda e, ib=ib, ob=ob: e.dma_start_transpose(
                           sbuf[0:128, ob * 128:(ob + 1) * 128],
                           tmp[ob * 128:(ob + 1) * 128, ib * 128:(ib + 1) * 128])),
                       "ptT", 16, waits=wc + tw)
            P.prod("sync",
                   (lambda e, ib=ib: e.dma_start(
                       out=dst_ap[ib * 128:(ib + 1) * 128, 0:o_dim],
                       in_=sbuf[0:128, 0:o_dim])),
                   "ptS", 16, waits=P.after("ptT"))
        P.prod("sync", lambda e: e.nop(), chname, 1, waits=P.after("ptS"))

    def load_w(self, wc_ap, dst_sb, dst_off, in_dim, o_lo, o_hi, chname, waits):
        P = self.P
        ow = o_hi - o_lo
        for ci in range(in_dim // 128):
            P.prod("sync",
                   (lambda e, ci=ci: e.dma_start(
                       out=dst_sb[:, dst_off + ci * ow: dst_off + (ci + 1) * ow],
                       in_=wc_ap[ci * 128:(ci + 1) * 128, o_lo:o_hi])),
                   chname, 16, waits=waits)

    def load_vec_cols(self, src_ap, n, chname, dst, dst_off=0, waits=()):
        P = self.P
        nch = max(n // 128, 1)
        np_ = min(n, 128)
        stage = self.sb("vrow", [32, 128], F32)
        bstage = self.sb("vrowb", [32, 128], BF16)
        nm = self.nid("vc")
        assert nch <= 32
        src2 = src_ap.rearrange("(c p) -> c p", p=np_)
        P.prod("sync", lambda e: e.dma_start(out=stage[0:nch, 0:np_], in_=src2),
               nm + "ld", 16, waits=list(waits) + P.after("vcC"))
        P.prod("vector", lambda e: e.tensor_copy(bstage[0:nch, 0:np_], stage[0:nch, 0:np_]),
               nm + "c", 1, waits=P.after(nm + "ld"))
        pbank = self.pb[7]
        P.prod("tensor",
               lambda e: e.transpose(pbank.ap().bitcast(BF16)[0:np_, 0:nch],
                                     bstage[0:nch, 0:np_], self.ident[0:nch, 0:nch]),
               nm + "t", 1, waits=P.after(nm + "c", "pb7r") + self.initw())
        P.prod("vector",
               lambda e: e.tensor_copy(dst[0:np_, dst_off:dst_off + nch],
                                       pbank.ap().bitcast(BF16)[0:np_, 0:nch]),
               "pb7r", 1, waits=P.after(nm + "t"))
        P.prod("vector", lambda e: e.nop(), "vcC", 1, waits=P.after("pb7r"))
        P.prod("vector", lambda e: e.nop(), chname, 1, waits=P.after("pb7r"))

    def emit_loadx(self, b):
        P = self.P
        nm = self.nid("lx")
        stg = self.scrf[:, 0:1024]
        hi = self.sb("tph", [128, D], BF16)
        lo = self.sb("tpl", [128, D], BF16)
        for tt in range(T // 128):
            P.prod("sync",
                   (lambda e, tt=tt: e.dma_start(out=stg.bitcast(F32) if False else stg, in_=self.x_in[b, tt * 128:(tt + 1) * 128, :])),
                   nm + "ld", 16, waits=P.after(nm + "lo", "xst"))
            w = P.after(nm + "ld")
            P.prod("vector", (lambda e: e.tensor_copy(hi[:], stg)), nm + "hi", 1,
                   waits=w + P.after(nm + "t"))
            P.prod("vector", (lambda e: e.tensor_sub(lo[:], stg, hi[:])), nm + "lo", 1,
                   waits=P.after(nm + "hi") + P.after(nm + "t2"))
            for c in range(NCHUNK):
                pv = self.pb[c % 4].ap().bitcast(BF16)
                pv2 = self.pb[4 + c % 4].ap().bitcast(BF16)
                P.prod("tensor",
                       (lambda e, c=c, pv=pv: e.transpose(pv[0:128, 0:128], hi[:, c * 128:(c + 1) * 128],
                                                          self.ident[:])),
                       nm + "t", 1, waits=P.after(nm + "hi", nm + "evh") + self.initw())
                P.prod("vector",
                       (lambda e, c=c, tt=tt, pv=pv: e.tensor_copy(
                           self.xs[:, c * T + tt * 128: c * T + (tt + 1) * 128], pv[0:128, 0:128])),
                       nm + "evh", 1, waits=P.after(nm + "t"))
                P.prod("tensor",
                       (lambda e, c=c, pv2=pv2: e.transpose(pv2[0:128, 0:128], lo[:, c * 128:(c + 1) * 128],
                                                            self.ident[:])),
                       nm + "t2", 1, waits=P.after(nm + "lo", nm + "ev"))
                P.prod("vector",
                       (lambda e, c=c, tt=tt, pv2=pv2: e.tensor_tensor(
                           self.xs[:, c * T + tt * 128: c * T + (tt + 1) * 128],
                           self.xs[:, c * T + tt * 128: c * T + (tt + 1) * 128],
                           pv2[0:128, 0:128], ALU.add)),
                       nm + "ev", 1, waits=P.after(nm + "t2", nm + "evh"))
        for c in range(NCHUNK):
            P.prod("vector",
                   (lambda e, c=c: e.tensor_copy(self.xb[:, c * T:(c + 1) * T],
                                                 self.xs[:, c * T:(c + 1) * T])),
                   "xbok", 1, waits=P.after(nm + "ev", "postln"))

    def emit_storex(self, b):
        P = self.P
        nm = self.nid("sx")
        hi = self.sb("tph", [128, D], BF16)
        lo = self.sb("tpl", [128, D], BF16)
        otile = self.scrf[:, 0:1024]
        for tt in range(T // 128):
            w = P.after("postln") + P.after(nm + "t", nm + "t2")
            for c in range(NCHUNK):
                P.prod("vector",
                       (lambda e, c=c, tt=tt: e.tensor_copy(
                           hi[:, c * 128:(c + 1) * 128],
                           self.xs[:, c * T + tt * 128: c * T + (tt + 1) * 128])),
                       nm + "hi", 1, waits=w)
                P.prod("vector",
                       (lambda e, c=c, tt=tt: e.tensor_sub(
                           lo[:, c * 128:(c + 1) * 128],
                           self.xs[:, c * T + tt * 128: c * T + (tt + 1) * 128],
                           hi[:, c * 128:(c + 1) * 128])),
                       nm + "lo", 1, waits=P.after(nm + "hi"))
            for c in range(NCHUNK):
                pv = self.pb[c % 4].ap().bitcast(BF16)
                pv2 = self.pb[4 + c % 4].ap().bitcast(BF16)
                P.prod("tensor",
                       (lambda e, c=c, pv=pv: e.transpose(pv[0:128, 0:128],
                                                          hi[:, c * 128:(c + 1) * 128], self.ident[:])),
                       nm + "t", 1, waits=P.after(nm + "hi", nm + "evh") + self.initw())
                P.prod("vector",
                       (lambda e, c=c, pv=pv: e.tensor_copy(otile[:, c * 128:(c + 1) * 128],
                                                            pv[0:128, 0:128])),
                       nm + "evh", 1, waits=P.after(nm + "t") + P.after(nm + "st"))
                P.prod("tensor",
                       (lambda e, c=c, pv2=pv2: e.transpose(pv2[0:128, 0:128],
                                                            lo[:, c * 128:(c + 1) * 128], self.ident[:])),
                       nm + "t2", 1, waits=P.after(nm + "lo", nm + "ev")),
                P.prod("vector",
                       (lambda e, c=c, pv2=pv2: e.tensor_tensor(
                           otile[:, c * 128:(c + 1) * 128], otile[:, c * 128:(c + 1) * 128],
                           pv2[0:128, 0:128], ALU.add)),
                       nm + "ev", 1, waits=P.after(nm + "t2", nm + "evh"))
            P.prod("sync",
                   (lambda e, tt=tt, b=b: e.dma_start(
                       out=self.x_out[b, tt * 128:(tt + 1) * 128, :], in_=otile)),
                   "xst", 16, waits=P.after(nm + "ev"))
            P.prod("sync", lambda e: e.nop(), nm + "st", 1, waits=P.after("xst"))

    def emit_postln(self, waits):
        P = self.P
        nm = self.nid("ln")
        sq = self.scrb[:, 0:512]
        murow = self.sb("tph", [128, D], BF16).ap().bitcast(F32)[0:1, 0:512]
        rstdrow = self.sb("tpl", [128, D], BF16).ap().bitcast(F32)[0:1, 0:512]
        varrow = self.scrf[0:1, 1024:1536]
        stat = self.pb[7]
        bc = self.pb[6]
        for ts in range(NTT):
            w = waits + P.after(nm + "apl")
            for c in range(NCHUNK):
                P.prod("vector",
                       (lambda e, c=c, ts=ts: e.tensor_copy(
                           self.xb[:, c * T + ts * 512: c * T + (ts + 1) * 512],
                           self.xs[:, c * T + ts * 512: c * T + (ts + 1) * 512])),
                       nm + "cast", 1, waits=w)
            wc = P.after(nm + "cast")
            for c in range(NCHUNK):
                P.prod("tensor",
                       (lambda e, c=c, ts=ts: e.matmul(
                           stat[0:1, 0:512], self.onescol[:, 0:1],
                           self.xb[:, c * T + ts * 512: c * T + (ts + 1) * 512],
                           start=(c == 0), stop=(c == NCHUNK - 1))),
                       nm + "mmu", 1,
                       waits=wc + P.after(nm + "apl", "pb7r") + self.initw()),
            P.prod("vector",
                   (lambda e: e.tensor_scalar(murow[:], stat[0:1, 0:512], 1.0 / D, None, ALU.mult)),
                   nm + "muev", 1, waits=P.after(nm + "mmu"))
            for c in range(NCHUNK):
                P.prod("scalar",
                       (lambda e, c=c, ts=ts: e.activation(
                           sq[:], self.xb[:, c * T + ts * 512: c * T + (ts + 1) * 512],
                           AF.Square)),
                       nm + "sq", 1, waits=wc + P.after(nm + "mvar")),
                P.prod("tensor",
                       (lambda e, c=c: e.matmul(
                           stat[0:1, 0:512], self.onescol[:, 0:1], sq[:],
                           start=(c == 0), stop=(c == NCHUNK - 1))),
                       nm + "mvar", 1,
                       waits=P.after(nm + "sq", nm + "muev"))
            P.prod("vector",
                   (lambda e: e.tensor_scalar(varrow[:], stat[0:1, 0:512], 1.0 / D, None, ALU.mult)),
                   nm + "v1", 1, waits=P.after(nm + "mvar"))
            P.prod("scalar",
                   (lambda e: e.activation(rstdrow[:], murow[:], AF.Square)),
                   nm + "v2", 1, waits=P.after(nm + "v1", nm + "muev"))
            P.prod("vector",
                   (lambda e: e.tensor_sub(varrow[:], varrow[:], rstdrow[:])),
                   nm + "v3", 1, waits=P.after(nm + "v2"))
            epsap = self.eps_col(1e-5, 1)
            P.prod("scalar",
                   (lambda e, epsap=epsap: e.activation(varrow[:], varrow[:], AF.Sqrt,
                                                        bias=epsap)),
                   nm + "v4", 1, waits=P.after(nm + "v3") + self.initw())
            P.prod("vector",
                   (lambda e: e.reciprocal(rstdrow[:], varrow[:])),
                   nm + "v5", 1, waits=P.after(nm + "v4"))
            P.prod("tensor",
                   (lambda e: e.matmul(bc[0:128, 0:512], self.onesrow[0:1, 0:128],
                                       murow[0:1, 0:512], start=True, stop=True)),
                   nm + "bmu", 1, waits=P.after(nm + "muev", nm + "apl") + self.initw()),
            P.prod("tensor",
                   (lambda e: e.matmul(stat[0:128, 0:512], self.onesrow[0:1, 0:128],
                                       rstdrow[0:1, 0:512], start=True, stop=True)),
                   nm + "brs", 1, waits=P.after(nm + "v5"))
            wb = P.after(nm + "bmu", nm + "brs")
            for c in range(NCHUNK):
                sl = slice(c * T + ts * 512, c * T + (ts + 1) * 512)
                P.prod("vector",
                       (lambda e, sl=sl: e.tensor_sub(self.xs[:, sl], self.xs[:, sl],
                                                      bc[0:128, 0:512])),
                       nm + "ap1", 1, waits=wb)
                P.prod("vector",
                       (lambda e, sl=sl: e.tensor_mul(self.xs[:, sl], self.xs[:, sl],
                                                      stat[0:128, 0:512])),
                       nm + "ap2", 1, waits=P.after(nm + "ap1"))
                P.prod("vector",
                       (lambda e, sl=sl: e.tensor_copy(self.xb[:, sl], self.xs[:, sl])),
                       nm + "apl", 1, waits=P.after(nm + "ap2"))
        P.prod("vector", lambda e: e.nop(), "postln", 1, waits=P.after(nm + "apl"))
        P.prod("vector", lambda e: e.nop(), "xbok", 1, waits=P.after(nm + "apl"))
        return P.after("postln")

    def emit_l2f(self, b):
        nc, P = self.nc, self.P
        U1 = self.win_decl("L2_ffn_U1", [1024, 256])
        V1 = self.win_decl("L2_ffn_V1", [256, 4096])
        b1v = self.win_decl("L2_ffn_b1", [4096])
        U2 = self.win_decl("L2_ffn_U2", [4096, 256])
        V2 = self.win_decl("L2_ffn_V2", [256, 1024])
        b2v = self.win_decl("L2_ffn_b2", [1024])
        key = "l2f"
        if key not in self.once:
            self.once.add(key)
            P.extra_waits = P.after("postln") + self.xbok()
            self.wc_l2f = (self.dram("wcU1", [1024, 256]), self.dram("wcV1", [256, 4096]),
                           self.dram("wcU2", [4096, 256]), self.dram("wcV2", [256, 1024]))
            self.prep_cast(U1, self.wc_l2f[0], 1024, 256, key + "p")
            self.prep_cast(V1, self.wc_l2f[1], 256, 4096, key + "p")
            self.prep_cast(U2, self.wc_l2f[2], 4096, 256, key + "p")
            self.prep_cast(V2, self.wc_l2f[3], 256, 1024, key + "p")
            self.l2f_bcols = self.nc.alloc_sbuf_tensor("l2f_bcols", [128, 40], F32)
            self.load_vec_cols(b1v, 4096, key + "b", dst=self.l2f_bcols, dst_off=0)
            self.load_vec_cols(b2v, 1024, key + "b", dst=self.l2f_bcols, dst_off=32)
            P.extra_waits = []
        wcU1, wcV1, wcU2, wcV2 = self.wc_l2f
        bcols = self.l2f_bcols
        nm = self.nid("l2f_")
        wprep = P.after(key + "p", key + "b")

        Pm = self.scrb[:, 0:4096]
        Hm = self.scra[:, 0:8192]
        Qm = self.scrb[:, 4096:8192]
        t1 = self.scrf[:, 0:512]
        t2 = self.scrf[:, 512:1024]

        import os
        stages = int(os.environ.get("L2F_STAGES", "9"))
        wld = wprep + P.after("postln")
        self.load_w(wcU1, self.wpool, 0, 1024, 0, 256, nm + "wu1", wld)
        self.load_w(wcV1, self.wpool, 2048, 256, 0, 4096, nm + "wv1", wld)
        self.load_w(wcU2, self.wpool2, 0, 4096, 0, 256, nm + "wu2", wld)
        self.load_w(wcV2, self.wpool, 10240, 256, 0, 1024, nm + "wv2", wld)

        for ts in range(NTT):
            for oc in range(2):
                bank = self.pb[oc]
                P.prod("tensor",
                       (lambda e, ts=ts, oc=oc, bank=bank: [
                           e.matmul(bank[0:128, 0:512],
                                    self.wpool[:, ci * 256 + oc * 128: ci * 256 + (oc + 1) * 128],
                                    self.xb[:, ci * T + ts * 512:ci * T + (ts + 1) * 512],
                                    start=(ci == 0), stop=(ci == NCHUNK - 1))
                           for ci in range(NCHUNK)][-1]),
                       nm + "mmP", 1,
                       waits=P.after(nm + "wu1", nm + "evP", "postln") + self.xbok())
                P.prod("vector",
                       (lambda e, ts=ts, oc=oc, bank=bank: e.tensor_copy(
                           Pm[:, oc * T + ts * 512: oc * T + (ts + 1) * 512], bank[0:128, 0:512])),
                       nm + "evP", 1, waits=P.after(nm + "mmP"))
        if stages < 2:
            return self.emit_postln(P.after(nm + "evP"))
        c1 = math.sqrt(2.0 / math.pi)
        c2 = c1 * 0.044715
        for ts in range(NTT):
            wP = P.after(nm + "evP")
            for half in range(2):
                for hh in range(16):
                    hc = half * 16 + hh
                    bank = self.pb[hh % 4]
                    P.prod("tensor",
                           (lambda e, ts=ts, hc=hc, bank=bank: [
                               e.matmul(bank[0:128, 0:512],
                                        self.wpool[:, 2048 + ci * 4096 + hc * 128: 2048 + ci * 4096 + (hc + 1) * 128],
                                        Pm[:, ci * T + ts * 512: ci * T + (ts + 1) * 512],
                                        start=(ci == 0), stop=(ci == 1))
                               for ci in range(2)][-1]),
                           nm + "mmH", 1,
                           waits=wP + P.after(nm + "wv1")
                           + P.after_lag(nm + "g1!", 3) + P.after(nm + "mmQ"))
                    mmw = P.after(nm + "mmH")
                    P.prod("scalar",
                           (lambda e, hc=hc, bank=bank: e.activation(
                               t2[:], bank[0:128, 0:512], AF.Identity,
                               bias=bcols[:, hc:hc + 1], scale=1.0)),
                           nm + "g1!", 1, waits=mmw + P.after(nm + "gel!")),
                    P.prod("vector",
                           (lambda e: e.tensor_mul(t1[:], t2[:], t2[:])),
                           nm + "g2", 1, waits=P.after(nm + "g1!", nm + "gel!")),
                    P.prod("vector",
                           (lambda e: e.tensor_scalar(t1[:], t1[:], c2, c1, ALU.mult, ALU.add)),
                           nm + "g3", 1, waits=P.after(nm + "g2"))
                    P.prod("vector",
                           (lambda e: e.tensor_mul(t1[:], t1[:], t2[:])),
                           nm + "g4", 1, waits=P.after(nm + "g3"))
                    P.prod("scalar",
                           (lambda e: e.activation(t1[:], t1[:], AF.Tanh)),
                           nm + "g5", 1, waits=P.after(nm + "g4"))
                    P.prod("vector",
                           (lambda e: e.tensor_scalar(t1[:], t1[:], 1.0, None, ALU.add)),
                           nm + "g6", 1, waits=P.after(nm + "g5"))
                    P.prod("vector",
                           (lambda e, hh=hh: e.tensor_mul(Hm[:, hh * 512:(hh + 1) * 512], t1[:], t2[:])),
                           nm + "gel!", 1, waits=P.after(nm + "g6"))
                for oc in range(2):
                    bank = self.pb[4 + oc]
                    P.prod("tensor",
                           (lambda e, oc=oc, bank=bank, half=half: [
                               e.matmul(bank[0:128, 0:512],
                                        self.wpool2[:, (half * 16 + ci) * 256 + oc * 128: (half * 16 + ci) * 256 + (oc + 1) * 128],
                                        Hm[:, ci * 512:(ci + 1) * 512],
                                        start=(half == 0 and ci == 0), stop=(half == 1 and ci == 15))
                               for ci in range(16)][-1]),
                           nm + "mmQ", 1,
                           waits=P.after(nm + "gel!", nm + "wu2", nm + "evQ"))
            for oc in range(2):
                bank = self.pb[4 + oc]
                P.prod("vector",
                       (lambda e, ts=ts, oc=oc, bank=bank: e.tensor_scalar(
                           Qm[:, oc * T + ts * 512: oc * T + (ts + 1) * 512],
                           bank[0:128, 0:512], 0.5, None, ALU.mult)),
                       nm + "evQ", 1, waits=P.after(nm + "mmQ"))
        if stages < 3:
            return self.emit_postln(P.after(nm + "evQ"))
        for ts in range(NTT):
            for oc in range(NCHUNK):
                bank = self.pb[oc % 4]
                P.prod("tensor",
                       (lambda e, ts=ts, oc=oc, bank=bank: [
                           e.matmul(bank[0:128, 0:512],
                                    self.wpool[:, 10240 + ci * 1024 + oc * 128: 10240 + ci * 1024 + (oc + 1) * 128],
                                    Qm[:, ci * T + ts * 512: ci * T + (ts + 1) * 512],
                                    start=(ci == 0), stop=(ci == 1))
                           for ci in range(2)][-1]),
                       nm + "mmO", 1,
                       waits=P.after(nm + "evQ", nm + "wv2") + P.after_lag(nm + "evO!", 3)),
                P.prod("vector",
                       (lambda e, oc=oc, bank=bank: e.tensor_scalar(
                           t1[:], bank[0:128, 0:512], bcols[:, 32 + oc:32 + oc + 1], None, ALU.add)),
                       nm + "evO1", 1, waits=P.after(nm + "mmO", nm + "evO!")),
                P.prod("vector",
                       (lambda e, ts=ts, oc=oc: e.tensor_tensor(
                           self.xs[:, oc * T + ts * 512: oc * T + (ts + 1) * 512],
                           self.xs[:, oc * T + ts * 512: oc * T + (ts + 1) * 512],
                           t1[:], ALU.add)),
                       nm + "evO!", 1, waits=P.after(nm + "evO1"))
        return self.emit_postln(P.after(nm + "evO!"))



    def emit_l3f(self, b):
        nc, P = self.nc, self.P
        Wr = self.win_decl("L3_ffn_W_r", [1024, 1024])
        Wk = self.win_decl("L3_ffn_W_k", [1024, 1024])
        Wv = self.win_decl("L3_ffn_W_v", [1024, 1024])
        mur = self.win_decl("L3_ffn_mu_r", [1024])
        muk = self.win_decl("L3_ffn_mu_k", [1024])
        key = "l3f"
        if key not in self.once:
            self.once.add(key)
            P.extra_waits = P.after("postln") + self.xbok()
            self.wc_l3f = (self.dram("wcWr", [1024, 1024]), self.dram("wcWk", [1024, 1024]),
                           self.dram("wcWv", [1024, 1024]))
            tmps = [self.prep_transpose_cast(w, 1024, 1024, key + "p")
                    for w in (Wr, Wk, Wv)]
            for w, wc, tmp in zip((Wr, Wk, Wv), self.wc_l3f, tmps):
                self.prep_transpose(w, wc, 1024, 1024, key + "p", tmp=tmp)
            self.l3f_cols = self.nc.alloc_sbuf_tensor("l3f_cols", [128, 32], F32)
            self.load_vec_cols(mur, 1024, key + "b", dst=self.l3f_cols, dst_off=0)
            self.load_vec_cols(muk, 1024, key + "b", dst=self.l3f_cols, dst_off=8)
            P.prod("vector",
                   (lambda e: e.tensor_scalar(self.l3f_cols[:, 16:32], self.l3f_cols[:, 0:16],
                                              -1.0, 1.0, ALU.mult, ALU.add)),
                   key + "b", 1, waits=P.after(key + "b"))
            P.extra_waits = []
        wcWr, wcWk, wcWv = self.wc_l3f
        mc = self.l3f_cols
        nm = self.nid("l3f_")
        wld = P.after(key + "p", key + "b", "postln")
        self.load_w(wcWr, self.wpool, 0, 1024, 0, 1024, nm + "ww", wld)
        self.load_w(wcWv, self.wpool, 8192, 1024, 0, 1024, nm + "ww", wld)
        self.load_w(wcWk, self.wpool2, 0, 1024, 0, 1024, nm + "ww", wld)
        kact = self.scra
        xset = self.scrb
        t1 = self.scrf[:, 0:512]
        t2 = self.scrf[:, 512:1024]
        t3 = self.scrf[:, 1024:1536]

        def build_shift(dst_off, mu_off, omu_off, ci, ts, chname):
            base = ci * T + ts * 512
            dsl = slice(dst_off + ci * 512, dst_off + (ci + 1) * 512)
            P.prod("vector",
                   (lambda e: e.tensor_scalar(t1[:], self.xb[:, base:base + 512],
                                              mc[:, mu_off + ci:mu_off + ci + 1], None, ALU.mult)),
                   nm + "sh1", 1, waits=self.xbok() + P.after(nm + "xk", nm + "xr",
                                                                              nm + "mmK", nm + "mmA", nm + "sgm"))
            if ts > 0:
                P.prod("vector",
                       (lambda e: e.scalar_tensor_tensor(
                           xset[:, dsl], self.xb[:, base - 1:base + 511],
                           mc[:, omu_off + ci:omu_off + ci + 1], t1[:], ALU.mult, ALU.add)),
                       chname, 1, waits=P.after(nm + "sh1"))
            else:
                P.prod("vector",
                       (lambda e: e.scalar_tensor_tensor(
                           xset[:, dsl.start + 1:dsl.stop], self.xb[:, base:base + 511],
                           mc[:, omu_off + ci:omu_off + ci + 1], t1[:, 1:512], ALU.mult, ALU.add)),
                       chname, 1, waits=P.after(nm + "sh1"))
                P.prod("vector",
                       (lambda e: e.tensor_copy(xset[:, dsl.start:dsl.start + 1], t1[:, 0:1])),
                       chname, 1, waits=P.after(nm + "sh1"))

        for ts in range(NTT):
            for ci in range(NCHUNK):
                build_shift(0, 8, 24, ci, ts, nm + "xk")
            for oc in range(NCHUNK):
                bank = self.pb[oc % 4]
                P.prod("tensor",
                       (lambda e, oc=oc, bank=bank: [
                           e.matmul(bank[0:128, 0:512],
                                    self.wpool2[:, ci * 1024 + oc * 128: ci * 1024 + (oc + 1) * 128],
                                    xset[:, ci * 512:(ci + 1) * 512],
                                    start=(ci == 0), stop=(ci == NCHUNK - 1))
                           for ci in range(NCHUNK)][-1]),
                       nm + "mmK", 1,
                       waits=P.after(nm + "xk", nm + "ww", nm + "kre", "postln"))
                P.prod("vector",
                       (lambda e, bank=bank: e.tensor_scalar_max(t2[:], bank[0:128, 0:512], 0.0)),
                       nm + "kre", 1, waits=P.after(nm + "mmK", nm + "ksq"))
                P.prod("scalar",
                       (lambda e, oc=oc, ts=ts: e.activation(
                           kact[:, oc * T + ts * 512: oc * T + (ts + 1) * 512], t2[:], AF.Square)),
                       nm + "ksq", 1, waits=P.after(nm + "kre"))
            for ci in range(NCHUNK):
                build_shift(4096, 0, 16, ci, ts, nm + "xr")
            for oc in range(NCHUNK):
                bankA = self.pb[4 + oc % 2]
                bankB = self.pb[oc % 4]
                P.prod("tensor",
                       (lambda e, oc=oc, ts=ts, bankA=bankA: [
                           e.matmul(bankA[0:128, 0:512],
                                    self.wpool[:, ci * 1024 + oc * 128: ci * 1024 + (oc + 1) * 128],
                                    self.scrb[:, 4096 + ci * 512:4096 + (ci + 1) * 512],
                                    start=(ci == 0), stop=(ci == NCHUNK - 1))
                           for ci in range(NCHUNK)][-1]),
                       nm + "mmA", 1,
                       waits=P.after(nm + "xr", nm + "sgm", nm + "ksq"))
                P.prod("tensor",
                       (lambda e, oc=oc, ts=ts, bankB=bankB: [
                           e.matmul(bankB[0:128, 0:512],
                                    self.wpool[:, 8192 + ci * 1024 + oc * 128: 8192 + ci * 1024 + (oc + 1) * 128],
                                    kact[:, ci * T + ts * 512: ci * T + (ts + 1) * 512],
                                    start=(ci == 0), stop=(ci == NCHUNK - 1))
                           for ci in range(NCHUNK)][-1]),
                       nm + "mmB", 1,
                       waits=P.after(nm + "ksq", nm + "vmul"))
                P.prod("scalar",
                       (lambda e, bankA=bankA: e.activation(t1[:], bankA[0:128, 0:512], AF.Sigmoid)),
                       nm + "sgm", 1, waits=P.after(nm + "mmA", nm + "vmul"))
                P.prod("vector",
                       (lambda e, bankB=bankB: e.tensor_mul(t3[:], bankB[0:128, 0:512], t1[:])),
                       nm + "vmul", 1, waits=P.after(nm + "mmB", nm + "sgm", nm + "vadd"))
                P.prod("vector",
                       (lambda e, oc=oc, ts=ts: e.tensor_tensor(
                           self.xs[:, oc * T + ts * 512: oc * T + (ts + 1) * 512],
                           self.xs[:, oc * T + ts * 512: oc * T + (ts + 1) * 512],
                           t3[:], ALU.add)),
                       nm + "vadd", 1, waits=P.after(nm + "vmul"))
        import os
        if os.environ.get("DBG_L3F"):
            dbg1 = self.nc.dram_tensor("dbg_kact", [128, NCHUNK * T], BF16, kind="ExternalOutput").ap()
            dbg2 = self.nc.dram_tensor("dbg_xset", [128, 8192], BF16, kind="ExternalOutput").ap()
            for q in range(4):
                P.prod("sync", (lambda e, q=q: e.dma_start(out=dbg1[:, q * 4096:(q + 1) * 4096],
                                                           in_=kact[:, q * 4096:(q + 1) * 4096])),
                       "dbg", 16, waits=P.after(nm + "vadd"))
            for q in range(2):
                P.prod("sync", (lambda e, q=q: e.dma_start(out=dbg2[:, q * 4096:(q + 1) * 4096],
                                                           in_=self.scrb[:, q * 4096:(q + 1) * 4096])),
                       "dbg", 16, waits=P.after(nm + "vadd"))
            dbg3 = self.nc.dram_tensor("dbg_cols", [128, 32], F32, kind="ExternalOutput").ap()
            P.prod("sync", (lambda e: e.dma_start(out=dbg3, in_=self.l3f_cols[:])),
                   "dbg", 16, waits=P.after(nm + "vadd"))
            dbg4 = self.nc.dram_tensor("dbg_xb", [128, NCHUNK * T], BF16, kind="ExternalOutput").ap()
            for q in range(4):
                P.prod("sync", (lambda e, q=q: e.dma_start(out=dbg4[:, q * 4096:(q + 1) * 4096],
                                                           in_=self.xb[:, q * 4096:(q + 1) * 4096])),
                       "dbg", 16, waits=P.after(nm + "vadd"))
            P.op("sync", lambda e: e.nop(), waits=P.after("dbg"))
            return self.emit_postln(P.after(nm + "vadd", "dbg"))
        return self.emit_postln(P.after(nm + "vadd"))



    def prep_ternary(self, src_ap, dst_ap, o_dim, i_dim, chname):
        P = self.P
        nm = self.nid("tern")
        cw = min(i_dim, 2048)
        ncc = i_dim // cw
        wtout = self.scra[:, 0:cw]
        wrow = self.scra.ap().bitcast(F32)[:, 1024:1024 + cw]
        wabs = self.scra.ap().bitcast(F32)[:, 3072:3072 + cw]
        asum = self.scrf[:, 0:1]
        acol = self.scrf[:, 1:2]
        a11 = self.scrf[0:1, 2:3]
        tmp = self.dram(self.nid("terntmp"), [o_dim, i_dim], BF16)
        nch = o_dim // 128
        P.prod("vector", lambda e: e.memset(asum[:], 0.0), nm + "as", 1)
        for r in range(nch):
            for cc in range(ncc):
                P.prod("sync",
                       (lambda e, r=r, cc=cc: e.dma_start(
                           out=wrow[:], in_=src_ap[r * 128:(r + 1) * 128, cc * cw:(cc + 1) * cw])),
                       nm + "ld", 16,
                       waits=P.after(nm + "abs", nm + "sgn", nm + "st", "ptT", "ptS"))
                P.prod("scalar",
                       (lambda e: e.activation(wabs[:], wrow[:], AF.Abs, accum_out=acol[:])),
                       nm + "abs", 1, waits=P.after(nm + "ld", nm + "as"))
                P.prod("vector",
                       (lambda e: e.tensor_add(asum[:], asum[:], acol[:])),
                       nm + "as", 1, waits=P.after(nm + "abs"))
        P.prod("tensor",
               (lambda e: e.matmul(self.pb[5][0:1, 0:1], asum[:], self.onescolf[:, 0:1],
                                   start=True, stop=True)),
               nm + "t1", 1, waits=P.after(nm + "as", nm + "bc") + self.initw())
        P.prod("vector",
               (lambda e: e.tensor_scalar(a11[:], self.pb[5][0:1, 0:1],
                                          0.5 / (o_dim * i_dim), None, ALU.mult)),
               nm + "t2", 1, waits=P.after(nm + "t1"))
        P.prod("tensor",
               (lambda e: e.matmul(self.pb[5][0:128, 0:1], self.onesrow[0:1, 0:128], a11[:],
                                   start=True, stop=True)),
               nm + "t3", 1, waits=P.after(nm + "t2"))
        P.prod("vector",
               (lambda e: e.tensor_copy(acol[:], self.pb[5][0:128, 0:1])),
               nm + "bc", 1, waits=P.after(nm + "t3"))
        for r in range(nch):
            for cc in range(ncc):
                P.prod("sync",
                       (lambda e, r=r, cc=cc: e.dma_start(
                           out=wrow[:], in_=src_ap[r * 128:(r + 1) * 128, cc * cw:(cc + 1) * cw])),
                       nm + "ld2", 16, waits=P.after(nm + "gt", nm + "sgn", nm + "bc"))
                w2 = P.after(nm + "ld2")
                P.prod("scalar",
                       (lambda e: e.activation(wabs[:], wrow[:], AF.Abs)),
                       nm + "ab2", 1, waits=w2 + P.after(nm + "gt"))
                P.prod("vector",
                       (lambda e: e.tensor_scalar(wabs[:], wabs[:], acol[:], None, ALU.is_gt)),
                       nm + "gt", 1, waits=P.after(nm + "ab2"))
                P.prod("scalar",
                       (lambda e: e.activation(wrow[:], wrow[:], AF.Sign)),
                       nm + "sgn", 1, waits=w2 + P.after(nm + "mul", nm + "ab2")),
                P.prod("vector",
                       (lambda e: e.tensor_mul(wtout[:], wabs[:], wrow[:])),
                       nm + "mul", 1, waits=P.after(nm + "gt", nm + "sgn", nm + "st")),
                P.prod("sync",
                       (lambda e, r=r, cc=cc: e.dma_start(
                           out=tmp[r * 128:(r + 1) * 128, cc * cw:(cc + 1) * cw], in_=wtout[:])),
                       nm + "st", 16, waits=P.after(nm + "mul"))
        P.prod("sync", lambda e: e.nop(), chname + "_c", 1, waits=P.after(nm + "st"))
        self.prep_transpose(src_ap, dst_ap, o_dim, i_dim, chname, tmp=tmp)

    def emit_l0f(self, b):
        nc, P = self.nc, self.P
        wu = self.win_decl("L0_ffn_w_up", [4096, 1024])
        wd = self.win_decl("L0_ffn_w_down", [1024, 4096])
        key = "l0f"
        if key not in self.once:
            self.once.add(key)
            P.extra_waits = P.after("postln") + self.xbok()
            self.wc_l0f = (self.dram("wcWu", [1024, 4096]), self.dram("wcWd", [4096, 1024]))
            self.prep_ternary(wu, self.wc_l0f[0], 4096, 1024, key + "pu")
            self.prep_ternary(wd, self.wc_l0f[1], 1024, 4096, key + "pd")
            P.extra_waits = []
        wcU, wcD = self.wc_l0f
        nm = self.nid("l0f_")
        wprep = P.after(key + "pu", key + "pd", "postln")
        Hm = self.scra
        t1 = self.scrf[:, 0:512]
        t2 = self.scrf[:, 512:1024]
        t3 = self.scrf[:, 1024:1536]
        sqb = self.scrb[:, 0:512]
        r1 = self.sb("tph", [128, D], BF16).ap().bitcast(F32)[0:1, 0:512]
        stats = self.pb[5]
        bcb = self.pb[6]

        for ts in range(NTT):
            for c in range(NCHUNK):
                P.prod("scalar",
                       (lambda e, c=c, ts=ts: e.activation(
                           sqb[:], self.xb[:, c * T + ts * 512: c * T + (ts + 1) * 512],
                           AF.Square)),
                       nm + "sq", 1, waits=self.xbok() + P.after(nm + "smm", "postln"))
                P.prod("tensor",
                       (lambda e, c=c: e.matmul(stats[0:1, 0:512], self.onescol[:, 0:1], sqb[:],
                                                start=(c == 0), stop=(c == NCHUNK - 1))),
                       nm + "smm", 1, waits=P.after(nm + "sq", nm + "r1u", nm + "r2u") + self.initw())
            P.prod("scalar",
                   (lambda e: e.activation(r1[:], stats[0:1, 0:512], AF.Sqrt, scale=1.0 / D)),
                   nm + "r1a", 1, waits=P.after(nm + "smm"))
            P.prod("vector",
                   (lambda e: e.tensor_scalar(r1[:], r1[:], 1e-8, None, ALU.add)),
                   nm + "r1b", 1, waits=P.after(nm + "r1a"))
            P.prod("vector",
                   (lambda e: e.reciprocal(r1[:], r1[:])),
                   nm + "r1u", 1, waits=P.after(nm + "r1b"))
            P.prod("tensor",
                   (lambda e: e.matmul(bcb[0:128, 0:512], self.onesrow[0:1, 0:128], r1[:],
                                       start=True, stop=True)),
                   nm + "r1m", 1, waits=P.after(nm + "r1u", nm + "hmul")),
            P.prod("vector",
                   (lambda e: e.tensor_copy(t1[:], bcb[0:128, 0:512])),
                   nm + "r1c", 1, waits=P.after(nm + "r1m"))
            for half in range(2):
                self.load_w(wcU, self.wpool, 0, 1024, half * 2048, (half + 1) * 2048,
                            nm + "wu", wprep + P.after(nm + "mmH", nm + "mmO"))
                for hh in range(16):
                    hc = half * 16 + hh
                    bank = self.pb[hh % 4]
                    P.prod("tensor",
                           (lambda e, hc=hc, hh=hh, ts=ts, bank=bank: [
                               e.matmul(bank[0:128, 0:512],
                                        self.wpool[:, ci * 2048 + hh * 128: ci * 2048 + (hh + 1) * 128],
                                        self.xb[:, ci * T + ts * 512:ci * T + (ts + 1) * 512],
                                        start=(ci == 0), stop=(ci == NCHUNK - 1))
                               for ci in range(NCHUNK)][-1]),
                           nm + "mmH", 1,
                           waits=P.after(nm + "wu", nm + "r1c") + P.after_lag(nm + "hgel!", 3))
                    P.prod("vector",
                           (lambda e, bank=bank: e.tensor_mul(t2[:], bank[0:128, 0:512], t1[:])),
                           nm + "hmul", 1, waits=P.after(nm + "mmH", nm + "hsq"))
                    P.prod("scalar",
                           (lambda e, hc=hc: e.activation(
                               Hm[:, hc * 512:(hc + 1) * 512], t2[:], AF.Gelu)),
                           nm + "hgel!", 1, waits=P.after(nm + "hmul"))
                    P.prod("scalar",
                           (lambda e, hc=hc: e.activation(sqb[:], Hm[:, hc * 512:(hc + 1) * 512],
                                                          AF.Square)),
                           nm + "hsq", 1, waits=P.after(nm + "hgel!", nm + "smm2", nm + "smm"))
                    P.prod("tensor",
                           (lambda e, hc=hc: e.matmul(stats[0:1, 0:512], self.onescol[:, 0:1],
                                                      sqb[:], start=(hc == 0), stop=(hc == 31))),
                           nm + "smm2", 1, waits=P.after(nm + "hsq", nm + "r1m"))
            P.prod("scalar",
                   (lambda e: e.activation(r1[:], stats[0:1, 0:512], AF.Sqrt, scale=1.0 / 4096)),
                   nm + "r2a", 1, waits=P.after(nm + "smm2", nm + "oc1"))
            P.prod("vector",
                   (lambda e: e.tensor_scalar(r1[:], r1[:], 1e-8, None, ALU.add)),
                   nm + "r2b", 1, waits=P.after(nm + "r2a"))
            P.prod("vector",
                   (lambda e: e.reciprocal(r1[:], r1[:])),
                   nm + "r2u", 1, waits=P.after(nm + "r2b"))
            P.prod("tensor",
                   (lambda e: e.matmul(bcb[0:128, 0:512], self.onesrow[0:1, 0:128], r1[:],
                                       start=True, stop=True)),
                   nm + "r2m", 1, waits=P.after(nm + "r2u", nm + "oc1", nm + "hmul")),
            P.prod("vector",
                   (lambda e: e.tensor_copy(t1[:], bcb[0:128, 0:512])),
                   nm + "r2c", 1, waits=P.after(nm + "r2m"))
            for oc in range(NCHUNK):
                self.load_w(wcD, self.wpool2, 0, 4096, oc * 128, (oc + 1) * 128,
                            nm + "wd", wprep + P.after(nm + "mmO"))
                bank = self.pb[4 + oc % 1]
                P.prod("tensor",
                       (lambda e, oc=oc, ts=ts, bank=bank: [
                           e.matmul(bank[0:128, 0:512],
                                    self.wpool2[:, ci * 128: (ci + 1) * 128],
                                    Hm[:, ci * 512:(ci + 1) * 512],
                                    start=(ci == 0), stop=(ci == 31))
                           for ci in range(32)][-1]),
                       nm + "mmO", 1,
                       waits=P.after(nm + "wd", nm + "hgel!", nm + "r2c", nm + "oc1"))
                P.prod("vector",
                       (lambda e, bank=bank: e.tensor_mul(t3[:], bank[0:128, 0:512], t1[:])),
                       nm + "oc1", 1, waits=P.after(nm + "mmO", nm + "oc2"))
                P.prod("vector",
                       (lambda e, oc=oc, ts=ts: e.tensor_tensor(
                           self.xs[:, oc * T + ts * 512: oc * T + (ts + 1) * 512],
                           self.xs[:, oc * T + ts * 512: oc * T + (ts + 1) * 512],
                           t3[:], ALU.add)),
                       nm + "oc2", 1, waits=P.after(nm + "oc1"))
        return self.emit_postln(P.after(nm + "oc2"))



    def emit_l1f(self, b):
        nc, P = self.nc, self.P
        Wr = self.win_decl("L1_ffn_Wr", [1024, 8])
        W1 = self.win_decl("L1_ffn_W1", [8, 1024, 4096])
        b1v = self.win_decl("L1_ffn_b1", [8, 4096])
        W2 = self.win_decl("L1_ffn_W2", [8, 4096, 1024])
        b2v = self.win_decl("L1_ffn_b2", [8, 1024])
        key = "l1f"
        CAP = 320
        NR = 2560
        BIGF = 1e4
        BIGI = 1e7
        if key not in self.once:
            self.once.add(key)
            P.extra_waits = P.after("postln") + self.xbok()
            self.wc_l1f = (self.dram("wcMr", [1024, 8]),
                           self.dram("wcM1", [8192, 4096]),
                           self.dram("wcM2", [32768, 1024]))
            self.prep_cast(Wr, self.wc_l1f[0], 1024, 8, key + "p")
            w1f = W1.rearrange("e c h -> (e c) h")
            for r in range(0, 8192, 256):
                P.prod("gpsimd",
                       (lambda e, r=r: e.dma_start(out=self.wc_l1f[1][r:r + 256, :],
                                                   in_=w1f[r:r + 256, :])),
                       key + "p", 16)
            w2f = W2.rearrange("e h c -> (e h) c")
            self.prep_cast(w2f, self.wc_l1f[2], 32768, 1024, key + "p")
            self.l1f_bcols = nc.alloc_sbuf_tensor("l1f_bcols", [128, 320], F32)
            for e_ in range(8):
                self.load_vec_cols(b1v[e_], 4096, key + "b", dst=self.l1f_bcols,
                                   dst_off=e_ * 32)
                self.load_vec_cols(b2v[e_], 1024, key + "b", dst=self.l1f_bcols,
                                   dst_off=256 + e_ * 8)
            self.l1f_wr = nc.alloc_sbuf_tensor("l1f_wr", [128, 64], BF16)
            self.load_w(self.wc_l1f[0], self.l1f_wr, 0, 1024, 0, 8, key + "wr",
                        P.after(key + "p"))
            c_sut = nc.inline_tensor(np.triu(np.ones((128, 128)), 1).astype(BF),
                                     "c_l1f_sut")
            c_on = nc.inline_tensor(np.ones((128, 128), BF), "c_l1f_on")
            c_rev = nc.inline_tensor(
                np.ascontiguousarray((8.0 - np.arange(8, dtype=np.float32))[None, :]
                                     .repeat(128, 0)), "c_l1f_rev")
            tid = np.zeros((128, 96), dtype=np.float32)
            p_ = np.arange(128)
            for tt in range(16):
                tid[:, tt * 6 + 0] = tt * 4 + p_ // 32
                tid[:, tt * 6 + 1] = p_ % 32
                tid[:, tt * 6 + 2] = tt * 8 + p_ // 16
                tid[:, tt * 6 + 3] = (2 * p_) % 32
                tid[:, tt * 6 + 4] = tt * 8 + p_ // 16
                tid[:, tt * 6 + 5] = (2 * p_ + 1) % 32
            c_tid = nc.inline_tensor(tid.astype(BF), "c_l1f_tid")
            c_iot = nc.inline_tensor(
                np.ascontiguousarray(np.arange(512, dtype=np.float32)[None, :]
                                     .repeat(128, 0)), "c_l1f_iot")
            self.l1f_csut = c_sut
            self.l1f_cson = c_on
            self.l1f_crev = c_rev
            self.l1f_ctid = c_tid
            self.l1f_ciot = c_iot
            self.l1f_w = nc.alloc_sbuf_tensor("l1f_w", [128, 64], F32)
            self.l1f_sl6 = nc.alloc_sbuf_tensor("l1f_sl6", [128, 16], BF16)
            self.l1f_invi = nc.alloc_sbuf_tensor("l1f_invi", [128, 20], I32)
            self.l1f_si = nc.alloc_sbuf_tensor("l1f_si", [128, 20], I32)
            self.l1f_wc = nc.alloc_sbuf_tensor("l1f_wc", [128, 24], F32)
            self.l1f_buf = self.dram("l1f_buf", [NR, 1024])
            self.l1f_ybuf = self.dram("l1f_ybuf", [NR, 1024])
            self.l1f_xtm = self.dram("l1f_xtm", [T, 1024])
            self.l1f_ytok = self.dram("l1f_ytok", [2 * T, 1024])
            self.l1f_rows = self.dram("l1f_rows", [3, NR], F32)
            P.extra_waits = []
        nm = self.nid("l1f_")
        bcols = self.l1f_bcols
        buf, ybuf = self.l1f_buf, self.l1f_ybuf
        rt = self.scrf[:, 0:8]
        A = self.scrf[:, 8:16]
        Bv = self.scrf[:, 16:24]
        tot = self.scrf[:, 24:32]
        prev = self.scrf[:, 32:40]
        etmp = self.scrf[:, 48:56]
        CB = 56

        def C(i):
            return self.scrf[:, CB + i:CB + i + 1]
        ohsb = self.scrb[:, 6976:6984]
        xtm = self.scrb[:, 5632:6656]
        zb = self.scrb[:, 2560:3584]
        sutb = self.scrb[:, 3584:3712]
        onb = self.scrb[:, 3712:3840]
        tidb = self.scrb[:, 3840:3936]
        revb = self.scrb.ap().bitcast(F32)[:, 1968:1976]
        wcld = P.after("postln", nm + "pmm", nm + "rv", nm + "ia")
        P.prod("sync",
               (lambda e: e.dma_start(out=sutb[:], in_=self.l1f_csut.ap())),
               nm + "cld", 16, waits=wcld)
        P.prod("sync",
               (lambda e: e.dma_start(out=onb[:], in_=self.l1f_cson.ap())),
               nm + "cld", 16, waits=wcld)
        P.prod("sync",
               (lambda e: e.dma_start(out=tidb[:], in_=self.l1f_ctid.ap())),
               nm + "cld", 16, waits=wcld)
        P.prod("sync",
               (lambda e: e.dma_start(out=revb[:], in_=self.l1f_crev.ap())),
               nm + "cld", 16, waits=wcld)

        def rv(fn, *extra):
            P.prod("vector", fn, nm + "rv", 1,
                   waits=P.after(nm + "rv", nm + "rev") + sum(map(list, extra), []))

        P.prod("vector", lambda e: e.memset(zb[0:128, :], 0.0), nm + "zst", 1,
               waits=P.after("postln", nm + "fyw"))
        for k in range(2 * T // 128):
            P.prod("sync",
                   (lambda e, k=k: e.dma_start(
                       out=self.l1f_ytok[k * 128:(k + 1) * 128, :], in_=zb[0:128, :])),
                   nm + "fbz", 16, waits=P.after(nm + "zst", nm + "fgd"))
        P.prod("vector", lambda e: e.memset(prev[:], 0.0), nm + "rpv", 1,
               waits=P.after("postln", nm + "rto"))
        P.prod("vector", lambda e: e.memset(self.l1f_sl6[:], 1.0), nm + "ia", 1,
               waits=P.after("postln", nm + "imm"))
        for tt in range(16):
            P.prod("tensor",
                   (lambda e, tt=tt: [
                       e.matmul(self.pb[0][0:128, 0:8],
                                self.xb[:, c * T + tt * 128:c * T + tt * 128 + 128],
                                self.l1f_wr[:, c * 8:(c + 1) * 8],
                                start=(c == 0), stop=(c == NCHUNK - 1))
                       for c in range(NCHUNK)][-1]),
                   nm + "zmm", 1,
                   waits=P.after(key + "wr", nm + "rev", "postln") + self.xbok())
            P.prod("vector", lambda e: e.tensor_copy(rt[:], self.pb[0][0:128, 0:8]),
                   nm + "rev", 1, waits=P.after(nm + "zmm", nm + "rv"))
            rv(lambda e: e.tensor_reduce(C(0), rt[:], AX.X, ALU.max))
            rv(lambda e: e.tensor_scalar(A[:], rt[:], C(0), None, ALU.is_equal))
            rv(lambda e: e.tensor_tensor(A[:], A[:], revb[:], ALU.mult),
               P.after(nm + "cld"))
            rv(lambda e: e.tensor_reduce(C(2), A[:], AX.X, ALU.max))
            rv(lambda e: e.tensor_scalar(A[:], A[:], C(2), None, ALU.is_equal))
            rv(lambda e: e.tensor_scalar(C(3), C(2), -1.0, 8.0, ALU.mult, ALU.add))
            rv(lambda e: e.scalar_tensor_tensor(etmp[:], A[:], -BIGF, rt[:],
                                                ALU.mult, ALU.add))
            rv(lambda e: e.tensor_reduce(C(4), etmp[:], AX.X, ALU.max))
            rv(lambda e: e.tensor_scalar(Bv[:], etmp[:], C(4), None, ALU.is_equal))
            rv(lambda e: e.tensor_tensor(Bv[:], Bv[:], revb[:], ALU.mult))
            rv(lambda e: e.tensor_reduce(C(5), Bv[:], AX.X, ALU.max))
            rv(lambda e: e.tensor_scalar(Bv[:], Bv[:], C(5), None, ALU.is_equal))
            rv(lambda e: e.tensor_scalar(C(6), C(5), -1.0, 8.0, ALU.mult, ALU.add))
            rv(lambda e: e.tensor_scalar(C(1), C(0), -1.0, None, ALU.mult))
            P.prod("scalar",
                   (lambda e: e.activation(etmp[:], rt[:], AF.Exp, bias=C(1),
                                           accum_out=C(7))),
                   nm + "rs", 1, waits=P.after(nm + "rv", nm + "rs"))
            rv(lambda e: e.reciprocal(C(8), C(7)), P.after(nm + "rs"))
            P.prod("scalar",
                   (lambda e: e.activation(C(9), C(4), AF.Exp, bias=C(1))),
                   nm + "rs", 1, waits=P.after(nm + "rv", nm + "rs"))
            rv(lambda e: e.tensor_tensor(C(10), C(9), C(8), ALU.mult), P.after(nm + "rs"))
            rv(lambda e: e.tensor_tensor(etmp[:], A[:], Bv[:], ALU.add))
            rv(lambda e: e.tensor_copy(ohsb[:], etmp[:]))
            P.prod("tensor",
                   (lambda e: e.matmul(self.pb[1][0:128, 0:8], sutb[:],
                                       ohsb[:], start=True, stop=True)),
                   nm + "pmm", 1,
                   waits=P.after(nm + "rv", nm + "rto", nm + "cld"))
            P.prod("tensor",
                   (lambda e: e.matmul(self.pb[2][0:128, 0:8], onb[:],
                                       ohsb[:], start=True, stop=True)),
                   nm + "pmm", 1, waits=P.after(nm + "rv", nm + "rpv", nm + "cld"))
            P.prod("vector",
                   (lambda e: e.tensor_tensor(tot[:], self.pb[1][0:128, 0:8], prev[:],
                                              ALU.add)),
                   nm + "rto", 1, waits=P.after(nm + "pmm", nm + "rpv", nm + "rv"))
            rv(lambda e: e.tensor_tensor(etmp[:], A[:], tot[:], ALU.mult),
               P.after(nm + "rto"))
            rv(lambda e: e.tensor_reduce(C(11), etmp[:], AX.X, ALU.add))
            rv(lambda e: e.tensor_tensor(Bv[:], Bv[:], tot[:], ALU.mult))
            rv(lambda e: e.tensor_reduce(C(12), Bv[:], AX.X, ALU.add))
            P.prod("vector",
                   (lambda e: e.tensor_tensor(prev[:], prev[:], self.pb[2][0:128, 0:8],
                                              ALU.add)),
                   nm + "rpv", 1, waits=P.after(nm + "pmm", nm + "rto"))
            rv(lambda e: e.tensor_scalar(C(13), C(11), float(CAP), None, ALU.is_lt))
            rv(lambda e: e.tensor_scalar(C(14), C(12), float(CAP), None, ALU.is_lt))
            rv(lambda e, tt=tt: e.tensor_tensor(self.l1f_w[:, tt * 4:tt * 4 + 1],
                                                C(13), C(8), ALU.mult),
               P.after(nm + "ia"))
            rv(lambda e, tt=tt: e.tensor_tensor(self.l1f_w[:, tt * 4 + 1:tt * 4 + 2],
                                                C(14), C(10), ALU.mult))
            rv(lambda e: e.tensor_scalar(C(15), C(11), float(CAP), BIGI,
                                         ALU.is_ge, ALU.mult))
            rv(lambda e, tt=tt: e.scalar_tensor_tensor(
                self.l1f_w[:, tt * 4 + 2:tt * 4 + 3], C(3), float(CAP), C(11),
                ALU.mult, ALU.add))
            rv(lambda e, tt=tt: e.tensor_tensor(
                self.l1f_w[:, tt * 4 + 2:tt * 4 + 3],
                self.l1f_w[:, tt * 4 + 2:tt * 4 + 3], C(15), ALU.add))
            rv(lambda e: e.tensor_scalar(C(15), C(12), float(CAP), BIGI,
                                         ALU.is_ge, ALU.mult))
            rv(lambda e, tt=tt: e.scalar_tensor_tensor(
                self.l1f_w[:, tt * 4 + 3:tt * 4 + 4], C(6), float(CAP), C(12),
                ALU.mult, ALU.add))
            rv(lambda e, tt=tt: e.tensor_tensor(
                self.l1f_w[:, tt * 4 + 3:tt * 4 + 4],
                self.l1f_w[:, tt * 4 + 3:tt * 4 + 4], C(15), ALU.add))
            for c in range(NCHUNK):
                bank = self.pb[3 + c % 2]
                P.prod("tensor",
                       (lambda e, c=c, tt=tt, bank=bank: e.transpose(
                           bank.ap().bitcast(BF16)[0:128, 0:128],
                           self.xb[:, c * T + tt * 128:c * T + tt * 128 + 128],
                           self.ident[:])),
                       nm + "xtp", 1,
                       waits=self.xbok() + self.initw() + P.after(nm + "xte"))
                P.prod("vector",
                       (lambda e, c=c, bank=bank: e.tensor_copy(
                           xtm[:, c * 128:(c + 1) * 128],
                           bank.ap().bitcast(BF16)[0:128, 0:128])),
                       nm + "xte", 1, waits=P.after(nm + "xtp", nm + "fxw"))
            P.prod("sync",
                   (lambda e, tt=tt: e.dma_start(
                       out=self.l1f_xtm[tt * 128:(tt + 1) * 128, :], in_=xtm[:])),
                   nm + "fxw", 16, waits=P.after(nm + "xte", nm + "fxg"))
        sl6 = self.l1f_sl6
        iot = self.scrf[:, 1024:1536]
        P.prod("sync",
               (lambda e: e.dma_start(out=iot, in_=self.l1f_ciot.ap())),
               nm + "ild", 16, waits=P.after("postln", nm + "ieq"))
        eqf = (self.scrf[:, 0:512], self.scrf[:, 512:1024])
        eqb = (self.scrb[:, 5632:6144], self.scrb[:, 6144:6656])
        segs = [self.scra.ap().bitcast(F32)[0:1, 5120 + q * 512:5120 + (q + 1) * 512]
                for q in range(6)]
        seg6 = self.scra.ap().bitcast(F32)[0:6, 5120:5632]
        rowst = self.scra.ap().bitcast(F32)[0:1, 5120:7680]
        ibank = self.pb[5]
        for rc in range(5):
            for ts2 in range(32):
                tt, sl = ts2 // 2, ts2 % 2
                db = ts2 % 2
                wia = P.after(nm + "imm") + P.after(nm + "ia")
                P.prod("vector",
                       (lambda e, tt=tt, db=db: e.tensor_copy(
                           sl6[:, db * 8:db * 8 + 2],
                           tidb[:, tt * 6:tt * 6 + 2])),
                       nm + "ia", 1, waits=wia + P.after(nm + "cld"))
                P.prod("vector",
                       (lambda e, tt=tt, sl=sl, db=db: e.tensor_copy(
                           sl6[:, db * 8 + 2:db * 8 + 4],
                           tidb[:, tt * 6 + 2 + 2 * sl:tt * 6 + 4 + 2 * sl])),
                       nm + "ia", 1, waits=P.after(nm + "ia"))
                P.prod("vector",
                       (lambda e, tt=tt, sl=sl, db=db: e.tensor_copy(
                           sl6[:, db * 8 + 4:db * 8 + 5],
                           self.l1f_w[:, tt * 4 + sl:tt * 4 + sl + 1])),
                       nm + "ia", 1, waits=P.after(nm + "ia", nm + "rv"))
                P.prod("vector",
                       (lambda e, rc=rc, tt=tt, sl=sl, db=db: e.tensor_scalar(
                           eqf[db][:], iot, float(rc * 512),
                           self.l1f_w[:, tt * 4 + 2 + sl:tt * 4 + 3 + sl],
                           ALU.add, ALU.is_equal)),
                       nm + "ieq", 1,
                       waits=P.after(nm + "imm", nm + "rv", nm + "ild"))
                P.prod("vector",
                       (lambda e, db=db: e.tensor_copy(eqb[db][:], eqf[db][:])),
                       nm + "ieb", 1, waits=P.after(nm + "ieq", nm + "fxw"))
                P.prod("tensor",
                       (lambda e, db=db, first=(ts2 == 0), last=(ts2 == 31): e.matmul(
                           ibank[0:6, 0:512], sl6[:, db * 8:db * 8 + 6], eqb[db][:],
                           start=first, stop=last)),
                       nm + "imm", 1,
                       waits=P.after(nm + "ieb", nm + "ia", nm + "iev"))
            P.prod("vector",
                   (lambda e: e.tensor_copy(seg6[:], ibank[0:6, 0:512])),
                   nm + "iev", 1, waits=P.after(nm + "imm", nm + "irw", nm + "icb"))
            for q in range(1, 6):
                P.prod("sync",
                       (lambda e, q=q: e.dma_start(
                           out=segs[q],
                           in_=self.scra.ap().bitcast(F32)[q:q + 1, 5120:5632])),
                       nm + "ish", 16, waits=P.after(nm + "iev", nm + "icb",
                                                     nm + "irw"))
            wsh = P.after(nm + "ish")
            P.prod("vector",
                   (lambda e: e.scalar_tensor_tensor(segs[0], segs[0], 32.0, segs[1],
                                                     ALU.mult, ALU.add)),
                   nm + "icb", 1, waits=wsh + P.after(nm + "iev"))
            P.prod("vector",
                   (lambda e: e.scalar_tensor_tensor(segs[2], segs[2], 32.0, segs[3],
                                                     ALU.mult, ALU.add)),
                   nm + "icb", 1, waits=P.after(nm + "icb"))
            P.prod("vector",
                   (lambda e: e.tensor_scalar(segs[1], segs[5], -BIGI, BIGI,
                                              ALU.mult, ALU.add)),
                   nm + "icb", 1, waits=P.after(nm + "icb"))
            P.prod("vector",
                   (lambda e: e.tensor_tensor(segs[2], segs[2], segs[1], ALU.add)),
                   nm + "icb", 1, waits=P.after(nm + "icb"))
            for j, q in ((0, 0), (1, 2), (2, 4)):
                P.prod("sync",
                       (lambda e, j=j, q=q, rc=rc: e.dma_start(
                           out=self.l1f_rows[j:j + 1, rc * 512:(rc + 1) * 512],
                           in_=segs[q])),
                       nm + "irw", 16, waits=P.after(nm + "icb"))
        xb7 = self.pb[6]
        for j in range(3):
            P.prod("sync",
                   (lambda e, j=j: e.dma_start(out=rowst,
                                               in_=self.l1f_rows[j:j + 1, :])),
                   nm + "irl", 16, waits=P.after(nm + "irw", nm + "ixm"))
            if j < 2:
                dst = self.l1f_invi if j == 0 else self.l1f_si
                for k in range(20):
                    P.prod("tensor",
                           (lambda e, k=k: e.matmul(
                               xb7[0:128, 0:1],
                               self.scra.ap().bitcast(F32)[0:1,
                                                           5120 + k * 128:5248 + k * 128],
                               self.onescolf[0:1, 0:1], start=True, stop=True)),
                           nm + "ixm", 1,
                           waits=P.after(nm + "irl", nm + "ixe") + self.initw())
                    P.prod("vector",
                           (lambda e, k=k, dst=dst: e.tensor_copy(
                               dst[:, k:k + 1], xb7[0:128, 0:1])),
                           nm + "ixe", 1, waits=P.after(nm + "ixm", nm + "fxg",
                                                        nm + "fys"))
            else:
                for ex in range(8):
                    for r in range(3):
                        nt = (128, 128, 64)[r]
                        off = ex * 320 + r * 128
                        P.prod("tensor",
                               (lambda e, off=off, nt=nt: e.matmul(
                                   xb7[0:nt, 0:1],
                                   self.scra.ap().bitcast(F32)[0:1,
                                                               5120 + off:5120 + off + nt],
                                   self.onescolf[0:1, 0:1], start=True, stop=True)),
                               nm + "ixm", 1,
                               waits=P.after(nm + "irl", nm + "ixe") + self.initw())
                        P.prod("vector",
                               (lambda e, ex=ex, r=r, nt=nt: e.tensor_copy(
                                   self.l1f_wc[0:nt, ex * 3 + r:ex * 3 + r + 1],
                                   xb7[0:nt, 0:1])),
                               nm + "ixe", 1, waits=P.after(nm + "ixm", nm + "ywf"))
        stgx = self.scrb[:, 5632:6656]
        for k in range(20):
            P.prod("gpsimd",
                   (lambda e, k=k: e.indirect_dma_start(
                       stgx[:], None, self.l1f_xtm,
                       IndirectOffsetOnAxis(self.l1f_invi[:, k:k + 1], 0))),
                   nm + "fxg", 16,
                   waits=P.after(nm + "ixe", nm + "fxw", nm + "fxs", nm + "imm"))
            P.prod("sync",
                   (lambda e, k=k: e.dma_start(out=buf[k * 128:(k + 1) * 128, :],
                                               in_=stgx[:])),
                   nm + "fxs", 16, waits=P.after(nm + "fxg", nm + "fbl"))
        xf = self.scrb[:, 0:2560]
        stg = self.scrb[:, 5632:6656]
        ytile = self.scrb[:, 6656:6976]
        NTOK = (128, 128, 64)
        for ex in range(8):
            for r in range(3):
                nt = NTOK[r]
                P.prod("sync",
                       (lambda e, ex=ex, r=r, nt=nt: e.dma_start(
                           out=stg[0:nt, :],
                           in_=buf[ex * CAP + r * 128:ex * CAP + r * 128 + nt, :])),
                       nm + "fbl", 16,
                       waits=P.after(nm + "fxs", nm + "fxt", nm + "imm"))
                for c in range(NCHUNK):
                    bank = self.pb[3 + c % 2]
                    P.prod("tensor",
                           (lambda e, c=c, nt=nt, bank=bank: e.transpose(
                               bank.ap().bitcast(BF16)[0:128, 0:nt],
                               stg[0:nt, c * 128:(c + 1) * 128],
                               self.ident[0:nt, 0:nt])),
                           nm + "fxt", 1,
                           waits=P.after(nm + "fbl", nm + "fxe") + self.initw())
                    P.prod("vector",
                           (lambda e, c=c, r=r, nt=nt, bank=bank: e.tensor_copy(
                               xf[0:128, c * 320 + r * 128:c * 320 + r * 128 + nt],
                               bank.ap().bitcast(BF16)[0:128, 0:nt])),
                           nm + "fxe", 1, waits=P.after(nm + "fxt", nm + "hmm"))
            for half in range(2):
                for ci in range(NCHUNK):
                    P.prod("sync",
                           (lambda e, ex=ex, half=half, ci=ci: e.dma_start(
                               out=self.wpool[:, ci * 2048:(ci + 1) * 2048],
                               in_=self.wc_l1f[1][ex * 1024 + ci * 128:
                                                  ex * 1024 + (ci + 1) * 128,
                                                  half * 2048:(half + 1) * 2048])),
                           nm + "wl", 16,
                           waits=P.after(key + "p", "postln", nm + "hmm", nm + "imm"))
                for hh in range(16):
                    bank = self.pb[hh % 4]
                    P.prod("tensor",
                           (lambda e, hh=hh, bank=bank: [
                               e.matmul(bank[0:128, 0:320],
                                        self.wpool[:, ci * 2048 + hh * 128:
                                                   ci * 2048 + (hh + 1) * 128],
                                        xf[:, ci * 320:(ci + 1) * 320],
                                        start=(ci == 0), stop=(ci == NCHUNK - 1))
                               for ci in range(NCHUNK)][-1]),
                           nm + "hmm", 1,
                           waits=P.after(nm + "wl", nm + "fxe")
                           + P.after_lag(nm + "hev!", 3))
                    P.prod("vector",
                           (lambda e, ex=ex, half=half, hh=hh, bank=bank: e.tensor_scalar(
                               self.scra[:, (half * 16 + hh) * 320:
                                         (half * 16 + hh + 1) * 320],
                               bank[0:128, 0:320],
                               bcols[:, ex * 32 + half * 16 + hh:
                                     ex * 32 + half * 16 + hh + 1],
                               0.0, ALU.add, ALU.max)),
                           nm + "hev!", 1, waits=P.after(nm + "hmm"))
            for q in range(4):
                for ci in range(NCHUNK):
                    P.prod("sync",
                           (lambda e, ex=ex, q=q, ci=ci: e.dma_start(
                               out=self.wpool2[:, ci * 1024:(ci + 1) * 1024],
                               in_=self.wc_l1f[2][ex * 4096 + q * 1024 + ci * 128:
                                                  ex * 4096 + q * 1024 + (ci + 1) * 128, :])),
                           nm + "wm", 16,
                           waits=P.after(key + "p", "postln", nm + "dmm"))
                for oc in range(NCHUNK):
                    P.prod("tensor",
                           (lambda e, q=q, oc=oc: [
                               e.matmul(self.pb[oc][0:128, 0:320],
                                        self.wpool2[:, hc * 1024 + oc * 128:
                                                    hc * 1024 + (oc + 1) * 128],
                                        self.scra[:, (q * 8 + hc) * 320:
                                                  (q * 8 + hc + 1) * 320],
                                        start=(q == 0 and hc == 0),
                                        stop=(q == 3 and hc == 7))
                               for hc in range(8)][-1]),
                           nm + "dmm", 1,
                           waits=P.after(nm + "wm", nm + "hev!", nm + "yev", nm + "fye"))
            for oc in range(NCHUNK):
                P.prod("vector",
                       (lambda e, ex=ex, oc=oc: e.tensor_scalar(
                           ytile[:], self.pb[oc][0:128, 0:320],
                           bcols[:, 256 + ex * 8 + oc:256 + ex * 8 + oc + 1],
                           None, ALU.add)),
                       nm + "yev", 1, waits=P.after(nm + "dmm", nm + "fyt"))
                for r in range(3):
                    nt = NTOK[r]
                    P.prod("tensor",
                           (lambda e, oc=oc, r=r, nt=nt: e.transpose(
                               self.pb[oc].ap().bitcast(BF16)[0:nt, 0:128],
                               ytile[0:128, r * 128:r * 128 + nt],
                               self.ident[:])),
                           nm + "fyt", 1,
                           waits=P.after(nm + "yev", nm + "fye") + self.initw())
                    P.prod("vector",
                           (lambda e, oc=oc, r=r, nt=nt: e.tensor_copy(
                               self.scrb[0:nt, 2560 + r * 1024 + oc * 128:
                                         2560 + r * 1024 + (oc + 1) * 128],
                               self.pb[oc].ap().bitcast(BF16)[0:nt, 0:128])),
                           nm + "fye", 1, waits=P.after(nm + "fyt", nm + "ywf",
                                                        nm + "fyw", nm + "fbz"))
            for r in range(3):
                nt = NTOK[r]
                P.prod("vector",
                       (lambda e, ex=ex, r=r, nt=nt: e.tensor_scalar(
                           self.scrb[0:nt, 2560 + r * 1024:2560 + (r + 1) * 1024],
                           self.scrb[0:nt, 2560 + r * 1024:2560 + (r + 1) * 1024],
                           self.l1f_wc[0:nt, ex * 3 + r:ex * 3 + r + 1],
                           None, ALU.mult)),
                       nm + "ywf", 1, waits=P.after(nm + "fye", nm + "ixe"))
                P.prod("sync",
                       (lambda e, ex=ex, r=r, nt=nt: e.dma_start(
                           out=ybuf[ex * CAP + r * 128:ex * CAP + r * 128 + nt, :],
                           in_=self.scrb[0:nt, 2560 + r * 1024:2560 + (r + 1) * 1024])),
                       nm + "fyw", 16, waits=P.after(nm + "ywf", nm + "fys"))
        for k in range(20):
            P.prod("sync",
                   (lambda e, k=k: e.dma_start(out=stgx[:],
                                               in_=ybuf[k * 128:(k + 1) * 128, :])),
                   nm + "fyl", 16,
                   waits=P.after(nm + "fyw", nm + "fys", nm + "fbl"))
            P.prod("gpsimd",
                   (lambda e, k=k: e.indirect_dma_start(
                       self.l1f_ytok, IndirectOffsetOnAxis(self.l1f_si[:, k:k + 1], 0),
                       stgx[:], None,
                       bounds_check=2 * T - 1, oob_is_err=False)),
                   nm + "fys", 16,
                   waits=P.after(nm + "fyl", nm + "fbz", nm + "ixe", nm + "fgd"))
        g1sb = self.scrb[:, 0:1024]
        g2sb = self.scrb[:, 1024:2048]
        cb = self.scrb[:, 2048:3072]
        y2v = self.l1f_ytok.rearrange("(t s) c -> t s c", s=2)
        for tt in range(16):
            P.prod("sync",
                   (lambda e, tt=tt: e.dma_start(
                       out=g1sb[:], in_=y2v[tt * 128:(tt + 1) * 128, 0, :])),
                   nm + "fgd", 16, waits=P.after(nm + "fys", nm + "hmm", nm + "cm"))
            P.prod("sync",
                   (lambda e, tt=tt: e.dma_start(
                       out=g2sb[:], in_=y2v[tt * 128:(tt + 1) * 128, 1, :])),
                   nm + "fgd", 16, waits=P.after(nm + "fys", nm + "hmm", nm + "cm"))
            P.prod("vector",
                   (lambda e: e.tensor_tensor(cb[:], g1sb[:], g2sb[:], ALU.add)),
                   nm + "cm", 1, waits=P.after(nm + "fgd", nm + "cmt", nm + "fyw"))
            for c in range(NCHUNK):
                bank = self.pb[3 + c % 2]
                P.prod("tensor",
                       (lambda e, c=c, bank=bank: e.transpose(
                           bank.ap().bitcast(BF16)[0:128, 0:128],
                           cb[:, c * 128:(c + 1) * 128], self.ident[:])),
                       nm + "cmt", 1,
                       waits=P.after(nm + "cm", nm + "cme") + self.initw())
                P.prod("vector",
                       (lambda e, c=c, tt=tt, bank=bank: e.tensor_tensor(
                           self.xs[:, c * T + tt * 128:c * T + tt * 128 + 128],
                           self.xs[:, c * T + tt * 128:c * T + tt * 128 + 128],
                           bank.ap().bitcast(BF16)[0:128, 0:128], ALU.add)),
                       nm + "cme", 1, waits=P.after(nm + "cmt"))
        return self.emit_postln(P.after(nm + "cme"))

    def emit_l1a(self, b):
        nc, P = self.nc, self.P
        wqkv = self.win_decl("L1_attn_w_qkv", [3072, 1024])
        wout = self.win_decl("L1_attn_w_out", [1024, 1024])
        bout = self.win_decl("L1_attn_b_out", [1024])
        key = "l1a"
        if key not in self.once:
            self.once.add(key)
            P.extra_waits = P.after("postln") + self.xbok()
            self.wc_l1a = (self.dram("wcQKV", [1024, 3072]), self.dram("wcWo1", [1024, 1024]))
            t1_ = self.prep_transpose_cast(wqkv, 3072, 1024, key + "p")
            t2_ = self.prep_transpose_cast(wout, 1024, 1024, key + "p")
            self.prep_transpose(wqkv, self.wc_l1a[0], 3072, 1024, key + "p", tmp=t1_)
            self.prep_transpose(wout, self.wc_l1a[1], 1024, 1024, key + "p", tmp=t2_)
            self.l1a_cols = self.nc.alloc_sbuf_tensor("l1a_cols", [128, 8], F32)
            self.load_vec_cols(bout, 1024, key + "b", dst=self.l1a_cols, dst_off=0)
            P.extra_waits = []
        if not hasattr(self, "attn_dram"):
            self.attn_dram = self.dram("attn_dram", [1024, T], BF16)
        wcQKV, wcWo = self.wc_l1a
        nm = self.nid("l1a_")
        wprep = P.after(key + "p", key + "b", "postln")

        qf = self.scra[:, 0:2048]
        kf = self.scra[:, 2048:4096]
        vf = self.scra[:, 4096:6144]
        vtm = self.scra[:, 6144:8224]
        et = self.scra[:, 8224:9248]
        qkp = self.scra[:, 9248:11296]
        keepb = self.scra[0:1, 11296:13344]
        wpoolF = self.wpool.ap().bitcast(F32)
        scrbF = self.scrb.ap().bitcast(F32)
        Drow = wpoolF[0:1, 4096:6144]
        diagrow = wpoolF[0:1, 6144:8192]
        keepf = scrbF[0:1, 1536:3584]
        kbias = self.sb("l1a_kbias", [128, 16], F32)
        g11 = self.scrf[0:1, 512:513]
        tq = self.scrf[0:1, 513:514]
        tk = self.scrf[0:1, 514:515]
        gcol = self.scrf[:, 515:516]
        nbg = self.scrf[:, 516:517]
        r512 = self.scrf[0:1, 517:1029]
        m11 = self.scrf[0:1, 1030:1031]
        s11 = self.scrf[0:1, 1031:1032]
        SC = 1.0 / 8.0
        BIG = 30000.0

        for hp in range(8):
            wv = wprep + P.after(nm + "pvm", nm + "expd", nm + "pmm")
            for woff in (0, 1024, 2048):
                self.load_w(wcQKV, self.wpool, woff, 1024, woff + hp * 128, woff + hp * 128 + 128,
                            nm + "wl", wv)
            for j, dst in enumerate((qf, kf, vf)):
                woff = j * 1024
                for ts in range(NTT):
                    bank = self.pb[ts % 2]
                    P.prod("tensor",
                           (lambda e, dst=dst, woff=woff, ts=ts, bank=bank: [
                               e.matmul(bank[0:128, 0:512],
                                        self.wpool[:, woff + ci * 128: woff + (ci + 1) * 128],
                                        self.xb[:, ci * T + ts * 512:ci * T + (ts + 1) * 512],
                                        start=(ci == 0), stop=(ci == NCHUNK - 1))
                               for ci in range(NCHUNK)][-1]),
                           nm + "pmm", 1,
                           waits=P.after(nm + "wl", nm + "pev", "postln") + self.xbok())
                    P.prod("vector",
                           (lambda e, dst=dst, ts=ts, bank=bank: e.tensor_copy(
                               dst[:, ts * 512:(ts + 1) * 512], bank[0:128, 0:512])),
                           nm + "pev", 1, waits=P.after(nm + "pmm", nm + "qk", nm + "gsq"))
            P.prod("vector", lambda e: e.nop(), nm + "pq", 1, waits=P.after(nm + "pev"))
            P.prod("vector", lambda e: e.memset(vtm[:], 1.0), nm + "vms", 1,
                   waits=P.after(nm + "pvm"))
            for h in range(2):
                for kt in range(16):
                    pv = self.pb[2 + kt % 2].ap().bitcast(BF16)
                    P.prod("tensor",
                           (lambda e, h=h, kt=kt, pv=pv: e.transpose(
                               pv[0:128, 0:64],
                               vf[h * 64:(h + 1) * 64, kt * 128:(kt + 1) * 128],
                               self.ident[h * 64:h * 64 + 64, h * 64:h * 64 + 64])),
                           nm + "vtr", 1,
                           waits=P.after(nm + "pq", nm + "vev") + self.initw())
                    P.prod("vector",
                           (lambda e, h=h, kt=kt, pv=pv: e.tensor_copy(
                               vtm[:, h * 1040 + kt * 65: h * 1040 + kt * 65 + 64], pv[0:128, 0:64])),
                           nm + "vev", 1, waits=P.after(nm + "vtr", nm + "vms"))
            for j, (src, dst11) in enumerate(((qf, tq), (kf, tk))):
                for ts in range(NTT):
                    P.prod("scalar",
                           (lambda e, src=src, ts=ts: e.activation(
                               self.scrb[:, 0:512], src[:, ts * 512:(ts + 1) * 512], AF.Square)),
                           nm + "gsq", 1, waits=P.after(nm + "pq", nm + "gmm", nm + "ost", "postln"))
                    P.prod("tensor",
                           (lambda e, ts=ts: e.matmul(
                               self.pb[4][0:1, 0:512], self.onescol[:, 0:1], self.scrb[:, 0:512],
                               start=True, stop=True)),
                           nm + "gmm", 1, waits=P.after(nm + "gsq", nm + "gcp") + self.initw())
                    P.prod("vector",
                           (lambda e, ts=ts: e.tensor_copy(Drow[:, ts * 512:(ts + 1) * 512],
                                                           self.pb[4][0:1, 0:512])),
                           nm + "gcp", 1, waits=P.after(nm + "gmm", nm + "grm"))
                P.prod("vector",
                       (lambda e: e.reduce_max(g11[:], Drow[:], axis=AX.X)),
                       nm + "grm", 1, waits=P.after(nm + "gcp"))
                P.prod("scalar",
                       (lambda e, dst11=dst11: e.activation(dst11, g11[:], AF.Sqrt)),
                       nm + "gsr", 1, waits=P.after(nm + "grm"))
            P.prod("vector",
                   (lambda e: e.tensor_mul(g11[:], tq[:], tk[:])),
                   nm + "g1", 1, waits=P.after(nm + "gsr"))
            P.prod("vector",
                   (lambda e: e.tensor_scalar(g11[:], g11[:], -SC, None, ALU.mult)),
                   nm + "g2", 1, waits=P.after(nm + "g1"))
            P.prod("tensor",
                   (lambda e: e.matmul(self.pb[4][0:128, 0:1], self.onesrow[0:1, 0:128],
                                       g11[:], start=True, stop=True)),
                   nm + "g3", 1, waits=P.after(nm + "g2", nm + "gcp")),
            P.prod("vector",
                   (lambda e: e.tensor_copy(gcol[:], self.pb[4][0:128, 0:1])),
                   nm + "g4", 1, waits=P.after(nm + "g3"))
            P.prod("vector",
                   (lambda e: e.tensor_scalar(nbg[:], gcol[:], -BIG, None, ALU.add)),
                   nm + "g5", 1, waits=P.after(nm + "g4"))

            for h in range(2):
                hb = h * 64
                hglob = hp * 2 + h
                for ts in range(NTT):
                    P.prod("vector",
                           (lambda e, ts=ts, hb=hb: e.tensor_mul(
                               qkp[hb:hb + 64, ts * 512:(ts + 1) * 512],
                               qf[hb:hb + 64, ts * 512:(ts + 1) * 512],
                               kf[hb:hb + 64, ts * 512:(ts + 1) * 512])),
                           nm + "qk", 1, waits=P.after(nm + "pq", nm + "dgm"))
                    P.prod("tensor",
                           (lambda e, ts=ts, hb=hb: e.matmul(
                               self.pb[4][0:1, 0:512],
                               self.onescol[hb:hb + 64, 0:1],
                               qkp[hb:hb + 64, ts * 512:(ts + 1) * 512],
                               start=True, stop=True)),
                           nm + "dgm", 1, waits=P.after(nm + "qk", nm + "dgc", nm + "g4", nm + "k13")),
                    P.prod("vector",
                           (lambda e, ts=ts: e.tensor_copy(diagrow[:, ts * 512:(ts + 1) * 512],
                                                           self.pb[4][0:1, 0:512])),
                           nm + "dgc", 1, waits=P.after(nm + "dgm"))
                for qt in range(NTT):
                    for kt in range(16):
                        bank = self.pb[kt % 2]
                        P.prod("tensor",
                               (lambda e, kt=kt, qt=qt, hb=hb, bank=bank: e.matmul(
                                   bank[0:128, 0:512],
                                   kf[hb:hb + 64, kt * 128:(kt + 1) * 128],
                                   qf[hb:hb + 64, qt * 512:(qt + 1) * 512],
                                   start=True, stop=True)),
                               nm + "smm", 1,
                               waits=P.after(nm + "expd", nm + "g4") + P.after_lag(nm + "lag!", 2)),
                        esl = et[:, (kt % 2) * 512:(kt % 2) * 512 + 512]
                        P.prod("scalar",
                               (lambda e, bank=bank, esl=esl: e.activation(
                                   esl, bank[0:128, 0:512], AF.Exp, bias=gcol[:], scale=SC)),
                               nm + "expd", 1, waits=P.after(nm + "smm", nm + "dmm")),
                        P.prod("tensor",
                               (lambda e, kt=kt, esl=esl: e.matmul(
                                   self.pb[5][0:1, 0:512], self.onescol[:, 0:1], esl,
                                   start=(kt == 0), stop=(kt == 15))),
                               nm + "dmm", 1, waits=P.after(nm + "expd", nm + "dcp"))
                        P.prod("tensor", lambda e: e.nop(), nm + "lag!", 1,
                               waits=P.after(nm + "dmm"))
                    P.prod("vector",
                           (lambda e, qt=qt: e.tensor_copy(Drow[:, qt * 512:(qt + 1) * 512],
                                                           self.pb[5][0:1, 0:512])),
                           nm + "dcp", 1, waits=P.after(nm + "dmm"))
                P.prod("vector",
                       (lambda e: e.reciprocal(Drow[:], Drow[:])),
                       nm + "k1", 1, waits=P.after(nm + "dcp"))
                P.prod("scalar",
                       (lambda e: e.activation(diagrow[:], diagrow[:], AF.Exp,
                                               bias=g11[:], scale=SC)),
                       nm + "k2", 1, waits=P.after(nm + "dgc", nm + "g2", nm + "kbc"))
                P.prod("vector",
                       (lambda e: e.tensor_mul(diagrow[:], diagrow[:], Drow[:])),
                       nm + "k3", 1, waits=P.after(nm + "k1", nm + "k2"))
                P.prod("vector",
                       (lambda e: e.reduce_sum(m11[:], diagrow[:], axis=AX.X)),
                       nm + "k4", 1, waits=P.after(nm + "k3"))
                P.prod("vector",
                       (lambda e: e.tensor_scalar(m11[:], m11[:], 1.0 / T, None, ALU.mult)),
                       nm + "k5", 1, waits=P.after(nm + "k4"))
                P.prod("vector",
                       (lambda e: e.tensor_scalar(diagrow[:], diagrow[:], m11[:], None, ALU.subtract)),
                       nm + "k6", 1, waits=P.after(nm + "k5"))
                P.prod("scalar",
                       (lambda e: e.activation(keepf[:], diagrow[:], AF.Square)),
                       nm + "k7", 1, waits=P.after(nm + "k6", nm + "kbc", nm + "od3"))
                P.prod("vector",
                       (lambda e: e.reduce_sum(s11[:], keepf[:], axis=AX.X)),
                       nm + "k8", 1, waits=P.after(nm + "k7"))
                P.prod("scalar",
                       (lambda e: e.activation(s11[:], s11[:], AF.Sqrt, scale=1.0 / (T - 1))),
                       nm + "k9", 1, waits=P.after(nm + "k8"))
                P.prod("vector",
                       (lambda e: e.tensor_scalar(s11[:], s11[:], 1.5, None, ALU.mult)),
                       nm + "k10", 1, waits=P.after(nm + "k9"))
                P.prod("scalar",
                       (lambda e: e.activation(diagrow[:], diagrow[:], AF.Abs)),
                       nm + "k11", 1, waits=P.after(nm + "k6", nm + "k8"))
                P.prod("vector",
                       (lambda e: e.tensor_scalar(keepf[:], diagrow[:], s11[:], None, ALU.is_le)),
                       nm + "k12", 1, waits=P.after(nm + "k10", nm + "k11"))
                P.prod("vector",
                       (lambda e: e.tensor_copy(keepb[:], keepf[:])),
                       nm + "k13", 1, waits=P.after(nm + "k12", nm + "ktr"))
                for kt in range(16):
                    pv = self.pb[2 + kt % 2].ap().bitcast(BF16)
                    P.prod("tensor",
                           (lambda e, kt=kt, pv=pv: e.transpose(
                               pv[0:128, 64:65], keepb[0:1, kt * 128:(kt + 1) * 128],
                               self.ident[0:1, 0:1])),
                           nm + "ktr", 1, waits=P.after(nm + "k13", nm + "kbb") + self.initw())
                    P.prod("vector",
                           (lambda e, kt=kt, pv=pv: e.tensor_scalar(
                               kbias[:, kt:kt + 1], pv[0:128, 64:65], BIG, None, ALU.mult)),
                           nm + "kbb", 1, waits=P.after(nm + "ktr", nm + "exp2"))
                    P.prod("vector",
                           (lambda e, kt=kt: e.tensor_tensor(
                               kbias[:, kt:kt + 1], kbias[:, kt:kt + 1], nbg[:], ALU.add)),
                           nm + "kbc", 1, waits=P.after(nm + "kbb", nm + "g5"))
                for qt in range(NTT):
                    for kt in range(16):
                        bank = self.pb[kt % 2]
                        P.prod("tensor",
                               (lambda e, kt=kt, qt=qt, hb=hb, bank=bank: e.matmul(
                                   bank[0:128, 0:512],
                                   kf[hb:hb + 64, kt * 128:(kt + 1) * 128],
                                   qf[hb:hb + 64, qt * 512:(qt + 1) * 512],
                                   start=True, stop=True)),
                               nm + "sm2", 1,
                               waits=P.after(nm + "exp2", nm + "kbc") + P.after_lag(nm + "lagb!", 2)),
                        esl = et[:, (kt % 2) * 512:(kt % 2) * 512 + 512]
                        P.prod("scalar",
                               (lambda e, kt=kt, bank=bank, esl=esl: e.activation(
                                   esl, bank[0:128, 0:512], AF.Exp,
                                   bias=kbias[:, kt:kt + 1], scale=SC)),
                               nm + "exp2", 1, waits=P.after(nm + "sm2", nm + "pvm")),
                        P.prod("tensor",
                               (lambda e, kt=kt, h=h, esl=esl: e.matmul(
                                   self.pb[6][0:65, 0:512],
                                   vtm[:, h * 1040 + kt * 65: h * 1040 + (kt + 1) * 65], esl,
                                   start=(kt == 0), stop=(kt == 15))),
                               nm + "pvm", 1, waits=P.after(nm + "exp2", nm + "oev", nm + "orc"))
                        P.prod("tensor", lambda e: e.nop(), nm + "lagb!", 1,
                               waits=P.after(nm + "pvm"))
                    P.prod("vector",
                           (lambda e: e.tensor_scalar(r512[:], self.pb[6][64:65, 0:512],
                                                      1e-30, None, ALU.add)),
                           nm + "od", 1, waits=P.after(nm + "pvm"))
                    P.prod("vector",
                           (lambda e: e.reciprocal(r512[:], r512[:])),
                           nm + "od2", 1, waits=P.after(nm + "od"))
                    P.prod("vector",
                           (lambda e, qt=qt: e.tensor_mul(r512[:], r512[:],
                                                          keepf[:, qt * 512:(qt + 1) * 512])),
                           nm + "od3", 1, waits=P.after(nm + "od2"))
                    P.prod("tensor",
                           (lambda e: e.matmul(self.pb[7][0:64, 0:512],
                                               self.onesrow[0:1, 0:64], r512[:],
                                               start=True, stop=True)),
                           nm + "obc", 1, waits=P.after(nm + "od3", nm + "omu")),
                    P.prod("vector",
                           (lambda e: e.tensor_copy(self.scrb[0:64, 512:1024],
                                                    self.pb[6][0:64, 0:512])),
                           nm + "oev", 1, waits=P.after(nm + "od", nm + "ost"))
                    P.prod("vector",
                           (lambda e: e.tensor_mul(self.scrb[0:64, 0:512],
                                                   self.scrb[0:64, 512:1024],
                                                   self.pb[7][0:64, 0:512])),
                           nm + "omu", 1, waits=P.after(nm + "obc", nm + "oev"))
                    P.prod("sync",
                           (lambda e, hglob=hglob, qt=qt: e.dma_start(
                               out=self.attn_dram[hglob * 64:(hglob + 1) * 64,
                                                  qt * 512:(qt + 1) * 512],
                               in_=self.scrb[0:64, 0:512])),
                           nm + "ost", 16, waits=P.after(nm + "omu"))
        self.load_w(wcWo, self.wpool2, 0, 1024, 0, 1024, nm + "wo",
                    wprep + P.after(nm + "pmm"))
        atile = self.scrb[:, 1024 + 0:1024 + 512]
        for ts in range(NTT):
            for ci in range(NCHUNK):
                P.prod("sync",
                       (lambda e, ci=ci, ts=ts: e.dma_start(
                           out=self.scrb[:, 1024 + ci * 512: 1024 + (ci + 1) * 512],
                           in_=self.attn_dram[ci * 128:(ci + 1) * 128, ts * 512:(ts + 1) * 512])),
                       nm + "ald", 16, waits=P.after(nm + "ost", nm + "omm"))
            for oc in range(NCHUNK):
                bank = self.pb[oc % 4]
                P.prod("tensor",
                       (lambda e, oc=oc, ts=ts, bank=bank: [
                           e.matmul(bank[0:128, 0:512],
                                    self.wpool2[:, ci * 1024 + oc * 128: ci * 1024 + (oc + 1) * 128],
                                    self.scrb[:, 1024 + ci * 512: 1024 + (ci + 1) * 512],
                                    start=(ci == 0), stop=(ci == NCHUNK - 1))
                           for ci in range(NCHUNK)][-1]),
                       nm + "omm", 1,
                       waits=P.after(nm + "ald", nm + "wo") + P.after_lag(nm + "oadd!", 3)),
                P.prod("vector",
                       (lambda e, oc=oc, ts=ts, bank=bank: e.scalar_tensor_tensor(
                           self.xs[:, oc * T + ts * 512: oc * T + (ts + 1) * 512],
                           bank[0:128, 0:512], self.l1a_cols[:, oc:oc + 1],
                           self.xs[:, oc * T + ts * 512: oc * T + (ts + 1) * 512],
                           ALU.add, ALU.add)),
                       nm + "oadd!", 1, waits=P.after(nm + "omm"))
        return self.emit_postln(P.after(nm + "oadd!"))



    def emit_l0a(self, b):
        nc, P = self.nc, self.P
        Wk = self.win_decl("L0_attn_Wk", [1024, 1024])
        bk = self.win_decl("L0_attn_bk", [1024])
        Wq = self.win_decl("L0_attn_Wq", [1024, 1024])
        bq = self.win_decl("L0_attn_bq", [1024])
        Wv = self.win_decl("L0_attn_Wv", [1024, 1024])
        bv = self.win_decl("L0_attn_bv", [1024])
        Wbeta = self.win_decl("L0_attn_Wbeta", [16, 1024])
        bbeta = self.win_decl("L0_attn_bbeta", [16])
        convk = self.win_decl("L0_attn_conv_k", [1024, 4])
        convq = self.win_decl("L0_attn_conv_q", [1024, 4])
        convv = self.win_decl("L0_attn_conv_v", [1024, 4])
        rmsw = self.win_decl("L0_attn_rms_w", [64])
        Wout = self.win_decl("L0_attn_Wout", [1024, 1024])
        bout = self.win_decl("L0_attn_bout", [1024])
        key = "l0a"
        NCH = 32
        if key not in self.once:
            self.once.add(key)
            P.extra_waits = P.after("postln") + self.xbok()
            self.wc_l0a = {n: self.dram("wcA" + n, [1024, 1024])
                           for n in ("k", "q", "v", "o")}
            tmps = {}
            for n, w in (("k", Wk), ("q", Wq), ("v", Wv), ("o", Wout)):
                tmps[n] = self.prep_transpose_cast(w, 1024, 1024, key + "p")
            for n, w in (("k", Wk), ("q", Wq), ("v", Wv), ("o", Wout)):
                self.prep_transpose(w, self.wc_l0a[n], 1024, 1024, key + "p",
                                    tmp=tmps[n])
            self.l0a_wb = nc.alloc_sbuf_tensor("l0a_wb", [128, 128], BF16)
            wbst = self.scrb.ap().bitcast(F32)[0:16, 2048:3072]
            wbb = self.scrb[0:16, 6144:7168]
            P.prod("sync",
                   (lambda e: e.dma_start(out=wbst, in_=Wbeta)),
                   key + "wbl", 16, waits=P.after(key + "pfs", key + "pfv"))
            P.prod("vector",
                   (lambda e: e.tensor_copy(wbb[:], wbst)),
                   key + "wbc", 1, waits=P.after(key + "wbl"))
            for c8 in range(8):
                P.prod("tensor",
                       (lambda e, c8=c8: e.transpose(
                           self.pb[3].ap().bitcast(BF16)[0:128, 0:16],
                           wbb[0:16, c8 * 128:(c8 + 1) * 128],
                           self.ident[0:16, 0:16])),
                       key + "wbt", 1,
                       waits=P.after(key + "wbc", key + "wbe") + self.initw())
                P.prod("vector",
                       (lambda e, c8=c8: e.tensor_copy(
                           self.l0a_wb[:, c8 * 16:(c8 + 1) * 16],
                           self.pb[3].ap().bitcast(BF16)[0:128, 0:16])),
                       key + "wbe", 1, waits=P.after(key + "wbt"))
            P.prod("vector", lambda e: e.nop(), key + "wb", 1,
                   waits=P.after(key + "wbe"))
            self.l0a_rw = nc.alloc_sbuf_tensor("l0a_rw", [128, 1], F32)
            for hp in range(2):
                P.prod("sync",
                       (lambda e, hp=hp: e.dma_start(
                           out=self.l0a_rw[hp * 64:(hp + 1) * 64, :],
                           in_=rmsw.rearrange("(d o) -> d o", o=1))),
                       key + "rw", 16)
            fstg = self.scrb[:, 0:2048]
            for c in range(8):
                P.prod("sync",
                       (lambda e, c=c: e.dma_start(
                           out=fstg[:, 0:1024],
                           in_=self.wc_l0a["o"][c * 128:(c + 1) * 128, :])),
                       key + "pfl", 16,
                       waits=P.after(key + "p", key + "pfs"))
                P.prod("vector",
                       (lambda e: e.tensor_scalar(fstg[:, 1024:2048], fstg[:, 0:1024],
                                                  self.l0a_rw[:], None, ALU.mult)),
                       key + "pfv", 1, waits=P.after(key + "pfl", key + "rw",
                                                     key + "pfs"))
                P.prod("sync",
                       (lambda e, c=c: e.dma_start(
                           out=self.wc_l0a["o"][c * 128:(c + 1) * 128, :],
                           in_=fstg[:, 1024:2048])),
                       key + "pfs", 16, waits=P.after(key + "pfv"))
            P.prod("sync", lambda e: e.nop(), key + "pf", 1, waits=P.after(key + "pfs"))
            self.l0a_cols = nc.alloc_sbuf_tensor("l0a_cols", [128, 136], F32)
            for j3, cv in enumerate((convk, convq, convv)):
                P.prod("sync",
                       (lambda e, j3=j3, cv=cv: e.dma_start(
                           out=self.l0a_cols[:, j3 * 32:(j3 + 1) * 32],
                           in_=cv.rearrange("(c p) j -> p c j", p=128))),
                       key + "b", 16)
            self.load_vec_cols(bk, 1024, key + "b", dst=self.l0a_cols, dst_off=96)
            self.load_vec_cols(bq, 1024, key + "b", dst=self.l0a_cols, dst_off=104)
            self.load_vec_cols(bv, 1024, key + "b", dst=self.l0a_cols, dst_off=112)
            self.load_vec_cols(bout, 1024, key + "b", dst=self.l0a_cols, dst_off=120)
            self.l0a_bb = nc.alloc_sbuf_tensor("l0a_bb", [16, 1], F32)
            P.prod("sync",
                   (lambda e: e.dma_start(out=self.l0a_bb[:],
                                          in_=bbeta.rearrange("(h o) -> h o", o=1))),
                   key + "b", 16)
            Cf = 64.0
            cm = np.zeros((NCH * 128, 256), np.float32)
            wi = np.zeros((128, NCH), np.float32)
            for i in range(NCH):
                tsv = float(i * 64)
                idx = np.arange(1, 65, dtype=np.float64) + tsv
                cs = np.cumsum(idx)
                weighted = idx + idx.sum() - cs
                w_i = weighted / ((1 + tsv + Cf) * (tsv + Cf) / 2)
                tri0 = (1 + tsv) * tsv / 2
                a_ = tri0 / (tri0 + cs)
                c_ = 1.0 - a_
                r = cs[:, None]
                jx = (cs - idx)[None, :]
                tm = np.where(r >= jx, (r - jx) / (tri0 + r), 0.0)
                tmt = tm.T
                blk = np.zeros((128, 256), np.float64)
                blk[0:64, 0:64] = tmt
                blk[64:128, 64:128] = tmt
                blk[:, 128:192] = np.tile(a_[None, :], (128, 1))
                blk[:, 192:256] = np.tile(c_[None, :], (128, 1))
                cm[i * 128:(i + 1) * 128, :] = blk
                wi[:, i] = np.tile(w_i, 2)
            self.l0a_cmask = nc.inline_tensor(cm.astype(BF), "c_l0a_cm")
            c_wi = nc.inline_tensor(np.ascontiguousarray(wi.astype(np.float32)),
                                    "c_l0a_wi")
            c_mska = np.zeros((128, 128), np.float32)
            su = -np.triu(np.ones((64, 64)), 1)
            c_mska[0:64, 0:64] = su
            c_mska[64:128, 64:128] = su
            c_mska_t = nc.inline_tensor(c_mska.astype(BF), "c_l0a_mska")
            self.l0a_cwi = c_wi
            self.l0a_cmska = c_mska_t
            c_ind = np.zeros((128, 2), np.float32)
            c_ind[0:64, 0] = 1.0
            c_ind[64:128, 1] = 1.0
            c_ind_t = nc.inline_tensor(c_ind.astype(BF), "c_l0a_ind")
            self.l0a_ind = nc.alloc_sbuf_tensor("l0a_ind", [128, 2], BF16)
            P.prod("sync", lambda e: e.dma_start(out=self.l0a_ind[:], in_=c_ind_t.ap()),
                   key + "c", 16)
            self.l0a_kd = self.dram("l0a_kd", [1024, T])
            self.l0a_qd = self.dram("l0a_qd", [1024, T])
            self.l0a_vd = self.dram("l0a_vd", [1024, T])
            self.l0a_ed = self.dram("l0a_ed", [16, T], F32)
            P.extra_waits = []
        nm = self.nid("l0a_")
        cols = self.l0a_cols
        wprep = P.after(key + "p", key + "b", key + "c", key + "pf", key + "wb")

        rawk = self.scrb[:, 0:2052]
        rawq = self.scrb[:, 2052:4104]
        rawv = self.scrb[:, 4104:6156]
        raws = (rawk, rawq, rawv)
        knb = self.scrb[:, 6156:6668]
        cf = self.scrf[:, 1024:1536]
        sqb = self.scrb[:, 6668:7180]
        e16 = self.scra.ap().bitcast(F32)[0:16, 0:512]
        krow = (self.scrf[0:1, 0:512], self.scrf[0:1, 512:1024])
        for ts in range(NTT):
            P.prod("tensor",
                   (lambda e, ts=ts: [
                       e.matmul(self.pb[3][0:16, 0:512],
                                self.l0a_wb[:, ci * 16:(ci + 1) * 16],
                                self.xb[:, ci * T + ts * 512:ci * T + (ts + 1) * 512],
                                start=(ci == 0), stop=(ci == NCHUNK - 1))
                       for ci in range(NCHUNK)][-1]),
                   nm + "abm", 1,
                   waits=wprep + P.after(nm + "abe", "postln") + self.xbok())
            P.prod("scalar",
                   (lambda e: e.activation(e16[:], self.pb[3][0:16, 0:512],
                                           AF.Sigmoid, bias=self.l0a_bb[:])),
                   nm + "abe", 1, waits=P.after(nm + "abm", nm + "aes") + wprep)
            P.prod("sync",
                   (lambda e, ts=ts: e.dma_start(
                       out=self.l0a_ed[:, ts * 512:(ts + 1) * 512], in_=e16[:])),
                   nm + "aes", 16, waits=P.after(nm + "abe", nm + "bec"))
        for c in range(8):
            wld = wprep + P.after(nm + "apm", "postln")
            for jn, wn in enumerate(("k", "q", "v")):
                for ci in range(NCHUNK):
                    P.prod("sync",
                           (lambda e, jn=jn, wn=wn, c=c, ci=ci: e.dma_start(
                               out=self.wpool[:, jn * 1024 + ci * 128:
                                              jn * 1024 + (ci + 1) * 128],
                               in_=self.wc_l0a[wn][ci * 128:(ci + 1) * 128,
                                                   c * 128:(c + 1) * 128])),
                           nm + "awl", 16, waits=wld)
            for j3 in range(3):
                P.prod("vector",
                       (lambda e, j3=j3: e.memset(raws[j3][:, 0:3], 0.0)),
                       nm + "arz", 1,
                       waits=P.after("postln", nm + "acv", key + "pf"))
            for ts in range(NTT):
                for j3 in range(3):
                    bank = self.pb[j3]
                    P.prod("tensor",
                           (lambda e, j3=j3, ts=ts, bank=bank: [
                               e.matmul(bank[0:128, 0:512],
                                        self.wpool[:, j3 * 1024 + ci * 128:
                                                   j3 * 1024 + (ci + 1) * 128],
                                        self.xb[:, ci * T + ts * 512:
                                                ci * T + (ts + 1) * 512],
                                        start=(ci == 0), stop=(ci == NCHUNK - 1))
                               for ci in range(NCHUNK)][-1]),
                           nm + "apm", 1,
                           waits=P.after(nm + "awl", nm + "arv", "postln")
                           + self.xbok())
                    P.prod("vector",
                           (lambda e, j3=j3, ts=ts, c=c, bank=bank: e.tensor_scalar(
                               raws[j3][:, 3 + ts * 512:3 + (ts + 1) * 512],
                               bank[0:128, 0:512],
                               cols[:, 96 + j3 * 8 + c:96 + j3 * 8 + c + 1],
                               None, ALU.add)),
                           nm + "arv", 1,
                           waits=P.after(nm + "apm", nm + "arz", nm + "acv"))
            for ts in range(NTT):
                for j3, dn in ((0, self.l0a_kd), (1, self.l0a_qd), (2, self.l0a_vd)):
                    P.prod("vector",
                           (lambda e, j3=j3, ts=ts, c=c: e.tensor_scalar(
                               cf[:], raws[j3][:, ts * 512:ts * 512 + 512],
                               cols[:, j3 * 32 + c * 4:j3 * 32 + c * 4 + 1],
                               None, ALU.mult)),
                           nm + "acv", 1,
                           waits=P.after(nm + "arv", nm + "akn", nm + "asq"))
                    for j in range(1, 4):
                        P.prod("vector",
                               (lambda e, j3=j3, ts=ts, j=j, c=c: e.scalar_tensor_tensor(
                                   cf[:], raws[j3][:, ts * 512 + j:ts * 512 + j + 512],
                                   cols[:, j3 * 32 + c * 4 + j:j3 * 32 + c * 4 + j + 1],
                                   cf[:], ALU.mult, ALU.add)),
                               nm + "acv", 1, waits=P.after(nm + "acv"))
                    if j3 < 2:
                        P.prod("scalar",
                               (lambda e: e.activation(sqb[:], cf[:], AF.Square)),
                               nm + "asq", 1, waits=P.after(nm + "acv", nm + "anm")),
                        for h in range(2):
                            P.prod("tensor",
                                   (lambda e, h=h: e.matmul(
                                       self.pb[4][0:1, 0:512],
                                       self.l0a_ind[:, h:h + 1], sqb[:],
                                       start=True, stop=True)),
                                   nm + "anm", 1,
                                   waits=P.after(nm + "asq", nm + "anr", key + "c")
                                   + self.initw())
                            P.prod("scalar",
                                   (lambda e, h=h: e.activation(
                                       krow[h][:], self.pb[4][0:1, 0:512], AF.Sqrt)),
                                   nm + "anr", 1,
                                   waits=P.after(nm + "anm", nm + "abc2")),
                            P.prod("vector",
                                   (lambda e, h=h: e.tensor_scalar(
                                       krow[h][:], krow[h][:], 1e-6, None, ALU.add)),
                                   nm + "anr2", 1, waits=P.after(nm + "anr"))
                            P.prod("vector",
                                   (lambda e, h=h: e.reciprocal(krow[h][:], krow[h][:])),
                                   nm + "anr2", 1, waits=P.after(nm + "anr2"))
                            P.prod("tensor",
                                   (lambda e, h=h: e.matmul(
                                       self.pb[5][h * 64:(h + 1) * 64, 0:512],
                                       self.onesrow[0:1, 0:64], krow[h][:],
                                       start=True, stop=True)),
                                   nm + "abc2", 1,
                                   waits=P.after(nm + "anr2", nm + "akn") + self.initw())
                        for h in range(2):
                            P.prod("vector",
                                   (lambda e, h=h: e.tensor_mul(
                                       knb[h * 64:(h + 1) * 64, :],
                                       cf[h * 64:(h + 1) * 64, :],
                                       self.pb[5][h * 64:(h + 1) * 64, 0:512])),
                                   nm + "akn", 1,
                                   waits=P.after(nm + "abc2", nm + "ast"))
                    else:
                        P.prod("scalar",
                               (lambda e: e.activation(sqb[:], cf[:], AF.Sigmoid)),
                               nm + "asq", 1, waits=P.after(nm + "acv", nm + "anm")),
                        P.prod("vector",
                               (lambda e: e.tensor_mul(knb[:], cf[:], sqb[:])),
                               nm + "akn", 1, waits=P.after(nm + "asq", nm + "ast"))
                    P.prod("sync",
                           (lambda e, dn=dn, c=c, ts=ts: e.dma_start(
                               out=dn[c * 128:(c + 1) * 128, ts * 512:(ts + 1) * 512],
                               in_=knb[:])),
                           nm + "ast", 16, waits=P.after(nm + "akn", nm + "bld"))
        karr = self.scrb[:, 0:512]
        qarr = self.scrb[:, 512:1024]
        varr = self.scrb[:, 1024:1536]
        Ab = self.scrb[:, 1536:2304]
        QaQc = self.scrb[:, 2304:2432]
        msk = self.scrb[:, 2432:2688]
        ktm = self.scrb[:, 2688:2752]
        vtm = self.scrb[:, 2752:2816]
        Ub = self.scrb[:, 2816:2880]
        Uwb = self.scrb[:, 2880:2944]
        On = self.scrb[:, 2944:3008]
        M3T = self.scrb[:, 3008:3136]
        sqt = self.scrb[:, 3136:3200]
        Ttmp = self.scrb[:, 3200:3328]
        w2f32 = self.wpool2.ap().bitcast(F32)
        Wt32 = w2f32[:, 0:512]
        Wa32 = w2f32[:, 512:1024]
        Wtb = self.wpool2[:, 2048:2560]
        Wab = self.wpool2[:, 2560:3072]
        ebs = self.scrf[:, 0:128]
        tmpf = self.scrf[:, 128:256]
        U32 = self.scrf[:, 256:320]
        mscol = self.scrf[:, 320:321]
        rccol = self.scrf[:, 321:322]
        ecol = self.scrf[:, 324:325]
        erow = self.scrf[0:1, 364:492]
        eps6 = self.eps_col(1e-6, 128)
        mskab = self.scrb[:, 3328:3456]
        wib = self.scrb.ap().bitcast(F32)[:, 1728:1760]
        wbcl = P.after(nm + "acv", nm + "arv", nm + "arz", nm + "bm", nm + "buw")
        P.prod("sync",
               (lambda e: e.dma_start(out=mskab[:], in_=self.l0a_cmska.ap())),
               nm + "bcl", 16, waits=wbcl)
        P.prod("sync",
               (lambda e: e.dma_start(out=wib[:], in_=self.l0a_cwi.ap())),
               nm + "bcl", 16, waits=wbcl)
        for t_, nmm in ((Wt32, "bi"), (Wa32, "bi"), (Wtb, "bi"), (Wab, "bi")):
            P.prod("vector", (lambda e, t_=t_: e.memset(t_[:], 0.0)), nm + "bi", 1,
                   waits=P.after("postln", nm + "bwq", nm + "bom", nm + "bwm"))
        for i in range(NCH):
            tsv = float(i * 64)
            coef1 = tsv / (tsv + 64.0) * ((1 + tsv) / (1 + tsv + 64.0))
            coef2 = 64.0 / (tsv + 64.0) * ((tsv * 2 + 1 + 64.0) / (1 + tsv + 64.0))
            wld = P.after(nm + "ast") + P.after(
                nm + "bgm", nm + "bwm", nm + "bkq", nm + "btv", nm + "bqa",
                nm + "bm", nm + "buw")
            for src, dst in ((self.l0a_kd, karr), (self.l0a_qd, qarr),
                             (self.l0a_vd, varr)):
                P.prod("sync",
                       (lambda e, src=src, dst=dst, i=i: e.dma_start(
                           out=dst[:],
                           in_=src[:, i * 64:(i + 1) * 64].rearrange(
                               "(f p) c -> p f c", p=128))),
                       nm + "bld", 16, waits=wld)
            P.prod("sync",
                   (lambda e, i=i: e.dma_start(
                       out=msk[:], in_=self.l0a_cmask.ap()[i * 128:(i + 1) * 128, :])),
                   nm + "bld", 16, waits=wld)
            wbl = P.after(nm + "bld")
            for p in range(8):
                for hh in range(2):
                    P.prod("sync",
                           (lambda e, i=i, p=p, hh=hh: e.dma_start(
                               out=erow[0:1, hh * 64:(hh + 1) * 64],
                               in_=self.l0a_ed[2 * p + hh:2 * p + hh + 1,
                                               i * 64:(i + 1) * 64])),
                           nm + "bec", 16,
                           waits=P.after(nm + "aes", nm + "bbc", nm + "brh"))
                    P.prod("sync",
                           (lambda e, i=i, p=p, hh=hh: e.dma_start(
                               out=ecol[hh * 64:(hh + 1) * 64, :],
                               in_=self.l0a_ed[2 * p + hh,
                                               i * 64:(i + 1) * 64].rearrange(
                                   "(t o) -> t o", o=1))),
                           nm + "bec", 16,
                           waits=P.after(nm + "aes", nm + "bbc", nm + "brh"))
                for h in range(2):
                    P.prod("tensor",
                           (lambda e, h=h, p=p: e.transpose(
                               self.pb[2].ap().bitcast(BF16)[0:64, h * 64:h * 64 + 64],
                               karr[h * 64:h * 64 + 64, p * 64:p * 64 + 64],
                               self.ident[h * 64:h * 64 + 64, h * 64:h * 64 + 64])),
                           nm + "btv", 1,
                           waits=wbl + self.initw() + P.after(nm + "bte"))
                    P.prod("tensor",
                           (lambda e, h=h, p=p: e.transpose(
                               self.pb[6].ap().bitcast(BF16)[0:64, h * 64:h * 64 + 64],
                               varr[h * 64:h * 64 + 64, p * 64:p * 64 + 64],
                               self.ident[h * 64:h * 64 + 64, h * 64:h * 64 + 64])),
                           nm + "btv", 1,
                           waits=wbl + self.initw() + P.after(nm + "bte"))
                for h in range(2):
                    P.prod("vector",
                           (lambda e, h=h: e.tensor_copy(
                               ktm[h * 64:(h + 1) * 64, :],
                               self.pb[2].ap().bitcast(BF16)[0:64, h * 64:h * 64 + 64])),
                           nm + "bte", 1,
                           waits=P.after(nm + "btv", nm + "bku", nm + "brh"))
                    P.prod("vector",
                           (lambda e, h=h: e.tensor_copy(
                               vtm[h * 64:(h + 1) * 64, :],
                               self.pb[6].ap().bitcast(BF16)[0:64, h * 64:h * 64 + 64])),
                           nm + "bte", 1,
                           waits=P.after(nm + "btv", nm + "brh"))
                for h in range(2):
                    P.prod("tensor",
                           (lambda e, h=h, p=p: e.matmul(
                               self.pb[0][h * 64:(h + 1) * 64, h * 64:h * 64 + 64],
                               karr[h * 64:h * 64 + 64, p * 64:p * 64 + 64],
                               karr[h * 64:h * 64 + 64, p * 64:p * 64 + 64],
                               start=True, stop=True)),
                           nm + "bgm", 1, waits=wbl + P.after(nm + "bm"))
                P.prod("tensor",
                       (lambda e, p=p: e.matmul(
                           self.pb[1][0:128, 0:128], self.onesrow[0:1, 0:128],
                           erow[0:1, 0:128], start=True, stop=True)),
                       nm + "bbc", 1,
                       waits=P.after(nm + "bec", nm + "bev") + self.initw())
                P.prod("vector",
                       (lambda e: e.tensor_copy(ebs[:], self.pb[1][0:128, 0:128])),
                       nm + "bev", 1, waits=P.after(nm + "bbc", nm + "bm"))
                P.prod("vector",
                       (lambda e: e.tensor_tensor(tmpf[:], self.pb[0][0:128, 0:128],
                                                  mskab[:], ALU.mult)),
                       nm + "bm", 1,
                       waits=P.after(nm + "bgm", nm + "bcl", nm + "bev"))
                P.prod("vector",
                       (lambda e: e.tensor_tensor(Ab[:, 0:128], tmpf[:], ebs[:],
                                                  ALU.mult)),
                       nm + "bm", 1, waits=P.after(nm + "bm", nm + "bsq", nm + "bap"))
                P.prod("tensor",
                       (lambda e: e.transpose(self.pb[2].ap().bitcast(BF16)[0:128, 0:128],
                                              Ab[:, 0:128], self.ident[:])),
                       nm + "btr", 1,
                       waits=P.after(nm + "bm", nm + "bte2", nm + "bte") + self.initw())
                P.prod("vector",
                       (lambda e: e.tensor_copy(Ttmp[:],
                                                self.pb[2].ap().bitcast(BF16)[0:128, 0:128])),
                       nm + "bte2", 1, waits=P.after(nm + "btr", nm + "bsq"))
                for k in range(1, 6):
                    P.prod("tensor",
                           (lambda e, k=k: e.matmul(self.pb[3][0:128, 0:128],
                                                    Ttmp[:], Ab[:, (k - 1) * 128:k * 128],
                                                    start=True, stop=True)),
                           nm + "bsq", 1,
                           waits=P.after(nm + "bte2", nm + "bse", nm + "bm"))
                    P.prod("vector",
                           (lambda e, k=k: e.tensor_copy(
                               Ab[:, k * 128:(k + 1) * 128], self.pb[3][0:128, 0:128])),
                           nm + "bse", 1, waits=P.after(nm + "bsq", nm + "bap"))
                    if k < 5:
                        P.prod("tensor",
                               (lambda e, k=k: e.transpose(
                                   self.pb[2].ap().bitcast(BF16)[0:128, 0:128],
                                   Ab[:, k * 128:(k + 1) * 128], self.ident[:])),
                               nm + "btr", 1,
                               waits=P.after(nm + "bse", nm + "bte2"))
                        P.prod("vector",
                               (lambda e: e.tensor_copy(
                                   Ttmp[:], self.pb[2].ap().bitcast(BF16)[0:128, 0:128])),
                               nm + "bte2", 1, waits=P.after(nm + "btr", nm + "bsq"))
                for h in range(2):
                    P.prod("tensor",
                           (lambda e, h=h, p=p: e.matmul(
                               self.pb[4][h * 64:(h + 1) * 64, 0:64],
                               karr[h * 64:h * 64 + 64, p * 64:p * 64 + 64],
                               Wtb[h * 64:h * 64 + 64, p * 64:p * 64 + 64],
                               start=True, stop=True)),
                           nm + "bwm", 1,
                           waits=wbl + P.after(nm + "brh", nm + "bi", nm + "bwq"))
                P.prod("vector",
                       (lambda e: e.tensor_tensor(tmpf[:, 0:64], vtm[:],
                                                  self.pb[4][0:128, 0:64],
                                                  ALU.subtract)),
                       nm + "brh", 1, waits=P.after(nm + "bwm", nm + "bte"))
                P.prod("vector",
                       (lambda e: e.tensor_scalar(U32[:], tmpf[:, 0:64],
                                                   ecol[:], None, ALU.mult)),
                       nm + "brh", 1, waits=P.after(nm + "brh"))
                P.prod("vector",
                       (lambda e: e.tensor_copy(Ub[:], U32[:])),
                       nm + "bub", 1, waits=P.after(nm + "brh", nm + "bom", nm + "bku"))
                for k in range(5, -1, -1):
                    P.prod("tensor",
                           (lambda e, k=k: e.matmul(self.pb[4][0:128, 0:64],
                                                    Ab[:, k * 128:(k + 1) * 128], Ub[:],
                                                    start=True, stop=True)),
                           nm + "bap", 1,
                           waits=P.after(nm + "bub", nm + "bau", nm + "bse",
                                         nm + "brh"))
                    P.prod("vector",
                           (lambda e: e.tensor_tensor(U32[:], U32[:],
                                                      self.pb[4][0:128, 0:64], ALU.add)),
                           nm + "bau", 1, waits=P.after(nm + "bap"))
                    P.prod("vector",
                           (lambda e: e.tensor_copy(Ub[:], U32[:])),
                           nm + "bub", 1, waits=P.after(nm + "bau"))
                P.prod("vector",
                       (lambda e, p=p: e.tensor_tensor(QaQc[:, 0:64],
                                                       qarr[:, p * 64:p * 64 + 64],
                                                       msk[:, 128:192], ALU.mult)),
                       nm + "bqa", 1, waits=wbl + P.after(nm + "bom"))
                P.prod("vector",
                       (lambda e, p=p: e.tensor_tensor(QaQc[:, 64:128],
                                                       qarr[:, p * 64:p * 64 + 64],
                                                       msk[:, 192:256], ALU.mult)),
                       nm + "bqa", 1, waits=P.after(nm + "bqa"))
                for h in range(2):
                    P.prod("tensor",
                           (lambda e, h=h, p=p: e.matmul(
                               self.pb[0][h * 64:(h + 1) * 64, h * 64:h * 64 + 64],
                               karr[h * 64:h * 64 + 64, p * 64:p * 64 + 64],
                               qarr[h * 64:h * 64 + 64, p * 64:p * 64 + 64],
                               start=True, stop=True)),
                           nm + "bkq", 1, waits=wbl + P.after(nm + "bm"))
                P.prod("vector",
                       (lambda e: e.tensor_tensor(M3T[:], self.pb[0][0:128, 0:128],
                                                  msk[:, 0:128], ALU.mult)),
                       nm + "bm3", 1, waits=P.after(nm + "bkq", nm + "bom"))
                for h in range(2):
                    P.prod("tensor",
                           (lambda e, h=h, p=p: e.matmul(
                               self.pb[5][h * 64:(h + 1) * 64, 0:64],
                               QaQc[h * 64:h * 64 + 64, 0:64],
                               Wab[h * 64:h * 64 + 64, p * 64:p * 64 + 64],
                               start=True, stop=False)),
                           nm + "bom", 1,
                           waits=P.after(nm + "bqa", nm + "bm3", nm + "bub",
                                         nm + "bo", nm + "bi", nm + "bwq"))
                    P.prod("tensor",
                           (lambda e, h=h, p=p: e.matmul(
                               self.pb[5][h * 64:(h + 1) * 64, 0:64],
                               QaQc[h * 64:h * 64 + 64, 64:128],
                               Wtb[h * 64:h * 64 + 64, p * 64:p * 64 + 64],
                               start=False, stop=False)),
                           nm + "bom", 1, waits=P.after(nm + "bom"))
                    P.prod("tensor",
                           (lambda e, h=h: e.matmul(
                               self.pb[5][h * 64:(h + 1) * 64, 0:64],
                               M3T[h * 64:h * 64 + 64, h * 64:h * 64 + 64],
                               Ub[h * 64:h * 64 + 64, :],
                               start=False, stop=True)),
                           nm + "bom", 1, waits=P.after(nm + "bom"))
                P.prod("scalar",
                       (lambda e: e.activation(sqt[:], self.pb[5][0:128, 0:64],
                                               AF.Square, accum_out=mscol[:])),
                       nm + "bos", 1, waits=P.after(nm + "bom", nm + "bo")),
                P.prod("vector",
                       (lambda e: e.tensor_scalar(rccol[:], mscol[:], 1.0 / 64, None,
                                                  ALU.mult)),
                       nm + "bo", 1, waits=P.after(nm + "bos"))
                P.prod("scalar",
                       (lambda e, eps6=eps6: e.activation(rccol[:], rccol[:], AF.Sqrt,
                                                          bias=eps6)),
                       nm + "bo", 1, waits=P.after(nm + "bo") + self.initw())
                P.prod("vector",
                       (lambda e: e.reciprocal(rccol[:], rccol[:])),
                       nm + "bo", 1, waits=P.after(nm + "bo"))
                P.prod("vector",
                       (lambda e: e.tensor_scalar(On[:], self.pb[5][0:128, 0:64],
                                                  rccol[:], None, ALU.mult)),
                       nm + "bo", 1, waits=P.after(nm + "bo"))
                for h in range(2):
                    P.prod("tensor",
                           (lambda e, h=h: e.transpose(
                               self.pb[2].ap().bitcast(BF16)[0:64, h * 64:h * 64 + 64],
                               On[h * 64:h * 64 + 64, :],
                               self.ident[h * 64:h * 64 + 64, h * 64:h * 64 + 64])),
                           nm + "bot", 1,
                           waits=P.after(nm + "bo", nm + "boe", nm + "bte2"))
                for h in range(2):
                    P.prod("vector",
                           (lambda e, h=h, p=p, i=i: e.tensor_copy(
                               self.scra[h * 64:(h + 1) * 64,
                                         p * 2048 + i * 64:p * 2048 + (i + 1) * 64],
                               self.pb[2].ap().bitcast(BF16)[0:64, h * 64:h * 64 + 64])),
                           nm + "boe", 1, waits=P.after(nm + "bot"))
                P.prod("vector",
                       (lambda e, i=i: e.tensor_scalar(Uwb[:], U32[:],
                                                       wib[:, i:i + 1], None,
                                                       ALU.mult)),
                       nm + "buw", 1,
                       waits=P.after(nm + "bau", nm + "bkw") + P.after(nm + "bcl"))
                for h in range(2):
                    P.prod("tensor",
                           (lambda e, h=h: e.matmul(
                               self.pb[4][h * 64:(h + 1) * 64, 0:64],
                               ktm[h * 64:h * 64 + 64, :], Ub[h * 64:h * 64 + 64, :],
                               start=True, stop=True)),
                           nm + "bku", 1,
                           waits=P.after(nm + "bub", nm + "bau", nm + "bte",
                                         nm + "bwq", nm + "bom"))
                    P.prod("tensor",
                           (lambda e, h=h: e.matmul(
                               self.pb[7][h * 64:(h + 1) * 64, 0:64],
                               ktm[h * 64:h * 64 + 64, :], Uwb[h * 64:h * 64 + 64, :],
                               start=True, stop=True)),
                           nm + "bkw", 1,
                           waits=P.after(nm + "buw", nm + "bte", nm + "bwq"))
                P.prod("vector",
                       (lambda e, p=p, coef1=coef1: e.tensor_scalar(
                           Wa32[:, p * 64:(p + 1) * 64], Wa32[:, p * 64:(p + 1) * 64],
                           coef1, None, ALU.mult)),
                       nm + "bwq", 1,
                       waits=P.after(nm + "bom", nm + "bku", nm + "bkw", nm + "bi"))
                P.prod("vector",
                       (lambda e, p=p, coef2=coef2: e.scalar_tensor_tensor(
                           Wa32[:, p * 64:(p + 1) * 64], Wt32[:, p * 64:(p + 1) * 64],
                           coef2, Wa32[:, p * 64:(p + 1) * 64], ALU.mult, ALU.add)),
                       nm + "bwq", 1, waits=P.after(nm + "bwq"))
                P.prod("vector",
                       (lambda e, p=p: e.tensor_tensor(
                           Wa32[:, p * 64:(p + 1) * 64], Wa32[:, p * 64:(p + 1) * 64],
                           self.pb[7][0:128, 0:64], ALU.add)),
                       nm + "bwq", 1, waits=P.after(nm + "bwq"))
                P.prod("vector",
                       (lambda e, p=p: e.tensor_copy(Wab[:, p * 64:(p + 1) * 64],
                                                     Wa32[:, p * 64:(p + 1) * 64])),
                       nm + "bwq", 1, waits=P.after(nm + "bwq"))
                P.prod("vector",
                       (lambda e, p=p: e.tensor_tensor(
                           Wt32[:, p * 64:(p + 1) * 64], Wt32[:, p * 64:(p + 1) * 64],
                           self.pb[4][0:128, 0:64], ALU.add)),
                       nm + "bwq", 1, waits=P.after(nm + "bwq"))
                P.prod("vector",
                       (lambda e, p=p: e.tensor_copy(Wtb[:, p * 64:(p + 1) * 64],
                                                     Wt32[:, p * 64:(p + 1) * 64])),
                       nm + "bwq", 1, waits=P.after(nm + "bwq"))
        t3 = self.scrf[:, 0:512]
        for gc in range(8):
            P.prod("sync",
                   (lambda e, gc=gc: e.dma_start(
                       out=self.wpool[:, gc * 1024:(gc + 1) * 1024],
                       in_=self.wc_l0a["o"][gc * 128:(gc + 1) * 128, :])),
                   nm + "wcl", 16,
                   waits=wprep + P.after(nm + "apm", nm + "abm", "postln"))
        for ts in range(NTT):
            for oc in range(NCHUNK):
                bank = self.pb[oc % 4]
                P.prod("tensor",
                       (lambda e, ts=ts, oc=oc, bank=bank: [
                           e.matmul(bank[0:128, 0:512],
                                    self.wpool[:, gc * 1024 + oc * 128:
                                               gc * 1024 + (oc + 1) * 128],
                                    self.scra[:, gc * 2048 + ts * 512:
                                              gc * 2048 + (ts + 1) * 512],
                                    start=(gc == 0), stop=(gc == NCHUNK - 1))
                           for gc in range(NCHUNK)][-1]),
                       nm + "com", 1,
                       waits=P.after(nm + "wcl", nm + "boe", nm + "coe"))
                P.prod("vector",
                       (lambda e, oc=oc, bank=bank: e.tensor_scalar(
                           t3[:], bank[0:128, 0:512],
                           cols[:, 120 + oc:120 + oc + 1], None, ALU.add)),
                       nm + "coe", 1, waits=P.after(nm + "com", nm + "cox"))
                P.prod("vector",
                       (lambda e, ts=ts, oc=oc: e.tensor_tensor(
                           self.xs[:, oc * T + ts * 512:oc * T + (ts + 1) * 512],
                           self.xs[:, oc * T + ts * 512:oc * T + (ts + 1) * 512],
                           t3[:], ALU.add)),
                       nm + "cox", 1, waits=P.after(nm + "coe"))
        return self.emit_postln(P.after(nm + "cox"))

    def emit_l2a(self, b):
        nc, P = self.nc, self.P
        Wq = self.win_decl("L2_attn_Wq", [1024, 1024])
        Wqr = self.win_decl("L2_attn_Wqr", [1024, 512])
        Wkr = self.win_decl("L2_attn_Wkr", [1024, 32])
        Wkvd = self.win_decl("L2_attn_Wkvd", [1024, 256])
        Wku = self.win_decl("L2_attn_Wku", [256, 1024])
        Wvu = self.win_decl("L2_attn_Wvu", [256, 1024])
        Wo = self.win_decl("L2_attn_Wo", [1024, 1024])
        lobo = self.win_decl("L2_attn_lobo_log", [16])
        key = "l2a"
        if key not in self.once:
            self.once.add(key)
            P.extra_waits = P.after("postln") + self.xbok()
            self.wc_l2a = {}
            for n, src in (("q", Wq), ("qr", Wqr), ("kr", Wkr), ("kvd", Wkvd),
                           ("ku", Wku), ("vu", Wvu), ("o", Wo)):
                self.wc_l2a[n] = self.dram("wc2" + n, [src.shape[0], src.shape[1]])
                self.prep_cast(src, self.wc_l2a[n], src.shape[0], src.shape[1], key + "p")
            self.l2a_lobo = self.nc.alloc_sbuf_tensor("l2a_lobo", [1, 16], F32)
            P.prod("sync", (lambda e: e.dma_start(out=self.l2a_lobo[:],
                                                  in_=lobo.rearrange("(a n) -> a n", a=1))),
                   key + "b", 16)
            half = 16
            inv = (1.0 / (10000.0 ** (np.arange(half, dtype=np.float64) / half)))
            ang = np.arange(T, dtype=np.float64)[:, None] * inv[None, :]
            self.c_cos = nc.inline_tensor(np.ascontiguousarray(np.cos(ang).T.astype(BF)), "c_cos")
            self.c_sin = nc.inline_tensor(np.ascontiguousarray(np.sin(ang).T.astype(BF)), "c_sin")
            addm = np.zeros((4, 128, 512), np.float32)
            mulm = np.zeros((4, 128, 512), np.float32)
            for oi, o in enumerate((0, 128, 256, 384)):
                qq = np.arange(128)[:, None]
                kk = np.arange(512)[None, :]
                addm[oi] = np.where(kk <= qq + o, 0.0, -30000.0)
            for di, d in enumerate((0, 128, 256, 384)):
                pp = np.arange(128)[:, None]
                ff = np.arange(512)[None, :]
                mulm[di] = (pp + d <= ff).astype(np.float32)
            self.c_addm = nc.inline_tensor(addm.astype(BF).reshape(4 * 128, 512), "c_addm")
            self.c_mulm = nc.inline_tensor(mulm.astype(BF).reshape(4 * 128, 512), "c_mulm")
            P.extra_waits = []
        if not hasattr(self, "attn_dram"):
            self.attn_dram = self.dram("attn_dram", [1024, T], BF16)
        wc = self.wc_l2a
        nm = self.nid("l2a_")
        wprep = P.after(key + "p", key + "b", "postln")

        latf = self.scra[:, 0:4096]
        QT = {0: self.scra[:, 4096:6144], 1: self.scra[:, 6144:8192]}
        KT = {0: self.scra[:, 8192:10240], 1: self.scra[:, 10240:12288]}
        vtm = self.scra[:, 12288:14336]
        krh1 = self.scra[0:16, 14336:16384]
        krh2 = self.scra[32:48, 14336:16384]
        vfm = self.scrb[:, 0:2048]
        l2a_addm = self.scrb[:, 2048:4096]
        l2a_mulm = self.scrb[:, 4096:6144]
        et = self.scrb[:, 6144:6656]
        epi1 = self.scrb[0:64, 6656:7168]
        epi2 = self.scrb[0:64, 7168:7680]
        mcst = self.scrb[:, 7680:7696]
        WQ, WQR, WKU, WVU = 0, 1024, 1536, 1792
        costab = self.wpool[0:16, 4096:6144]
        sintab = self.wpool[0:16, 6144:8192]
        pre1 = self.wpool[0:16, 8192:10240]
        pre2 = self.wpool[0:16, 10240:12288]
        rt1 = self.wpool[0:16, 12288:14336]
        rt2 = self.wpool[0:16, 14336:16384]
        wkr_off = 2048
        mcol = self.sb("l2a_mcol", [128, 16], F32)
        g11 = self.scrf[0:1, 512:513]
        gcol = self.scrf[:, 515:516]
        lg11 = self.scrf[0:1, 516:517]
        r512 = self.scrf[0:1, 517:1029]
        rmax = self.scrf[0:1, 1029:1030]
        rtmp = self.scrf[:, 1030:1031]
        tq = self.scrf[0:1, 513:514]
        tk = self.scrf[0:1, 514:515]
        SC2 = 1.0 / math.sqrt(96.0)
        import os
        L2STG = int(os.environ.get("L2A_STAGES", "9"))

        P.prod("sync", (lambda e: e.dma_start(out=costab, in_=self.c_cos.ap())), nm + "rt", 16,
               waits=wprep)
        P.prod("sync", (lambda e: e.dma_start(out=sintab, in_=self.c_sin.ap())), nm + "rt", 16,
               waits=wprep)
        for mi in range(4):
            P.prod("sync",
                   (lambda e, mi=mi: e.dma_start(
                       out=l2a_addm[:, mi * 512:(mi + 1) * 512],
                       in_=self.c_addm.ap()[mi * 128:(mi + 1) * 128, :])),
                   nm + "mld", 16, waits=wprep + P.after(nm + "qma"))
            P.prod("sync",
                   (lambda e, mi=mi: e.dma_start(
                       out=l2a_mulm[:, mi * 512:(mi + 1) * 512],
                       in_=self.c_mulm.ap()[mi * 128:(mi + 1) * 128, :])),
                   nm + "mld", 16, waits=wprep + P.after(nm + "emk"))

        def rope16(p1, p2, o1, o2, chin, chname):
            P.prod("vector", (lambda e: e.tensor_mul(rt1[:], p1[:], costab[:])),
                   nm + "ro1", 1, waits=P.after(chin, nm + "rt", nm + "ro5", nm + "rodma"))
            P.prod("vector", (lambda e: e.tensor_mul(rt2[:], p2[:], sintab[:])),
                   nm + "ro2", 1, waits=P.after(chin, nm + "rt", nm + "ro5", nm + "rodma"))
            P.prod("vector", (lambda e, o1=o1: e.tensor_sub(o1, rt1[:], rt2[:])),
                   nm + "ro3", 1, waits=P.after(nm + "ro1", nm + "ro2"))
            P.prod("vector", (lambda e: e.tensor_mul(rt1[:], p2[:], costab[:])),
                   nm + "ro4", 1, waits=P.after(nm + "ro3"))
            P.prod("vector", (lambda e: e.tensor_mul(rt2[:], p1[:], sintab[:])),
                   nm + "ro4", 1, waits=P.after(nm + "ro3"))
            P.prod("vector", (lambda e: e.tensor_add(rt1[:], rt1[:], rt2[:])),
                   nm + "ro5", 1, waits=P.after(nm + "ro4"))
            P.prod("gpsimd", (lambda e, o2=o2: e.dma_start(out=o2, in_=rt1[:])),
                   nm + "rodma", 16, waits=P.after(nm + "ro5"))

        self.load_w(wc["kvd"], self.wpool2, 0, 1024, 0, 256, nm + "wla", wprep)
        for ts in range(NTT):
            for oc in range(2):
                bank = self.pb[oc]
                P.prod("tensor",
                       (lambda e, ts=ts, oc=oc, bank=bank: [
                           e.matmul(bank[0:128, 0:512],
                                    self.wpool2[:, ci * 256 + oc * 128: ci * 256 + (oc + 1) * 128],
                                    self.xb[:, ci * T + ts * 512:ci * T + (ts + 1) * 512],
                                    start=(ci == 0), stop=(ci == NCHUNK - 1))
                           for ci in range(NCHUNK)][-1]),
                       nm + "lmm", 1,
                       waits=P.after(nm + "wla", nm + "lev", "postln") + self.xbok())
                P.prod("vector",
                       (lambda e, ts=ts, oc=oc, bank=bank: e.tensor_copy(
                           latf[:, oc * T + ts * 512: oc * T + (ts + 1) * 512], bank[0:128, 0:512])),
                       nm + "lev", 1, waits=P.after(nm + "lmm"))
        self.load_w(wc["kr"], self.wpool, wkr_off, 1024, 0, 32, nm + "wlb", wprep)
        for half, dst in ((0, pre1), (1, pre2)):
            for ts in range(NTT):
                bank = self.pb[2]
                P.prod("tensor",
                       (lambda e, half=half, ts=ts, bank=bank: [
                           e.matmul(bank[0:16, 0:512],
                                    self.wpool[:, wkr_off + ci * 32 + half * 16:
                                                wkr_off + ci * 32 + half * 16 + 16],
                                    self.xb[:, ci * T + ts * 512:ci * T + (ts + 1) * 512],
                                    start=(ci == 0), stop=(ci == NCHUNK - 1))
                           for ci in range(NCHUNK)][-1]),
                       nm + "krm", 1,
                       waits=P.after(nm + "wlb", nm + "krev", "postln") + self.xbok())
                P.prod("vector",
                       (lambda e, dst=dst, ts=ts, bank=bank: e.tensor_copy(
                           dst[:, ts * 512:(ts + 1) * 512], bank[0:16, 0:512])),
                       nm + "krev", 1, waits=P.after(nm + "krm"))
        rope16(pre1, pre2, krh1[:], self.scra[32:48, 14336:16384], nm + "krev", nm + "kr")
        P.prod("vector", lambda e: e.nop(), nm + "krok", 1,
               waits=P.after(nm + "ro3", nm + "rodma"))

        for hp in range(8):
            wv = wprep + P.after(nm + "pvm", nm + "exp2", nm + "qmm", nm + "pmm", nm + "qrm",
                                 nm + "mex", nm + "dnm", nm + "ost", nm + "gm0", nm + "sm2")
            self.load_w(wc["q"], self.wpool, WQ, 1024, hp * 128, (hp + 1) * 128, nm + "wlc", wv)
            self.load_w(wc["qr"], self.wpool, WQR, 1024, hp * 64, (hp + 1) * 64, nm + "wlc", wv)
            self.load_w(wc["ku"], self.wpool, WKU, 256, hp * 128, (hp + 1) * 128, nm + "wlc", wv)
            self.load_w(wc["vu"], self.wpool, WVU, 256, hp * 128, (hp + 1) * 128, nm + "wlc", wv)
            for h in range(2):
                P.prod("vector",
                       (lambda e, h=h: e.memset(QT[h][64:128, :], 0.0)),
                       nm + "zp", 1, waits=P.after(nm + "qmm", nm + "pvm", nm + "exp2", nm + "sm2"))
                P.prod("vector",
                       (lambda e, h=h: e.memset(KT[h][64:128, :], 0.0)),
                       nm + "zp", 1, waits=P.after(nm + "qmm", nm + "pvm", nm + "exp2", nm + "sm2"))
            for h in range(2):
                for ts in range(NTT):
                    bank = self.pb[ts % 2]
                    P.prod("tensor",
                           (lambda e, h=h, ts=ts, bank=bank: [
                               e.matmul(bank[0:64, 0:512],
                                        self.wpool[:, WQ + ci * 128 + h * 64: WQ + ci * 128 + (h + 1) * 64],
                                        self.xb[:, ci * T + ts * 512:ci * T + (ts + 1) * 512],
                                        start=(ci == 0), stop=(ci == NCHUNK - 1))
                               for ci in range(NCHUNK)][-1]),
                           nm + "pmm", 1,
                           waits=P.after(nm + "wlc", nm + "pev", nm + "lev", nm + "zp", "postln")
                           + self.xbok())
                    P.prod("vector",
                           (lambda e, h=h, ts=ts, bank=bank: e.tensor_copy(
                               QT[h][0:64, ts * 512:(ts + 1) * 512], bank[0:64, 0:512])),
                           nm + "pev", 1, waits=P.after(nm + "pmm", nm + "gsq"))
                    bank2 = self.pb[2 + ts % 2]
                    P.prod("tensor",
                           (lambda e, h=h, ts=ts, bank2=bank2: [
                               e.matmul(bank2[0:64, 0:512],
                                        self.wpool[:, WKU + ci * 128 + h * 64:
                                                    WKU + ci * 128 + h * 64 + 64],
                                        latf[:, ci * T + ts * 512:ci * T + (ts + 1) * 512],
                                        start=(ci == 0), stop=(ci == 1))
                               for ci in range(2)][-1]),
                           nm + "pmm", 1,
                           waits=P.after(nm + "wlc", nm + "pev", nm + "lev", nm + "zp", "postln"))
                    P.prod("vector",
                           (lambda e, h=h, ts=ts, bank2=bank2: e.tensor_copy(
                               KT[h][0:64, ts * 512:(ts + 1) * 512], bank2[0:64, 0:512])),
                           nm + "pev", 1, waits=P.after(nm + "pmm", nm + "gsq"))
            for ts in range(NTT):
                bank = self.pb[2 + ts % 2]
                P.prod("tensor",
                       (lambda e, ts=ts, bank=bank: [
                           e.matmul(bank[0:128, 0:512],
                                    self.wpool[:, WVU + ci * 128: WVU + (ci + 1) * 128],
                                    latf[:, ci * T + ts * 512:ci * T + (ts + 1) * 512],
                                    start=(ci == 0), stop=(ci == 1))
                           for ci in range(2)][-1]),
                       nm + "vmm", 1,
                       waits=P.after(nm + "wlc", nm + "vevv", nm + "lev", "postln")),
                P.prod("vector",
                       (lambda e, ts=ts, bank=bank: e.tensor_copy(
                           vfm[:, ts * 512:(ts + 1) * 512], bank[0:128, 0:512])),
                       nm + "vevv", 1, waits=P.after(nm + "vmm", nm + "omm"))
            for h in range(2):
                for half, dst in ((0, pre1), (1, pre2)):
                    wcol = WQR + h * 32 + half * 16
                    for ts in range(NTT):
                        bank = self.pb[6 + half]
                        P.prod("tensor",
                               (lambda e, wcol=wcol, ts=ts, bank=bank: [
                                   e.matmul(bank[0:16, 0:512],
                                            self.wpool[:, wcol + ci * 64: wcol + ci * 64 + 16],
                                            self.xb[:, ci * T + ts * 512:ci * T + (ts + 1) * 512],
                                            start=(ci == 0), stop=(ci == NCHUNK - 1))
                                   for ci in range(NCHUNK)][-1]),
                               nm + "qrm", 1,
                               waits=P.after(nm + "wlc", nm + "qrev", nm + "krok", "postln")
                               + self.xbok())
                        P.prod("vector",
                               (lambda e, dst=dst, ts=ts, bank=bank: e.tensor_copy(
                                   dst[:, ts * 512:(ts + 1) * 512], bank[0:16, 0:512])),
                               nm + "qrev", 1, waits=P.after(nm + "qrm", nm + "ro3", nm + "rodma"))
                rope16(pre1, pre2, QT[h][64:80, :], self.scra[80:96, 4096 + h * 2048:
                                                              4096 + (h + 1) * 2048],
                       nm + "qrev", nm + "q")
            for h in range(2):
                P.prod("vector",
                       (lambda e, h=h: e.tensor_copy(KT[h][64:80, :], krh1[:])),
                       nm + "krc", 1, waits=P.after(nm + "krok", nm + "zp", nm + "qmm"))
                P.prod("gpsimd",
                       (lambda e, h=h: e.dma_start(out=self.scra[80:96, 8192 + h * 2048:
                                                                 8192 + (h + 1) * 2048],
                                                   in_=self.scra[32:48, 14336:16384])),
                       nm + "krc2", 16, waits=P.after(nm + "krok", nm + "zp", nm + "qmm"))
            P.prod("vector", lambda e: e.nop(), nm + "pq", 1,
                   waits=P.after(nm + "pev", nm + "ro3", nm + "rodma", nm + "krc", nm + "krc2",
                                 nm + "zp"))
            for h in range(2):
                for kt in range(16):
                    pv = self.pb[4 + kt % 2].ap().bitcast(BF16)
                    P.prod("tensor",
                           (lambda e, h=h, kt=kt, pv=pv: e.transpose(
                               pv[0:128, 0:64],
                               vfm[h * 64:(h + 1) * 64, kt * 128:(kt + 1) * 128],
                               self.ident[h * 64:h * 64 + 64, h * 64:h * 64 + 64])),
                           nm + "vtr", 1,
                           waits=P.after(nm + "vevv", nm + "vev", nm + "pvm") + self.initw())
                    P.prod("vector",
                           (lambda e, h=h, kt=kt, pv=pv: e.tensor_copy(
                               vtm[:, h * 1024 + kt * 64: h * 1024 + (kt + 1) * 64], pv[0:128, 0:64])),
                           nm + "vev", 1, waits=P.after(nm + "vtr"))
            if L2STG < 2:
                continue
            for h in range(2):
                hglob = hp * 2 + h
                P.prod("vector", (lambda e: e.memset(rmax[:], 0.0)), nm + "gmx", 1,
                       waits=P.after(nm + "gred", nm + "gmx2", nm + "gsr"))
                for j, fm1 in enumerate((QT[h], KT[h])):
                    for ts in range(NTT):
                        P.prod("scalar",
                               (lambda e, fm1=fm1, ts=ts: e.activation(
                                   self.scrf[:, 0:512], fm1[:, ts * 512:(ts + 1) * 512],
                                   AF.Square)),
                               nm + "gsq", 1,
                               waits=P.after(nm + "pq", nm + "gm0", nm + "qma", nm + "qrd",
                                             "postln"))
                        P.prod("tensor",
                               (lambda e: e.matmul(self.pb[6][0:1, 0:512],
                                                   self.onescolf[:, 0:1],
                                                   self.scrf[:, 0:512], start=True, stop=True)),
                               nm + "gm0", 1, waits=P.after(nm + "gsq", nm + "gred", nm + "oev")
                               + self.initw())
                        P.prod("vector",
                               (lambda e: e.reduce_max(rtmp[0:1, 0:1], self.pb[6][0:1, 0:512],
                                                       axis=AX.X)),
                               nm + "gred", 1, waits=P.after(nm + "gm0", nm + "gmx2", nm + "gmx"))
                        P.prod("vector",
                               (lambda e, j=j: e.tensor_tensor(
                                   (rmax[:] if j == 0 else tk[:]) if False else rmax[:],
                                   rmax[:], rtmp[0:1, 0:1], ALU.max)),
                               nm + "gmx2", 1, waits=P.after(nm + "gred"))
                    P.prod("scalar",
                           (lambda e, j=j: e.activation(tq[:] if j == 0 else tk[:],
                                                        rmax[:], AF.Sqrt)),
                           nm + "gsr", 1, waits=P.after(nm + "gmx2"))
                    if j == 0:
                        P.prod("vector", (lambda e: e.memset(rmax[:], 0.0)), nm + "gmx", 1,
                               waits=P.after(nm + "gred", nm + "gmx2", nm + "gsr"))
                P.prod("vector", (lambda e: e.tensor_mul(g11[:], tq[:], tk[:])),
                       nm + "g1", 1, waits=P.after(nm + "gsr"))
                P.prod("vector", (lambda e: e.tensor_scalar(g11[:], g11[:], -SC2, None, ALU.mult)),
                       nm + "g2", 1, waits=P.after(nm + "g1"))
                P.prod("tensor",
                       (lambda e: e.matmul(self.pb[6][0:128, 0:1], self.onesrow[0:1, 0:128],
                                           g11[:], start=True, stop=True)),
                       nm + "g3", 1, waits=P.after(nm + "g2", nm + "gred")),
                P.prod("vector",
                       (lambda e: e.tensor_copy(gcol[:], self.pb[6][0:128, 0:1])),
                       nm + "g4", 1, waits=P.after(nm + "g3"))
                P.prod("vector",
                       (lambda e, hglob=hglob: e.tensor_tensor(
                           lg11[:], g11[:], self.l2a_lobo[0:1, hglob:hglob + 1], ALU.add)),
                       nm + "lg", 1, waits=P.after(nm + "g2", key + "b", nm + "mex"))
                if L2STG < 3:
                    continue
                for i in range(16):
                    nkb = (i + 4) // 4
                    for kb in range(nkb):
                        bank = self.pb[kb % 2]
                        P.prod("tensor",
                               (lambda e, i=i, kb=kb, h=h, bank=bank: e.matmul(
                                   bank[0:128, 0:512],
                                   QT[h][:, i * 128:(i + 1) * 128],
                                   KT[h][:, kb * 512:(kb + 1) * 512],
                                   start=True, stop=True)),
                               nm + "qmm", 1,
                               waits=P.after(nm + "pq", nm + "g4", nm + "qrd", nm + "exp2",
                                             nm + "gm0")),
                        if kb == nkb - 1:
                            oi = (i * 128 - kb * 512) // 128
                            P.prod("vector",
                                   (lambda e, oi=oi, bank=bank: e.tensor_tensor(
                                       self.scrf[:, 0:512], bank[0:128, 0:512],
                                       l2a_addm[:, oi * 512:(oi + 1) * 512], ALU.add)),
                                   nm + "qma", 1, waits=P.after(nm + "qmm", nm + "qrd",
                                                                nm + "gsq", nm + "mld"))
                            P.prod("vector",
                                   (lambda e, i=i, kb=kb: e.tensor_reduce(
                                       mcol[:, i:i + 1] if kb == 0 else rtmp[:],
                                       self.scrf[:, 0:512], AX.X, ALU.max)),
                                   nm + "qrd", 1, waits=P.after(nm + "qma"))
                        else:
                            P.prod("vector",
                                   (lambda e, i=i, kb=kb, bank=bank: e.tensor_reduce(
                                       mcol[:, i:i + 1] if kb == 0 else rtmp[:],
                                       bank[0:128, 0:512], AX.X, ALU.max)),
                                   nm + "qrd", 1, waits=P.after(nm + "qmm", nm + "qmx"))
                        if kb > 0:
                            P.prod("vector",
                                   (lambda e, i=i: e.tensor_tensor(
                                       mcol[:, i:i + 1], mcol[:, i:i + 1], rtmp[:], ALU.max)),
                                   nm + "qmx", 1, waits=P.after(nm + "qrd"))
                if L2STG < 4:
                    continue
                for qt in range(NTT):
                    nkt = (qt + 1) * 4
                    for kt in range(nkt):
                        bank = self.pb[kt % 2]
                        P.prod("tensor",
                               (lambda e, kt=kt, qt=qt, h=h, bank=bank: e.matmul(
                                   bank[0:128, 0:512],
                                   KT[h][:, kt * 128:(kt + 1) * 128],
                                   QT[h][:, qt * 512:(qt + 1) * 512],
                                   start=True, stop=True)),
                               nm + "sm2", 1,
                               waits=P.after(nm + "exp2", nm + "qmx", nm + "qrd")),
                        P.prod("scalar",
                               (lambda e, bank=bank: e.activation(
                                   et[:], bank[0:128, 0:512], AF.Exp, bias=gcol[:], scale=SC2)),
                               nm + "exp2", 1,
                               waits=P.after(nm + "sm2", nm + "pvm", nm + "dnm", nm + "emk")),
                        d = kt * 128 - qt * 512
                        if d >= 0:
                            P.prod("vector",
                                   (lambda e, d=d: e.tensor_mul(
                                       et[:], et[:],
                                       l2a_mulm[:, (d // 128) * 512:(d // 128 + 1) * 512])),
                                   nm + "emk", 1, waits=P.after(nm + "exp2", nm + "mld"))
                        P.prod("tensor",
                               (lambda e, kt=kt, h=h, nkt=nkt: e.matmul(
                                   self.pb[6][0:64, 0:512],
                                   vtm[:, h * 1024 + kt * 64: h * 1024 + (kt + 1) * 64], et[:],
                                   start=(kt == 0), stop=(kt == nkt - 1))),
                               nm + "pvm", 1,
                               waits=P.after(nm + "exp2", nm + "emk", nm + "oev"))
                        P.prod("tensor",
                               (lambda e, kt=kt, nkt=nkt: e.matmul(
                                   self.pb[5][0:1, 0:512], self.onescol[:, 0:1], et[:],
                                   start=(kt == 0), stop=(kt == nkt - 1))),
                               nm + "dnm", 1,
                               waits=P.after(nm + "exp2", nm + "emk", nm + "od"))
                    for sub in range(4):
                        pv3 = self.pb[7].ap().bitcast(BF16)
                        P.prod("vector",
                               (lambda e, qt=qt, sub=sub: e.tensor_copy(
                                   mcst[:, sub:sub + 1], mcol[:, qt * 4 + sub:qt * 4 + sub + 1])),
                               nm + "mc1", 1, waits=P.after(nm + "qrd", nm + "qmx", nm + "mtr"))
                        P.prod("tensor",
                               (lambda e, sub=sub, pv3=pv3: e.transpose(
                                   pv3[0:1, sub * 128:sub * 128 + 128],
                                   mcst[:, sub:sub + 1], self.ident[:])),
                               nm + "mtr", 1, waits=P.after(nm + "mc1", nm + "mex", nm + "omu") + self.initw())
                    P.prod("scalar",
                           (lambda e, pv3=self.pb[7].ap().bitcast(BF16): e.activation(
                               r512[:], pv3[0:1, 0:512], AF.Exp, bias=lg11[:], scale=SC2)),
                           nm + "mex", 1, waits=P.after(nm + "mtr", nm + "lg"))
                    P.prod("vector",
                           (lambda e: e.tensor_tensor(r512[:], r512[:],
                                                      self.pb[5][0:1, 0:512], ALU.add)),
                           nm + "od", 1, waits=P.after(nm + "mex", nm + "dnm"))
                    P.prod("vector",
                           (lambda e: e.reciprocal(r512[:], r512[:])),
                           nm + "od2", 1, waits=P.after(nm + "od"))
                    P.prod("tensor",
                           (lambda e: e.matmul(self.pb[7][0:64, 0:512],
                                               self.onesrow[0:1, 0:64], r512[:],
                                               start=True, stop=True)),
                           nm + "obc", 1, waits=P.after(nm + "od2", nm + "omu", nm + "mex")),
                    P.prod("vector",
                           (lambda e: e.tensor_copy(epi1[:], self.pb[6][0:64, 0:512])),
                           nm + "oev", 1, waits=P.after(nm + "pvm", nm + "ost")),
                    P.prod("vector",
                           (lambda e: e.tensor_mul(epi2[:], epi1[:], self.pb[7][0:64, 0:512])),
                           nm + "omu", 1, waits=P.after(nm + "obc", nm + "oev"))
                    P.prod("sync",
                           (lambda e, hglob=hglob, qt=qt: e.dma_start(
                               out=self.attn_dram[hglob * 64:(hglob + 1) * 64,
                                                  qt * 512:(qt + 1) * 512],
                               in_=epi2[:])),
                           nm + "ost", 16, waits=P.after(nm + "omu"))
        if L2STG < 5:
            P.prod("vector", lambda e: e.nop(), nm + "oadd!", 1,
                   waits=P.after(nm + "pev", nm + "qrev", nm + "vev", nm + "gsr", nm + "qrd",
                                 nm + "ost", nm + "lev", nm + "krev", nm + "krc", nm + "krc2",
                                 nm + "rodma"))
            return self.emit_postln(P.after(nm + "oadd!"))
        self.load_w(wc["o"], self.wpool2, 0, 1024, 0, 1024, nm + "wo",
                    wprep + P.after(nm + "lmm"))
        for ts in range(NTT):
            for ci in range(NCHUNK):
                P.prod("sync",
                       (lambda e, ci=ci, ts=ts: e.dma_start(
                           out=self.scrb[:, ci * 512: (ci + 1) * 512],
                           in_=self.attn_dram[ci * 128:(ci + 1) * 128, ts * 512:(ts + 1) * 512])),
                       nm + "ald", 16, waits=P.after(nm + "ost", nm + "omm", nm + "vevv",
                                                     nm + "qma", nm + "emk"))
            for oc in range(NCHUNK):
                bank = self.pb[oc % 4]
                P.prod("tensor",
                       (lambda e, oc=oc, ts=ts, bank=bank: [
                           e.matmul(bank[0:128, 0:512],
                                    self.wpool2[:, ci * 1024 + oc * 128: ci * 1024 + (oc + 1) * 128],
                                    self.scrb[:, ci * 512: (ci + 1) * 512],
                                    start=(ci == 0), stop=(ci == NCHUNK - 1))
                           for ci in range(NCHUNK)][-1]),
                       nm + "omm", 1,
                       waits=P.after(nm + "ald", nm + "wo") + P.after_lag(nm + "oadd!", 3)),
                P.prod("vector",
                       (lambda e, oc=oc, ts=ts, bank=bank: e.tensor_tensor(
                           self.xs[:, oc * T + ts * 512: oc * T + (ts + 1) * 512],
                           self.xs[:, oc * T + ts * 512: oc * T + (ts + 1) * 512],
                           bank[0:128, 0:512], ALU.add)),
                       nm + "oadd!", 1, waits=P.after(nm + "omm"))
        return self.emit_postln(P.after(nm + "oadd!"))



    def emit_l3a(self, b):
        nc, P = self.nc, self.P
        Wq = self.win_decl("L3_attn_Wq", [1024, 1024])
        bq = self.win_decl("L3_attn_bq", [1024])
        Wk = self.win_decl("L3_attn_Wk", [1024, 1024])
        bk = self.win_decl("L3_attn_bk", [1024])
        Wv = self.win_decl("L3_attn_Wv", [1024, 1024])
        bv = self.win_decl("L3_attn_bv", [1024])
        Wo = self.win_decl("L3_attn_Wo", [1024, 1024])
        bo = self.win_decl("L3_attn_bo", [1024])
        alpha = self.win_decl("L3_attn_alpha", [16, 5])
        phi = self.win_decl("L3_attn_phi", [16, 5])
        temp = self.win_decl("L3_attn_temperature", [1])
        gam = self.win_decl("L3_attn_gamma", [1024])
        bet = self.win_decl("L3_attn_beta", [1024])
        key = "l3a"
        FREQS = [31.0, 37.0, 41.0, 43.0, 47.0]
        if key not in self.once:
            self.once.add(key)
            P.extra_waits = P.after("postln") + self.xbok()
            self.wc_l3a = {}
            for n, src in (("q", Wq), ("k", Wk), ("v", Wv), ("o", Wo)):
                self.wc_l3a[n] = self.dram("wc3" + n, [1024, 1024])
            tmps = [self.prep_transpose_cast(w, 1024, 1024, key + "p")
                    for w in (Wq, Wk, Wv, Wo)]
            for (n, w), tmp in zip((("q", Wq), ("k", Wk), ("v", Wv), ("o", Wo)), tmps):
                self.prep_transpose(w, self.wc_l3a[n], 1024, 1024, key + "p", tmp=tmp)
            self.l3a_bcols = self.nc.alloc_sbuf_tensor("l3a_bcols", [128, 48], F32)
            for i, v in enumerate((bq, bk, bv, bo, gam, bet)):
                self.load_vec_cols(v, 1024, key + "b", dst=self.l3a_bcols, dst_off=8 * i)
            self.l3a_rows = self.scrf[0:1, 1031:1223]
            self.l3a_cols = self.nc.alloc_sbuf_tensor("l3a_cols", [128, 176], F32)
            c_wbase = nc.inline_tensor(
                np.asarray([[2.0 * math.pi / (f * 8.0) for f in FREQS]], dtype=np.float32),
                "c_wbase")
            self.l3a_wbase = self.nc.alloc_sbuf_tensor("l3a_wbase", [1, 5], F32)
            P.prod("sync", (lambda e: e.dma_start(out=self.l3a_wbase[:], in_=c_wbase.ap())),
                   key + "b", 16)
            R = self.l3a_rows
            PI = math.pi
            P.prod("vector", (lambda e: e.tensor_scalar(R[0:1, 96 + 0:96 + 0], R[0:1, 0:1], 0.0,
                                                        None, ALU.mult)) if False else
                   (lambda e: e.nop()), key + "b", 1)
            P.extra_waits = []
        wcq = self.wc_l3a
        nm = self.nid("l3a_")
        wprep = P.after(key + "p", key + "b", "postln")
        R = self.l3a_rows
        PI = math.pi
        P.prod("sync", (lambda e: e.dma_start(out=R[0:1, 0:80],
                                              in_=alpha.rearrange("(a h) f -> a (h f)", a=1))),
               nm + "rld", 16, waits=wprep)
        P.prod("sync", (lambda e: e.dma_start(out=R[0:1, 80:160],
                                              in_=phi.rearrange("(a h) f -> a (h f)", a=1))),
               nm + "rld", 16, waits=wprep)
        P.prod("sync", (lambda e: e.dma_start(out=R[0:1, 160:161],
                                              in_=temp.rearrange("(a n) -> a n", a=1))),
               nm + "rld", 16, waits=wprep)

        P.prod("vector", (lambda e: e.tensor_scalar(R[0:1, 96:176] if False else R[0:1, 96:97],
                                                    R[0:1, 96:97], 0.0, None, ALU.mult)) if False
               else (lambda e: e.nop()), nm + "sc0", 1, waits=wprep)
        ph2 = self.scrf[0:1, 1223:1303]
        m1 = self.scrf[0:1, 1303:1383]
        m2 = self.scrf[0:1, 1383:1463]
        app = m1
        P.prod("vector", (lambda e: e.tensor_scalar(m1[:], R[0:1, 80:160], PI, None, ALU.is_gt)),
               nm + "s1", 1, waits=wprep + P.after(nm + "bc", nm + "rld"))
        P.prod("vector", (lambda e: e.scalar_tensor_tensor(ph2[:], m1[:], -2.0 * PI,
                                                           R[0:1, 80:160], ALU.mult, ALU.add)),
               nm + "s2", 1, waits=P.after(nm + "s1"))
        P.prod("vector", (lambda e: e.tensor_scalar(m1[:], ph2[:], PI / 2, None, ALU.is_gt)),
               nm + "s3", 1, waits=P.after(nm + "s2"))
        P.prod("vector", (lambda e: e.tensor_scalar(m2[:], ph2[:], -PI / 2, None, ALU.is_lt)),
               nm + "s4", 1, waits=P.after(nm + "s2"))
        tdx = self.scrf[0:1, 1463:1536]
        P.prod("vector", (lambda e: e.scalar_tensor_tensor(ph2[:], m1[:], -PI, ph2[:],
                                                           ALU.mult, ALU.add)),
               nm + "s5", 1, waits=P.after(nm + "s3", nm + "s4"))
        P.prod("vector", (lambda e: e.scalar_tensor_tensor(ph2[:], m2[:], PI, ph2[:],
                                                           ALU.mult, ALU.add)),
               nm + "s6", 1, waits=P.after(nm + "s5"))
        P.prod("vector", (lambda e: e.tensor_add(m1[:], m1[:], m2[:])),
               nm + "s7", 1, waits=P.after(nm + "s5", nm + "s6"))
        P.prod("vector", (lambda e: e.tensor_scalar(m1[:], m1[:], -2.0, 1.0, ALU.mult, ALU.add)),
               nm + "s8", 1, waits=P.after(nm + "s7"))
        P.prod("vector", (lambda e: e.tensor_mul(app[:], m1[:], R[0:1, 0:80])),
               nm + "s9", 1, waits=P.after(nm + "s8"))
        trec = self.scrf[0:1, 1463:1464]
        P.prod("scalar", (lambda e: e.activation(trec[:], R[0:1, 160:161], AF.Abs)),
               nm + "s10", 1, waits=wprep + P.after(nm + "rld"))
        P.prod("vector", (lambda e: e.tensor_scalar(trec[:], trec[:], 0.1, None, ALU.add)),
               nm + "s11", 1, waits=P.after(nm + "s10"))
        P.prod("vector", (lambda e: e.reciprocal(trec[:], trec[:])),
               nm + "s12", 1, waits=P.after(nm + "s11"))
        wrow = self.scrf[0:1, 1464:1469]
        P.prod("vector", (lambda e: e.tensor_scalar(wrow[:], self.l3a_wbase[:], trec[:],
                                                    None, ALU.mult)),
               nm + "s13", 1, waits=P.after(nm + "s12") + wprep)
        for (src, off, n) in ((ph2, 0, 80), (app, 80, 80), (wrow, 160, 5)):
            P.prod("tensor",
                   (lambda e, src=src, n=n: e.matmul(self.pb[7][0:128, 0:n],
                                                     self.onesrow[0:1, 0:128], src[0:1, 0:n],
                                                     start=True, stop=True)),
                   nm + "bm", 1,
                   waits=P.after(nm + "s6", nm + "s9", nm + "s13", nm + "bc") + self.initw())
            P.prod("vector",
                   (lambda e, off=off, n=n: e.tensor_copy(self.l3a_cols[:, off:off + n],
                                                          self.pb[7][0:128, 0:n])),
                   nm + "bc", 1, waits=P.after(nm + "bm"))
        PHB, ALB, WCB = 0, 80, 160
        if not hasattr(self, "attn_dram"):
            self.attn_dram = self.dram("attn_dram", [1024, T], BF16)
        if not hasattr(self, "y_dram"):
            self.y_dram = self.dram("y_dram", [1024, T], BF16)

        qf = self.scra[:, 0:2048]
        kf = self.scra[:, 2048:4096]
        vf = self.scra[:, 4096:6144]
        vtm = self.scra[:, 6144:8224]
        sint = (self.scra[:, 8224:8736], self.scra[:, 8736:9248])
        acc = tuple(self.scra[:, 9248 + j * 512:9760 + j * 512] for j in range(4))
        e2 = tuple(self.scra[:, 11296 + j * 512:11808 + j * 512] for j in range(4))
        epi1 = self.scra[0:64, 13344:13856]
        epi2 = self.scra[0:64, 13856:14368]
        r512 = self.scrf[0:1, 517:1029]
        murow = self.sb("tph", [128, D], BF16).ap().bitcast(F32)[0:1, 0:512]
        bcl = self.l3a_bcols

        for hp in range(8):
            wv = wprep + P.after(nm + "pvm", nm + "sinq!", nm + "pmm",
                                 nm + "acca", nm + "accb", nm + "accc", nm + "accd")
            for j, wn in enumerate(("q", "k", "v")):
                self.load_w(self.wc_l3a[wn], self.wpool, j * 1024, 1024,
                            hp * 128, (hp + 1) * 128, nm + "wl", wv)
            for j, dst in enumerate((qf, kf, vf)):
                for ts in range(NTT):
                    bank = self.pb[ts % 2]
                    P.prod("tensor",
                           (lambda e, j=j, ts=ts, bank=bank: [
                               e.matmul(bank[0:128, 0:512],
                                        self.wpool[:, j * 1024 + ci * 128: j * 1024 + (ci + 1) * 128],
                                        self.xb[:, ci * T + ts * 512:ci * T + (ts + 1) * 512],
                                        start=(ci == 0), stop=(ci == NCHUNK - 1))
                               for ci in range(NCHUNK)][-1]),
                           nm + "pmm", 1,
                           waits=P.after(nm + "wl", nm + "pev", "postln") + self.xbok())
                    P.prod("vector",
                           (lambda e, j=j, dst=dst, ts=ts, bank=bank, hp=hp: e.tensor_scalar(
                               dst[:, ts * 512:(ts + 1) * 512], bank[0:128, 0:512],
                               bcl[:, 8 * j + hp:8 * j + hp + 1], None, ALU.add)),
                           nm + "pev", 1, waits=P.after(nm + "pmm"))
            P.prod("vector", lambda e: e.memset(vtm[:], 1.0), nm + "vms", 1,
                   waits=P.after(nm + "pvm"))
            for h in range(2):
                for kt in range(16):
                    pv = self.pb[2 + kt % 2].ap().bitcast(BF16)
                    P.prod("tensor",
                           (lambda e, h=h, kt=kt, pv=pv: e.transpose(
                               pv[0:128, 0:64],
                               vf[h * 64:(h + 1) * 64, kt * 128:(kt + 1) * 128],
                               self.ident[h * 64:h * 64 + 64, h * 64:h * 64 + 64])),
                           nm + "vtr", 1,
                           waits=P.after(nm + "pev", nm + "vev", nm + "sinq!")
                           + self.initw())
                    P.prod("vector",
                           (lambda e, h=h, kt=kt, pv=pv: e.tensor_copy(
                               vtm[:, h * 1040 + kt * 65: h * 1040 + kt * 65 + 64], pv[0:128, 0:64])),
                           nm + "vev", 1, waits=P.after(nm + "vtr", nm + "vms"))
            for h in range(2):
                hb = h * 64
                hglob = hp * 2 + h
                for qt in range(NTT):
                    def emit_smm(kt, lag, qt=qt, hb=hb):
                        bank = self.pb[kt % 4]
                        P.prod("tensor",
                               (lambda e, kt=kt, qt=qt, hb=hb, bank=bank: e.matmul(
                                   bank[0:128, 0:512],
                                   kf[hb:hb + 64, kt * 128:(kt + 1) * 128],
                                   qf[hb:hb + 64, qt * 512:(qt + 1) * 512],
                                   start=True, stop=True)),
                               nm + "smm", 1,
                               waits=P.after(nm + "pev", nm + "vev", nm + "bc")
                               + P.after_lag(nm + "sinq!", lag))
                    for j in range(4):
                        emit_smm(j, (3 - j) * 5)
                    for kt in range(16):
                        bank = self.pb[kt % 4]
                        db = kt % 2
                        pc = "abcd"[kt % 4]
                        import os as _os
                        if _os.environ.get("DBG_L3A") and hp == 0 and h == 0 and qt == 0 and kt == 0:
                            dq = self.nc.dram_tensor("dbg_qf", [128, 2048], BF16,
                                                     kind="ExternalOutput").ap()
                            dk = self.nc.dram_tensor("dbg_kf", [128, 2048], BF16,
                                                     kind="ExternalOutput").ap()
                            P.prod("sync", (lambda e, dq=dq: e.dma_start(out=dq, in_=qf[:])),
                                   nm + "dbg3", 16, waits=P.after(nm + "pev"))
                            P.prod("sync", (lambda e, dk=dk: e.dma_start(out=dk, in_=kf[:])),
                                   nm + "dbg3", 16, waits=P.after(nm + "pev"))
                            db = self.nc.dram_tensor("dbg_bcols", [128, 48], F32,
                                                     kind="ExternalOutput").ap()
                            P.prod("sync", (lambda e, db=db: e.dma_start(out=db, in_=bcl[:])),
                                   nm + "dbg3", 16, waits=P.after(nm + "pev"))
                            dsc = self.nc.dram_tensor("dbg_sc", [128, 512], F32,
                                                      kind="ExternalOutput").ap()
                            P.prod("vector",
                                   (lambda e, bank=bank: e.tensor_copy(self.scrf[:, 0:512],
                                                                       bank[0:128, 0:512])),
                                   nm + "dbg2", 1, waits=P.after(nm + "smm"))
                            P.prod("sync",
                                   (lambda e, dsc=dsc: e.dma_start(out=dsc, in_=self.scrf[:, 0:512])),
                                   nm + "dbg3", 16, waits=P.after(nm + "dbg2"))
                        j4 = kt % 4
                        for f in range(5):
                            P.prod("scalar",
                                   (lambda e, f=f, hglob=hglob, bank=bank, db=db: e.activation(
                                       sint[db][:], bank[0:128, 0:512], AF.Sin,
                                       bias=self.l3a_cols[:, PHB + hglob * 5 + f:
                                                          PHB + hglob * 5 + f + 1],
                                       scale=self.l3a_cols[:, WCB + f:WCB + f + 1])),
                                   nm + "sinq!", 1,
                                   waits=P.after(nm + "smm", nm + "acc" + pc,
                                                 nm + "dbg2")),
                            if f == 0:
                                P.prod("vector",
                                       (lambda e, f=f, hglob=hglob, j4=j4, db=db: e.tensor_scalar(
                                           acc[j4][:], sint[db][:],
                                           self.l3a_cols[:, ALB + hglob * 5 + f:
                                                         ALB + hglob * 5 + f + 1],
                                           None, ALU.mult)),
                                       nm + "acc" + pc, 1,
                                       waits=P.after(nm + "sinq!", nm + "sgq"))
                            else:
                                P.prod("vector",
                                       (lambda e, f=f, hglob=hglob, j4=j4, db=db: e.scalar_tensor_tensor(
                                           acc[j4][:], sint[db][:],
                                           self.l3a_cols[:, ALB + hglob * 5 + f:
                                                         ALB + hglob * 5 + f + 1],
                                           acc[j4][:], ALU.mult, ALU.add)),
                                       nm + "acc" + pc, 1,
                                       waits=P.after(nm + "sinq!"))
                        if kt + 4 < 16:
                            emit_smm(kt + 4, 0)
                        if kt % 4 == 3:
                            g0 = kt - 3
                            for j in range(4):
                                P.prod("scalar",
                                       (lambda e, j=j: e.activation(e2[j][:], acc[j][:],
                                                                    AF.Sigmoid)),
                                       nm + "sgq", 1,
                                       waits=P.after(nm + "acc" + "abcd"[j],
                                                     nm + "pvm")),
                            for j in range(4):
                                ktj = g0 + j
                                P.prod("tensor",
                                       (lambda e, ktj=ktj, h=h, j=j: e.matmul(
                                           self.pb[6][0:65, 0:512],
                                           vtm[:, h * 1040 + ktj * 65:
                                               h * 1040 + (ktj + 1) * 65],
                                           e2[j][:],
                                           start=(ktj == 0), stop=(ktj == 15))),
                                       nm + "pvm", 1,
                                       waits=P.after(nm + "sgq", nm + "oev",
                                                     nm + "od"))
                    P.prod("vector",
                           (lambda e: e.tensor_scalar(r512[:], self.pb[6][64:65, 0:512],
                                                      1e-10, None, ALU.add)),
                           nm + "od", 1, waits=P.after(nm + "pvm"))
                    P.prod("vector",
                           (lambda e: e.reciprocal(r512[:], r512[:])),
                           nm + "od2", 1, waits=P.after(nm + "od"))
                    P.prod("tensor",
                           (lambda e: e.matmul(self.pb[7][0:64, 0:512],
                                               self.onesrow[0:1, 0:64], r512[:],
                                               start=True, stop=True)),
                           nm + "obc", 1, waits=P.after(nm + "od2", nm + "omu", nm + "bc")),
                    P.prod("vector",
                           (lambda e: e.tensor_copy(epi1[:], self.pb[6][0:64, 0:512])),
                           nm + "oev", 1, waits=P.after(nm + "od", nm + "ost")),
                    P.prod("vector",
                           (lambda e: e.tensor_mul(epi2[:], epi1[:], self.pb[7][0:64, 0:512])),
                           nm + "omu", 1, waits=P.after(nm + "obc", nm + "oev"))
                    P.prod("sync",
                           (lambda e, hglob=hglob, qt=qt: e.dma_start(
                               out=self.attn_dram[hglob * 64:(hglob + 1) * 64,
                                                  qt * 512:(qt + 1) * 512],
                               in_=epi2[:])),
                           nm + "ost", 16, waits=P.after(nm + "omu"))
        self.load_w(self.wc_l3a["o"], self.wpool2, 0, 1024, 0, 1024, nm + "wo",
                    wprep + P.after(nm + "pmm"))
        ytile = self.scrb[:, 4096 + 0:4096 + 512]
        stats = self.pb[5]
        rowm = self.sb("tph", [128, D], BF16).ap().bitcast(F32)[0:1, 0:512]
        rowr = self.sb("tpl", [128, D], BF16).ap().bitcast(F32)[0:1, 0:512]
        sq = self.scrb[:, 4608:5120]
        for ts in range(NTT):
            for ci in range(NCHUNK):
                P.prod("sync",
                       (lambda e, ci=ci, ts=ts: e.dma_start(
                           out=self.scrb[:, ci * 512: (ci + 1) * 512],
                           in_=self.attn_dram[ci * 128:(ci + 1) * 128, ts * 512:(ts + 1) * 512])),
                       nm + "ald", 16, waits=P.after(nm + "ost", nm + "omm"))
            for oc in range(NCHUNK):
                bank = self.pb[oc % 4]
                P.prod("tensor",
                       (lambda e, oc=oc, ts=ts, bank=bank: [
                           e.matmul(bank[0:128, 0:512],
                                    self.wpool2[:, ci * 1024 + oc * 128: ci * 1024 + (oc + 1) * 128],
                                    self.scrb[:, ci * 512: (ci + 1) * 512],
                                    start=(ci == 0), stop=(ci == NCHUNK - 1))
                           for ci in range(NCHUNK)][-1]),
                       nm + "omm", 1,
                       waits=P.after(nm + "ald", nm + "wo") + P.after_lag(nm + "yst!", 3)),
                P.prod("vector",
                       (lambda e, oc=oc, bank=bank: e.tensor_scalar(
                           ytile[:], bank[0:128, 0:512], bcl[:, 24 + oc:24 + oc + 1],
                           None, ALU.add)),
                       nm + "yev", 1, waits=P.after(nm + "omm", nm + "yst!"))
                P.prod("tensor",
                       (lambda e, oc=oc: e.matmul(stats[0:1, 0:512], self.onescol[:, 0:1],
                                                  ytile[:], start=(oc == 0),
                                                  stop=(oc == NCHUNK - 1))),
                       nm + "ymm", 1, waits=P.after(nm + "yev", nm + "mu1") + self.initw()),
                P.prod("scalar",
                       (lambda e: e.activation(sq[:], ytile[:], AF.Square)),
                       nm + "ysq", 1, waits=P.after(nm + "yev", nm + "ymm2")),
                P.prod("tensor",
                       (lambda e, oc=oc: e.matmul(self.pb[7][0:1, 0:512],
                                                  self.onescol[:, 0:1], sq[:],
                                                  start=(oc == 0), stop=(oc == NCHUNK - 1))),
                       nm + "ymm2", 1, waits=P.after(nm + "ysq", nm + "mu1")),
                P.prod("sync",
                       (lambda e, oc=oc, ts=ts: e.dma_start(
                           out=self.y_dram[oc * 128:(oc + 1) * 128, ts * 512:(ts + 1) * 512],
                           in_=ytile[:])),
                       nm + "yst!", 16, waits=P.after(nm + "yev"))
            P.prod("vector",
                   (lambda e: e.tensor_scalar(rowm[:], stats[0:1, 0:512], 1.0 / D, None,
                                              ALU.mult)),
                   nm + "mu1", 1, waits=P.after(nm + "ymm", nm + "ymm2"))
            P.prod("scalar",
                   (lambda e: e.activation(r512[:], rowm[:], AF.Square)),
                   nm + "mu2", 1, waits=P.after(nm + "mu1"))
            P.prod("vector",
                   (lambda e: e.tensor_scalar(rowr[:], self.pb[7][0:1, 0:512], 1.0 / D, None,
                                              ALU.mult)),
                   nm + "mu3", 1, waits=P.after(nm + "ymm2", nm + "mu2"))
            P.prod("vector",
                   (lambda e: e.tensor_sub(rowr[:], rowr[:], r512[:])),
                   nm + "mu4", 1, waits=P.after(nm + "mu3"))
            epsap = self.eps_col(1e-5, 1)
            P.prod("scalar",
                   (lambda e, epsap=epsap: e.activation(rowr[:], rowr[:], AF.Sqrt, bias=epsap)),
                   nm + "mu5", 1, waits=P.after(nm + "mu4") + self.initw())
            P.prod("vector",
                   (lambda e: e.reciprocal(rowr[:], rowr[:])),
                   nm + "mu6", 1, waits=P.after(nm + "mu5"))
            P.prod("tensor",
                   (lambda e: e.matmul(self.pb[4][0:128, 0:512], self.onesrow[0:1, 0:128],
                                       rowm[:], start=True, stop=True)),
                   nm + "bmu", 1, waits=P.after(nm + "mu6", nm + "ya1")),
            P.prod("tensor",
                   (lambda e: e.matmul(self.pb[6][0:128, 0:512], self.onesrow[0:1, 0:128],
                                       rowr[:], start=True, stop=True)),
                   nm + "brs", 1, waits=P.after(nm + "mu6", nm + "ya2", nm + "pvm", nm + "od")),
            for oc in range(NCHUNK):
                P.prod("sync",
                       (lambda e, oc=oc, ts=ts: e.dma_start(
                           out=ytile[:], in_=self.y_dram[oc * 128:(oc + 1) * 128,
                                                         ts * 512:(ts + 1) * 512])),
                       nm + "yld", 16, waits=P.after(nm + "ya1", nm + "yst!",
                                                     nm + "ymm", nm + "ymm2"))
                P.prod("vector",
                       (lambda e, oc=oc: e.tensor_sub(self.scrf[:, 0:512], ytile[:],
                                                      self.pb[4][0:128, 0:512])),
                       nm + "ya1", 1, waits=P.after(nm + "yld", nm + "bmu", nm + "ya2")),
                P.prod("vector",
                       (lambda e: e.tensor_mul(self.scrf[:, 0:512], self.scrf[:, 0:512],
                                               self.pb[6][0:128, 0:512])),
                       nm + "ya2", 1, waits=P.after(nm + "ya1", nm + "brs"))
                P.prod("vector",
                       (lambda e, oc=oc: e.tensor_scalar(
                           self.scrf[:, 0:512], self.scrf[:, 0:512],
                           bcl[:, 32 + oc:32 + oc + 1], bcl[:, 40 + oc:40 + oc + 1],
                           ALU.mult, ALU.add)),
                       nm + "ya3", 1, waits=P.after(nm + "ya2"))
                P.prod("vector",
                       (lambda e, oc=oc, ts=ts: e.tensor_tensor(
                           self.xs[:, oc * T + ts * 512: oc * T + (ts + 1) * 512],
                           self.xs[:, oc * T + ts * 512: oc * T + (ts + 1) * 512],
                           self.scrf[:, 0:512], ALU.add)),
                       nm + "ya4", 1, waits=P.after(nm + "ya3"))
        import os
        if os.environ.get("DBG_L3A"):
            d1 = self.nc.dram_tensor("dbg_cols3", [128, 176], F32, kind="ExternalOutput").ap()
            P.prod("sync", (lambda e: e.dma_start(out=d1[:, 0:165], in_=self.l3a_cols[:, 0:165])),
                   nm + "dbg", 16, waits=P.after(nm + "ya4"))
            d2 = self.nc.dram_tensor("dbg_attn3", [1024, T], BF16, kind="ExternalOutput").ap()
            for q in range(4):
                P.prod("sync", (lambda e, q=q: e.dma_start(
                    out=d2[q * 256:(q + 1) * 256, :],
                    in_=self.attn_dram[q * 256:(q + 1) * 256, :])),
                    nm + "dbg", 16, waits=P.after(nm + "ya4"))
            return self.emit_postln(P.after(nm + "ya4", nm + "dbg"))
        return self.emit_postln(P.after(nm + "ya4"))


LAYER_EMITTERS = {"l2f": "emit_l2f", "l3f": "emit_l3f", "l0f": "emit_l0f", "l0a": "emit_l0a", "l1a": "emit_l1a", "l1f": "emit_l1f", "l2a": "emit_l2a", "l3a": "emit_l3a"}
ALL_LAYERS = ["l0a", "l0f", "l1a", "l1f", "l2a", "l2f", "l3a", "l3f"]


def build_program(layers, nb=NB):
    B = Builder(layers, nb)
    P = B.P
    for b in range(nb):
        B.emit_loadx(b)
        for name in layers:
            getattr(B, LAYER_EMITTERS[name])(b)
        B.emit_storex(b)
    P.op("sync", lambda e: e.nop(), waits=P.after("xst"))
    P.build()
    return B


def run_layers(layers, x, inputs, nb=NB):
    B = build_program(layers, nb)
    in_maps = []
    for c in range(8):
        m = {"x_in": np.ascontiguousarray(x[nb * c:nb * (c + 1)])}
        for name in B.win:
            m[name] = np.ascontiguousarray(inputs[name])
        in_maps.append(m)
    res = run_bass_kernel_spmd(B.nc, in_maps, core_ids=list(range(8)))
    return np.concatenate([r["x_out"] for r in res.results], axis=0)


def kernel(**inputs):
    x = np.ascontiguousarray(inputs["L0_attn_x"]).astype(np.float32, copy=False)
    for group in [ALL_LAYERS]:
        x = run_layers(group, x, inputs)
    return x
